# version 03
# baseline (speedup 1.0000x reference)
import numpy as np
import concourse.bass as bass
import concourse.mybir as mybir
from concourse.bass_utils import run_bass_kernel_spmd
from contextlib import ExitStack

F32 = mybir.dt.float32
BF16 = mybir.dt.bfloat16
U8 = mybir.dt.uint8
AF = mybir.ActivationFunctionType
ALU = mybir.AluOpType
AX = mybir.AxisListType.X

B_CORE = 4
S = 2048
D = 1024
NHEAD = 16
H = 64
DFF = 4096
MAGIC = 12582912.0
N_CORES = 8

ENGS = ("sync", "tensor", "vector", "scalar", "gpsimd")

DBG_PHASES = "QLFAW"
DBG_BATCHES = B_CORE
DBG_TAP = False


class Prog:
    def __init__(self, nc):
        self.nc = nc
        self.ops = {e: [] for e in ENGS}
        self.semh = {}
        self.semc = {}
        self.last_wait = {}

    def sem(self, name, handle):
        self.semh[name] = handle
        self.semc[name] = 0

    def wait(self, eng, sem, val):
        if val is None or val <= 0:
            return
        key = (eng, sem)
        if self.last_wait.get(key, -1) >= val:
            return
        self.last_wait[key] = val
        h = self.semh[sem]
        self.ops[eng].append(lambda E, h=h, v=val: E.wait_ge(h, v))

    OWN = {"tensor": "pe", "vector": "ve", "scalar": "ac"}

    def fence(self, eng):
        own = self.OWN[eng]
        self.wait(eng, own, self.semc[own])

    def run(self, eng, fn, inc=None, n=1):
        if inc is None and eng in ("vector", "scalar"):
            inc = self.OWN[eng]
        tok = None
        if inc is not None:
            self.semc[inc] += n
            tok = self.semc[inc]
            h = self.semh[inc]

            def cl(E, fn=fn, h=h, n=n):
                fn(E).then_inc(h, n)
        else:
            def cl(E, fn=fn):
                fn(E)
        self.ops[eng].append(cl)
        return tok

    def load(self, out_ap, in_ap):
        return self.run("sync", lambda E, o=out_ap, i=in_ap: E.dma_start(out=o, in_=i),
                        inc="dl", n=16)

    def store(self, out_ap, in_ap):
        return self.run("gpsimd", lambda E, o=out_ap, i=in_ap: E.dma_start(out=o, in_=i),
                        inc="ds", n=16)

    def mm(self, out, lhsT, rhs, start, stop, inc=None):
        return self.run("tensor",
                        lambda E, o=out, l=lhsT, r=rhs, s=start, p=stop:
                        E.matmul(o, l, r, start=s, stop=p), inc=inc)

    def barrier_psum(self, *engs):
        for e in engs:
            for sem in ("pe", "ve", "ac"):
                if sem == self.OWN.get(e):
                    continue
                self.wait(e, sem, self.semc[sem])


def build_nc():
    nc = bass.Bass("TRN2", target_bir_lowering=False)

    x_tm = nc.dram_tensor("x_tm", [B_CORE, S, D], F32, kind="ExternalInput").ap()
    x_fm = nc.dram_tensor("x_fm", [B_CORE, D, S], F32, kind="ExternalInput").ap()
    wk_t = nc.dram_tensor("wk_t", [D, NHEAD * H], F32, kind="ExternalInput").ap()
    wq_t = nc.dram_tensor("wq_t", [D, NHEAD * H], F32, kind="ExternalInput").ap()
    wv_t = nc.dram_tensor("wv_t", [D, NHEAD * H], F32, kind="ExternalInput").ap()
    wout = nc.dram_tensor("wout", [NHEAD * H, D], F32, kind="ExternalInput").ap()
    bk_m = nc.dram_tensor("bk_m", [128, 8], F32, kind="ExternalInput").ap()
    bq_m = nc.dram_tensor("bq_m", [128, 8], F32, kind="ExternalInput").ap()
    bv_row = nc.dram_tensor("bv_row", [1, NHEAD * H], F32, kind="ExternalInput").ap()
    bo_row = nc.dram_tensor("bo_row", [1, D], F32, kind="ExternalInput").ap()
    w1 = nc.dram_tensor("w1", [DFF, D], F32, kind="ExternalInput").ap()
    b1_m = nc.dram_tensor("b1_m", [128, 32], F32, kind="ExternalInput").ap()
    w2 = nc.dram_tensor("w2", [D, DFF], F32, kind="ExternalInput").ap()
    b2_row = nc.dram_tensor("b2_row", [1, D], F32, kind="ExternalInput").ap()
    identc = nc.dram_tensor("identc", [128, 128], F32, kind="ExternalInput").ap()
    maskinv = nc.dram_tensor("maskinv", [128, 128], U8, kind="ExternalInput").ap()
    out = nc.dram_tensor("out", [B_CORE, S, D], F32, kind="ExternalOutput").ap()

    kw = {"kind": "ExternalOutput"} if DBG_TAP else {}
    wq1t = nc.dram_tensor("wq1t", [D, DFF], BF16, **kw).ap()
    wq2t = nc.dram_tensor("wq2t", [DFF, D], BF16, **kw).ap()
    g_scr = nc.dram_tensor("g_scr", [B_CORE, S, D], F32, **kw).ap()
    wgh_scr = nc.dram_tensor("wgh_scr", [NHEAD * H, S], BF16, **kw).ap()
    wgl_scr = nc.dram_tensor("wgl_scr", [NHEAD * H, S], BF16, **kw).ap()
    awh = nc.dram_tensor("awh", [3, D, NHEAD * H], BF16).ap()
    awl = nc.dram_tensor("awl", [3, D, NHEAD * H], BF16).ap()
    woh = nc.dram_tensor("woh", [NHEAD * H, D], BF16).ap()
    wol = nc.dram_tensor("wol", [NHEAD * H, D], BF16).ap()

    with ExitStack() as ctx:
        E_ = ctx.enter_context
        xh = E_(nc.sbuf_tensor("xh", [128, 16384], BF16)).ap()
        xl = E_(nc.sbuf_tensor("xl", [128, 16384], BF16)).ap()
        r2 = E_(nc.sbuf_tensor("r2", [128, 8192], F32)).ap()
        kqK = E_(nc.sbuf_tensor("kqK", [128, 4096], F32)).ap()
        kqQ = E_(nc.sbuf_tensor("kqQ", [128, 4096], F32)).ap()
        vsl = E_(nc.sbuf_tensor("vsl", [128, 16 * 4 * 65], F32)).ap()
        srow1 = E_(nc.sbuf_tensor("srow1", [1, 16 * 65], F32)).ap()
        crow1 = E_(nc.sbuf_tensor("crow1", [1, 16 * 65], F32)).ap()
        rows = E_(nc.sbuf_tensor("rows", [1, 1536], F32)).ap()
        wkb = E_(nc.sbuf_tensor("wkb", [128, 2048], BF16)).ap()
        wqb = E_(nc.sbuf_tensor("wqb", [128, 2048], BF16)).ap()
        wvb = E_(nc.sbuf_tensor("wvb", [128, 4096], BF16)).ap()
        wq1b = E_(nc.sbuf_tensor("wq1b", [128, 2048], BF16)).ap()
        wq2b = E_(nc.sbuf_tensor("wq2b", [128, 2048], BF16)).ap()
        wgtb = E_(nc.sbuf_tensor("wgtb", [128, 2048], BF16)).ap()
        ident = E_(nc.sbuf_tensor("ident", [128, 128], F32)).ap()
        mski = E_(nc.sbuf_tensor("mski", [128, 128], U8)).ap()
        onest = E_(nc.sbuf_tensor("onest", [128, 128], F32)).ap()
        onescol = E_(nc.sbuf_tensor("onescol", [128, 1], F32)).ap()
        onesrow = E_(nc.sbuf_tensor("onesrow", [1, 512], F32)).ap()
        b1s = E_(nc.sbuf_tensor("b1s", [128, 32], F32)).ap()
        bks = E_(nc.sbuf_tensor("bks", [128, 8], F32)).ap()
        bqs = E_(nc.sbuf_tensor("bqs", [128, 8], F32)).ap()
        bvs = E_(nc.sbuf_tensor("bvs", [1, 1024], F32)).ap()
        bos = E_(nc.sbuf_tensor("bos", [1, 1024], F32)).ap()
        b2s = E_(nc.sbuf_tensor("b2s", [1, 1024], F32)).ap()
        colt = E_(nc.sbuf_tensor("colt", [128, 8], F32)).ap()

        pA = E_(nc.psum_tensor("pA", [128, 512], F32)).ap()
        pB = E_(nc.psum_tensor("pB", [128, 512], F32)).ap()
        pC = E_(nc.psum_tensor("pC", [128, 1024], F32)).ap()
        pD = E_(nc.psum_tensor("pD", [128, 1024], F32)).ap()
        pE = E_(nc.psum_tensor("pE", [128, 512], F32)).ap()
        pF = E_(nc.psum_tensor("pF", [128, 512], F32)).ap()

        sem_handles = {}
        for name in ("dl", "ds", "pe", "ve", "ac"):
            sem_handles[name] = E_(nc.semaphore(name))

        block = E_(nc.Block())
        p = Prog(nc)
        for name, h in sem_handles.items():
            p.sem(name, h)

        build_program(p, locals())

        @block.sync
        def _(sync):
            for f in p.ops["sync"]:
                f(sync)

        @block.tensor
        def _(tensor):
            for f in p.ops["tensor"]:
                f(tensor)

        @block.vector
        def _(vector):
            for f in p.ops["vector"]:
                f(vector)

        @block.scalar
        def _(scalar):
            for f in p.ops["scalar"]:
                f(scalar)

        @block.gpsimd
        def _(gpsimd):
            for f in p.ops["gpsimd"]:
                f(gpsimd)

    return nc


def build_program(p, t):
    xh, xl, r2 = t["xh"], t["xl"], t["r2"]
    kqK, kqQ, vsl = t["kqK"], t["kqQ"], t["vsl"]
    srow1, crow1, rows = t["srow1"], t["crow1"], t["rows"]
    wkb, wqb, wvb = t["wkb"], t["wqb"], t["wvb"]
    wq1b, wq2b, wgtb = t["wq1b"], t["wq2b"], t["wgtb"]
    gsb = t["kqK"][:, 0:1024]
    hfp = t["kqK"][:, 1024:1280]
    stage = wq1b[:, 1024:1280]
    stg2 = wgtb
    uh = wq1b[0:64, 0:1024]
    ul = wq1b[64:128, 0:1024]
    ustg = wq2b.bitcast(F32)[0:64, 0:512]
    ident, mski, onest = t["ident"], t["mski"], t["onest"]
    onescol, onesrow = t["onescol"], t["onesrow"]
    b1s, bks, bqs, bvs, bos, b2s, colt = (t["b1s"], t["bks"], t["bqs"], t["bvs"],
                                          t["bos"], t["b2s"], t["colt"])
    pA, pB, pC, pD, pE, pF = t["pA"], t["pB"], t["pC"], t["pD"], t["pE"], t["pF"]
    x_tm, x_fm = t["x_tm"], t["x_fm"]
    wk_t, wq_t, wv_t, wout = t["wk_t"], t["wq_t"], t["wv_t"], t["wout"]
    w1, b1_m, w2 = t["w1"], t["b1_m"], t["w2"]
    identc, maskinv, out = t["identc"], t["maskinv"], t["out"]
    wq1t, wq2t, g_scr = t["wq1t"], t["wq2t"], t["g_scr"]
    wgh_scr, wgl_scr = t["wgh_scr"], t["wgl_scr"]
    awh, awl, woh, wol = t["awh"], t["awl"], t["woh"], t["wol"]
    vslv = vsl.rearrange("p (c h e) -> p c h e", c=16, h=4)

    p.load(ident[:, :], identc)
    p.load(mski[:, :], maskinv)
    p.load(b1s[:, :], t["b1_m"])
    p.load(bks[:, :], t["bk_m"])
    p.load(bqs[:, :], t["bq_m"])
    p.load(bvs[:, :], t["bv_row"])
    p.load(bos[:, :], t["bo_row"])
    tok_consts = p.load(b2s[:, :], t["b2_row"])

    p.run("vector", lambda E: E.memset(onest[:, :], 1.0))
    p.run("vector", lambda E: E.memset(onescol[:, :], 1.0))
    p.run("vector", lambda E: E.memset(onesrow[:, :], 1.0))
    p.run("vector", lambda E: E.memset(vslv[:, :, :, 64:65], 1.0))
    p.fence("vector")

    f32buf = r2[:, 0:1024]
    st_aw = None

    def split_stage(src_ap, dst_h, dst_l):
        nonlocal st_aw
        p.wait("sync", "ds", st_aw)
        p.wait("sync", "ve", p.semc["ve"])
        dt_ = p.load(f32buf, src_ap)
        p.wait("vector", "dl", dt_)
        p.wait("vector", "ds", st_aw)
        p.run("vector", lambda E: E.tensor_copy(stg2[:, 0:1024], f32buf))
        p.fence("vector")
        vv = p.run("vector", lambda E: E.tensor_tensor(
            stg2[:, 1024:2048], f32buf, stg2[:, 0:1024], ALU.subtract))
        p.wait("gpsimd", "ve", vv)
        p.store(dst_h, stg2[:, 0:1024])
        st_aw = p.store(dst_l, stg2[:, 1024:2048])

    for mi, wsrc in enumerate((wk_t, wq_t, wv_t)):
        for i in range(8):
            split_stage(wsrc[i * 128:(i + 1) * 128, :],
                        awh[mi, i * 128:(i + 1) * 128, :],
                        awl[mi, i * 128:(i + 1) * 128, :])
    for i in range(8):
        split_stage(wout[i * 128:(i + 1) * 128, :],
                    woh[i * 128:(i + 1) * 128, :],
                    wol[i * 128:(i + 1) * 128, :])

    def quantize(w_dram, wq_dram, n_rows, acc_i, st_prev):
        n_cols = w_dram.shape[1]
        ntile = n_rows // 128
        ncol_t = n_cols // 128
        wbuf = [r2[:, 0:n_cols], r2[:, n_cols:2 * n_cols]]
        acc = colt[:, acc_i:acc_i + 1]
        sc = colt[:, acc_i + 1:acc_i + 2]
        red = colt[:, 6:7]
        p.run("vector", lambda E, a=acc: E.memset(a, 0.0))
        p.fence("vector")
        ve_toks = [None] * ntile
        for i in range(ntile):
            b = wbuf[i % 2]
            if i >= 2:
                p.wait("sync", "ve", ve_toks[i - 2])
            else:
                p.wait("sync", "ve", p.semc["ve"])
                p.wait("sync", "pe", p.semc["pe"])
                p.wait("sync", "ac", p.semc["ac"])
            dt_ = p.load(b, w_dram[i * 128:(i + 1) * 128, :])
            p.wait("scalar", "dl", dt_)
            aca = p.run("scalar", lambda E, b=b: E.activation(b, b, AF.Abs))
            p.wait("vector", "ac", aca)
            p.run("vector", lambda E, b=b, r=red: E.tensor_reduce(
                r, b, AX, ALU.add))
            p.fence("vector")
            ve_toks[i] = p.run("vector", lambda E, a=acc, r=red: E.tensor_tensor(
                a, a, r, ALU.add))
            p.fence("vector")
        p.wait("tensor", "ve", ve_toks[-1])
        p.wait("tensor", "dl", tok_consts)
        pe1 = p.mm(pF[0:1, 0:1], acc, onescol[:, :], True, True, inc="pe")
        p.wait("vector", "pe", pe1)
        sc1 = rows[0:1, 0:1]
        p.run("vector", lambda E, s=sc1: E.tensor_scalar(
            s, pF[0:1, 0:1], 1.0 / (n_rows * n_cols), 1e-5, ALU.mult, ALU.add))
        p.fence("vector")
        ve1 = p.run("vector", lambda E, s=sc1: E.reciprocal(s, s))
        p.wait("tensor", "ve", ve1)
        pe2 = p.mm(pF[0:128, 0:1], onesrow[0:1, 0:128], sc1, True, True, inc="pe")
        p.wait("vector", "pe", pe2)
        ve2 = p.run("vector", lambda E, s=sc: E.tensor_copy(s, pF[0:128, 0:1]))
        pe_cons = [None] * ntile
        st_toks = [st_prev, st_prev]
        last_store = st_prev
        vet_prev = ve2
        for i in range(ntile):
            b = wbuf[i % 2]
            if i == 0:
                p.wait("sync", "ve", ve2)
            else:
                p.wait("sync", "pe", pe_cons[i - 1])
            dt_ = p.load(b, w_dram[i * 128:(i + 1) * 128, :])
            p.wait("vector", "dl", dt_)
            p.run("vector", lambda E, b=b, s=sc: E.tensor_scalar(
                b, b, s, MAGIC, ALU.mult, ALU.add))
            p.fence("vector")
            p.run("vector", lambda E, b=b: E.tensor_scalar(
                b, b, MAGIC, -1.0, ALU.subtract, ALU.max))
            p.fence("vector")
            veq = p.run("vector", lambda E, b=b: E.tensor_scalar(
                b, b, 1.0, None, ALU.min))
            p.wait("tensor", "ve", veq)
            for j in range(ncol_t):
                sg = stage[:, (j % 2) * 128:(j % 2) * 128 + 128]
                p.wait("tensor", "ve", vet_prev)
                pet = p.run("tensor", lambda E, src=b[:, j * 128:(j + 1) * 128]:
                            E.transpose(pE[0:128, 0:128], src, ident[:, :]), inc="pe")
                p.wait("vector", "ds", st_toks[j % 2])
                p.wait("vector", "pe", pet)
                vet = p.run("vector", lambda E, sg=sg: E.tensor_copy(
                    sg, pE[0:128, 0:128]))
                vet_prev = vet
                p.wait("gpsimd", "ve", vet)
                stt = p.store(wq_dram[j * 128:(j + 1) * 128, i * 128:(i + 1) * 128], sg)
                st_toks[j % 2] = stt
                last_store = stt
            pe_cons[i] = p.semc["pe"]
        return last_store

    st_total = st_aw
    if "Q" in DBG_PHASES:
        st_total = quantize(w1, wq1t, DFF, 0, st_total)
        st_total = quantize(w2, wq2t, D, 2, st_total)
    q_done = st_total
    ve_xn = 0

    for b in range(DBG_BATCHES):
        xf = r2[:, 0:4096]
        sq = [r2[:, 4096:4608], r2[:, 4608:5120]]
        mb = r2[:, 5120:5632]
        rb = r2[:, 5632:6144]
        xfv = xf.rearrange("p (dc t) -> p dc t", dc=8)
        x_fmv = x_fm[b].rearrange("(dc p) t -> p dc t", p=128)
        m_row = rows[0:1, 0:512]
        var_row = rows[0:1, 512:1024]
        rstd_row = rows[0:1, 1024:1536]
        for tt in range(4 if "L" in DBG_PHASES else 0):
            p.wait("sync", "pe", p.semc["pe"])
            p.wait("sync", "ve", p.semc["ve"])
            p.wait("sync", "ac", p.semc["ac"])
            dt_ = p.load(xfv, x_fmv[:, :, tt * 512:(tt + 1) * 512])
            p.barrier_psum("tensor")
            p.wait("tensor", "dl", dt_)
            p.wait("scalar", "dl", dt_)
            for dc in range(8):
                p.mm(pA[0:1, 0:512], onescol[:, :], xfv[:, dc, :], dc == 0, dc == 7,
                     inc="pe" if dc == 7 else None)
            sq_mm = [None, None]
            for dc in range(8):
                p.wait("scalar", "pe", sq_mm[dc % 2])
                act = p.run("scalar", lambda E, dc=dc, s=sq[dc % 2]: E.activation(
                    s, xfv[:, dc, :], AF.Square))
                p.wait("tensor", "ac", act)
                sq_mm[dc % 2] = p.mm(pB[0:1, 0:512], onescol[:, :], sq[dc % 2],
                                     dc == 0, dc == 7, inc="pe")
            pe_stats = p.semc["pe"]
            p.wait("vector", "pe", pe_stats)
            p.run("vector", lambda E: E.tensor_scalar(
                m_row, pA[0:1, 0:512], 1.0 / D, None, ALU.mult))
            p.fence("vector")
            p.run("vector", lambda E: E.tensor_tensor(
                var_row, m_row, m_row, ALU.mult))
            p.fence("vector")
            p.run("vector", lambda E: E.scalar_tensor_tensor(
                var_row, pB[0:1, 0:512], 1.0 / D, var_row, ALU.mult, ALU.subtract))
            p.fence("vector")
            ve_v = p.run("vector", lambda E: E.tensor_scalar(
                var_row, var_row, 1e-5, None, ALU.add))
            p.wait("scalar", "ve", ve_v)
            p.run("scalar", lambda E: E.activation(rstd_row, var_row, AF.Ln))
            p.fence("scalar")
            ac_r = p.run("scalar", lambda E: E.activation(
                rstd_row, rstd_row, AF.Exp, scale=-0.5))
            p.wait("tensor", "ve", ve_v)
            p.wait("tensor", "ac", ac_r)
            p.mm(pC[:, 0:512], onesrow[0:1, 0:128], m_row, True, True)
            pe_bc = p.mm(pC[:, 512:1024], onesrow[0:1, 0:128], rstd_row, True, True,
                         inc="pe")
            p.wait("vector", "pe", pe_bc)
            p.run("vector", lambda E: E.tensor_copy(mb, pC[:, 0:512]))
            p.run("vector", lambda E: E.tensor_copy(rb, pC[:, 512:1024]))
            p.wait("vector", "pe", p.semc["pe"])
            p.wait("vector", "ds", st_total if b > 0 else 0)
            p.fence("vector")
            for dc in range(8):
                p.run("vector", lambda E, i=xfv[:, dc, :]: E.scalar_tensor_tensor(
                    i, i, 1.0, mb, ALU.mult, ALU.subtract))
            p.fence("vector")
            for dc in range(8):
                p.run("vector", lambda E, i=xfv[:, dc, :]: E.tensor_tensor(
                    i, i, rb, ALU.mult))
            p.fence("vector")
            for dc in range(8):
                o = xh[:, dc * 2048 + tt * 512: dc * 2048 + (tt + 1) * 512]
                p.run("vector", lambda E, o=o, i=xfv[:, dc, :]: E.tensor_copy(o, i))
            p.fence("vector")
            for dc in range(8):
                oh = xh[:, dc * 2048 + tt * 512: dc * 2048 + (tt + 1) * 512]
                ol = xl[:, dc * 2048 + tt * 512: dc * 2048 + (tt + 1) * 512]
                ve_xn = p.run("vector", lambda E, oh=oh, ol=ol, i=xfv[:, dc, :]:
                              E.tensor_tensor(ol, i, oh, ALU.subtract))

        r2bf = r2.bitcast(BF16)
        hh = r2bf[:, 0:8192]
        hl = r2bf[:, 8192:16384]
        wq1v = wq1t.rearrange("(dc p) f -> p dc f", p=128)
        wq1bv = wq1b.rearrange("p (u dc f) -> p u dc f", u=2, dc=8)
        wq2bv = wq2b.rearrange("p (u f) -> p u f", u=2)
        if "F" in DBG_PHASES:
            p.wait("sync", "ds", q_done)
        d_toks = {}
        pe_wq = {}
        ve_h_prev = 0
        hdone = {}
        rl = {}

        def emit_ffn2(fc, u, gi):
            ac_h, ve_h = hdone[fc]
            p.wait("tensor", "ac", ac_h)
            p.wait("tensor", "ve", ve_h)
            for sub in range(2):
                gp = pC if sub == 0 else pD
                for half in range(2):
                    for hs in (hh, hl):
                        p.mm(gp[:, half * 512:(half + 1) * 512],
                             hs[:, fc * 256 + sub * 128:
                                fc * 256 + sub * 128 + 128],
                             wq2bv[:, u, half * 512:(half + 1) * 512],
                             fc == 0 and hs is hh, False,
                             inc="pe" if (sub == 1 and half == 1 and hs is hl)
                             else None)
            pe_wq[gi] = p.semc["pe"]

        for ts in range(8 if "F" in DBG_PHASES else 0):
            t0 = ts * 256
            for fc in range(32):
                u = fc % 2
                gi = ts * 32 + fc
                if gi == 0:
                    p.wait("sync", "pe", p.semc["pe"])
                    p.wait("sync", "ac", p.semc["ac"])
                    p.wait("sync", "ve", p.semc["ve"])
                    p.wait("sync", "ds", st_total)
                if gi >= 2:
                    p.wait("sync", "pe", pe_wq[gi - 2])
                p.load(wq1bv[:, u, :, :], wq1v[:, :, fc * 128:(fc + 1) * 128])
                d_toks[gi] = p.load(wq2bv[:, u, :], wq2t[fc * 128:(fc + 1) * 128, :])
                if gi == 0:
                    p.wait("tensor", "ve", ve_xn)
                    p.barrier_psum("tensor")
                p.wait("tensor", "dl", d_toks[gi])
                hp = pA if fc % 2 == 0 else pB
                p.wait("tensor", "ac", rl.get(gi - 2, 0))
                for dc in range(8):
                    for cm, xs in ((0, xh), (1, xl)):
                        p.mm(hp[:, 0:256], wq1bv[:, u, dc, :],
                             xs[:, dc * 2048 + t0: dc * 2048 + t0 + 256],
                             dc == 0 and cm == 0, dc == 7 and cm == 1,
                             inc="pe" if (dc == 7 and cm == 1) else None)
                pe_h = p.semc["pe"]
                p.wait("scalar", "pe", pe_h)
                p.wait("scalar", "ve", ve_h_prev)
                rl[gi] = p.run("scalar", lambda E, i=hp[:, 0:256], fc=fc:
                               E.activation(hfp[:, :], i, AF.Relu,
                                            bias=b1s[:, fc:fc + 1]))
                p.fence("scalar")
                ac_h = p.run("scalar", lambda E, o=hh[:, fc * 256:(fc + 1) * 256]:
                             E.activation(o, hfp[:, :], AF.Copy))
                p.wait("vector", "ac", ac_h)
                ve_h = p.run("vector", lambda E, o=hl[:, fc * 256:(fc + 1) * 256],
                             h=hh[:, fc * 256:(fc + 1) * 256]:
                             E.tensor_tensor(o, hfp[:, :], h, ALU.subtract))
                ve_h_prev = ve_h
                hdone[fc] = (ac_h, ve_h)
                if fc > 0:
                    emit_ffn2(fc - 1, (fc - 1) % 2, gi - 1)
            emit_ffn2(31, 31 % 2, ts * 32 + 31)
            for sub in range(2):
                gp = pC if sub == 0 else pD
                for half in range(2):
                    p.mm(gp[:, half * 512:(half + 1) * 512], onesrow[0:1, 0:128],
                         b2s[0:1, half * 512:(half + 1) * 512], False, True,
                         inc="pe" if (sub == 1 and half == 1) else None)
            pe_g = p.semc["pe"]
            pe_wq[ts * 32 + 31] = pe_g
            p.wait("scalar", "pe", pe_g)
            for sub in range(2):
                gp = pC if sub == 0 else pD
                p.wait("scalar", "ds", st_total)
                ac_g = p.run("scalar", lambda E, gp=gp: E.activation(
                    gsb[:, :], gp[:, :], AF.Silu))
                p.wait("gpsimd", "ac", ac_g)
                st_total = p.store(g_scr[b, t0 + sub * 128: t0 + sub * 128 + 128, :],
                                   gsb[:, :])

        esl = r2
        bank_ac = {0: p.semc["ac"], 1: p.semc["ac"]}
        ustg_ds = [st_total, st_total]
        nbufs = wq2b.bitcast(F32)
        nbuf = [nbufs[0:65, 0:512], nbufs[0:65, 512:1024]]
        zrow = [rows[0:1, 0:512], rows[0:1, 512:1024]]
        pend = None
        ui = 0
        zx_par = {0: 0, 1: 0}
        zb_par = {0: 0, 1: 0}
        veW_prev = 0

        def flush_norm():
            nonlocal pend, st_total, veW_prev
            if pend is None:
                return
            par0, n0, qt0 = pend["par"], pend["n"], pend["qt"]
            p.wait("tensor", "ac", pend["zx"])
            p.wait("tensor", "ve", veW_prev)
            zb = p.mm(pA[0:64, :], onesrow[0:1, 0:64], zrow[par0], True, True,
                      inc="pe")
            zb_par[par0] = zb
            nb = nbuf[par0]
            p.wait("vector", "pe", zb)
            p.wait("vector", "ds", ustg_ds[par0])
            p.fence("vector")
            uhs = uh[:, par0 * 512:(par0 + 1) * 512]
            uls = ul[:, par0 * 512:(par0 + 1) * 512]
            p.run("vector", lambda E, nb=nb: E.tensor_tensor(
                nb[0:64, :], nb[0:64, :], pA[0:64, :], ALU.mult))
            p.fence("vector")
            p.run("vector", lambda E, o=uhs, nb=nb: E.tensor_copy(o, nb[0:64, :]))
            p.fence("vector")
            veW = p.run("vector", lambda E, o=uls, nb=nb, h=uhs: E.tensor_tensor(
                o, nb[0:64, :], h, ALU.subtract))
            veW_prev = veW
            p.wait("gpsimd", "ve", veW)
            p.store(wgh_scr[64 * n0: 64 * n0 + 64, qt0 * 512:(qt0 + 1) * 512], uhs)
            st_total = p.store(
                wgl_scr[64 * n0: 64 * n0 + 64, qt0 * 512:(qt0 + 1) * 512], uls)
            ustg_ds[par0] = st_total
            pend = None
        awhv = awh.rearrange("m (dc p) n -> m p dc n", p=128)
        awlv = awl.rearrange("m (dc p) n -> m p dc n", p=128)
        wkbv = wkb.rearrange("p (u dc n) -> p u dc n", u=2, dc=8)
        wqbv = wqb.rearrange("p (u dc n) -> p u dc n", u=2, dc=8)
        wvbv = wvb.rearrange("p (u dc n) -> p u dc n", u=2, dc=8)
        for qd in range(4 if "A" in DBG_PHASES else 0):
            for pp in range(2):
                gp_ = 2 * qd + pp
                p.wait("sync", "pe", p.semc["pe"])
                p.load(wkbv[:, 0, :, :], awhv[0, :, :, gp_ * 128:(gp_ + 1) * 128])
                p.load(wkbv[:, 1, :, :], awlv[0, :, :, gp_ * 128:(gp_ + 1) * 128])
                p.load(wqbv[:, 0, :, :], awhv[1, :, :, gp_ * 128:(gp_ + 1) * 128])
                dq = p.load(wqbv[:, 1, :, :], awlv[1, :, :, gp_ * 128:(gp_ + 1) * 128])
                p.wait("tensor", "dl", dq)
                p.wait("tensor", "ve", p.semc["ve"])
                p.barrier_psum("tensor")
                for tt in range(4):
                    for ps, wb in ((pA, wkbv), (pB, wqbv)):
                        for dc in range(8):
                            for wu, xs in ((0, xh), (0, xl), (1, xh)):
                                p.mm(ps[:, :], wb[:, wu, dc, :],
                                     xs[:, dc * 2048 + tt * 512:
                                        dc * 2048 + (tt + 1) * 512],
                                     dc == 0 and wu == 0 and xs is xh,
                                     dc == 7 and wu == 1,
                                     inc="pe" if (dc == 7 and wu == 1) else None)
                    peQ = p.semc["pe"]
                    p.wait("vector", "pe", peQ)
                    ksl = kqK[:, pp * 2048 + tt * 512: pp * 2048 + (tt + 1) * 512]
                    p.run("vector", lambda E, o=ksl, g=gp_: E.tensor_scalar(
                        o, pA[:, :], bks[:, g:g + 1], None, ALU.add))
                    qsl = kqQ[:, pp * 2048 + tt * 512: pp * 2048 + (tt + 1) * 512]
                    veKQ = p.run("vector", lambda E, o=qsl, g=gp_: E.tensor_scalar(
                        o, pB[:, :], bqs[:, g:g + 1], None, ALU.add))
                    p.wait("tensor", "ve", veKQ)
            p.load(wvbv[:, 0, :, :], awhv[2, :, :, qd * 256:(qd + 1) * 256])
            dv = p.load(wvbv[:, 1, :, :], awlv[2, :, :, qd * 256:(qd + 1) * 256])
            p.wait("tensor", "dl", dv)
            for kc in range(16):
                for dc in range(8):
                    for xs, wu in ((xh, 0), (xh, 1), (xl, 0)):
                        p.mm(pF[:, 0:256],
                             xs[:, dc * 2048 + kc * 128: dc * 2048 + (kc + 1) * 128],
                             wvbv[:, wu, dc, :],
                             dc == 0 and xs is xh and wu == 0, False)
                peV = p.mm(pF[:, 0:256], onesrow[0:1, 0:128],
                           bvs[0:1, qd * 256:(qd + 1) * 256], False, True, inc="pe")
                p.wait("vector", "pe", peV)
                veV = p.run("vector", lambda E, kc=kc: E.tensor_copy(
                    vslv[:, kc, :, 0:64],
                    pF[:, 0:256].rearrange("p (h e) -> p h e", h=4)))
                p.wait("tensor", "ve", veV)
            for j in range(4):
                n = 4 * qd + j
                pbase = 64 * (j % 2)
                poff = (j // 2) * 2048
                p.wait("tensor", "ac", bank_ac[0])
                p.wait("tensor", "ve", p.semc["ve"])
                for cb in range(4):
                    for u in range(4):
                        kc = 4 * cb + u
                        p.mm(pC[0:1, u * 65: u * 65 + 65], onescol[:, :],
                             vslv[:, kc, j, :], True, True,
                             inc="pe" if u == 3 else None)
                    peC = p.semc["pe"]
                    p.wait("vector", "pe", peC)
                    vcp = p.run("vector", lambda E, cb=cb: E.tensor_copy(
                        crow1[0:1, cb * 260:(cb + 1) * 260], pC[0:1, 0:260]))
                    p.wait("tensor", "ve", vcp)
                def scp(dst_kc, src_kc):
                    p.fence("vector")
                    p.run("vector", lambda E, d=dst_kc, s=src_kc: E.tensor_copy(
                        srow1[0:1, d * 65:(d + 1) * 65],
                        srow1[0:1, s * 65:(s + 1) * 65]))

                def sacc(dst_kc, src_kcs, copy_first):
                    first = copy_first
                    for kc in src_kcs:
                        p.fence("vector")
                        src = crow1[0:1, kc * 65:(kc + 1) * 65]
                        dsl = srow1[0:1, dst_kc * 65:(dst_kc + 1) * 65]
                        if first:
                            p.run("vector", lambda E, d=dsl, s=src: E.tensor_copy(d, s))
                            first = False
                        else:
                            p.run("vector", lambda E, d=dsl, s=src: E.tensor_tensor(
                                d, d, s, ALU.add))
                sacc(11, [12, 13, 14, 15], copy_first=True)
                scp(7, 11)
                sacc(7, [8, 9, 10, 11], copy_first=False)
                scp(3, 7)
                sacc(3, [4, 5, 6, 7], copy_first=False)
                p.fence("vector")
                ve_sfx = p.semc["ve"]
                p.wait("tensor", "ve", ve_sfx)
                for qt in range(4):
                    nmat = 4 * qt + 4
                    has_tail = nmat < 16
                    par = ui % 2
                    fix_tok = {}
                    exp_tok = {}

                    def emit_pv(kc, last):
                        p.wait("tensor", "ac", exp_tok[kc // 2])
                        p.wait("tensor", "ve", fix_tok.get(kc, 0))
                        p.mm(pE[0:65, :], vslv[:, kc, j, :],
                             esl[:, kc * 512:(kc + 1) * 512], kc == 0, last,
                             inc="pe" if last else None)

                    for g in range(nmat // 2):
                        sp, bank = (pC, 0) if g % 2 == 0 else (pD, 1)
                        p.wait("tensor", "ac", bank_ac[bank])
                        if g == 0 and pend is not None:
                            p.wait("tensor", "ve", pend["ncp"])
                        for u in range(2):
                            kc = 2 * g + u
                            lhs = kqK[pbase:pbase + 64,
                                      poff + kc * 128: poff + (kc + 1) * 128]
                            rhs = kqQ[pbase:pbase + 64,
                                      poff + qt * 512: poff + (qt + 1) * 512]
                            p.mm(sp[:, u * 512:(u + 1) * 512], lhs, rhs, True, True,
                                 inc="pe" if u == 1 else None)
                        p.wait("scalar", "pe", p.semc["pe"])
                        p.wait("scalar", "ve", p.semc["ve"])
                        exp_tok[g] = p.run(
                            "scalar", lambda E, g=g, sp=sp: E.activation(
                                esl[:, 2 * g * 512: 2 * g * 512 + 1024],
                                sp[:, :], AF.Exp, scale=0.125))
                        bank_ac[bank] = exp_tok[g]
                        p.wait("vector", "ac", exp_tok[g])
                        for u in range(2):
                            kc = 2 * g + u
                            j2 = kc - 4 * qt
                            if 0 <= j2 <= 3:
                                tk = None
                                for j3 in range(j2):
                                    tk = p.run("vector", lambda E, kc=kc, j3=j3:
                                               E.memset(
                                                   esl[:, kc * 512 + j3 * 128:
                                                       kc * 512 + (j3 + 1) * 128],
                                                   1.0))
                                tk = p.run("vector", lambda E, kc=kc, j2=j2:
                                           E.copy_predicated(
                                               esl[:, kc * 512 + j2 * 128:
                                                   kc * 512 + (j2 + 1) * 128],
                                               mski[:, :], onest[:, :]))
                                fix_tok[kc] = tk
                        if g >= 1:
                            emit_pv(2 * (g - 1), False)
                            emit_pv(2 * (g - 1) + 1, False)
                    emit_pv(nmat - 2, False)
                    emit_pv(nmat - 1, not has_tail)
                    if has_tail:
                        p.wait("tensor", "ve", ve_sfx)
                        sfx = srow1[0:1, (4 * qt + 3) * 65:(4 * qt + 3) * 65 + 65]
                        p.mm(pE[0:65, :], sfx, onesrow[0:1, 0:512],
                             False, True, inc="pe")
                    peU = p.semc["pe"]
                    flush_norm()
                    p.wait("vector", "pe", peU)
                    p.wait("vector", "ac", zx_par[par])
                    ncp = p.run("vector", lambda E, nb=nbuf[par]: E.tensor_copy(
                        nb, pE[0:65, :]))
                    p.wait("scalar", "ve", ncp)
                    p.wait("scalar", "pe", zb_par[par])
                    p.run("scalar", lambda E, z=zrow[par], nb=nbuf[par]:
                          E.activation(z, nb[64:65, :], AF.Ln))
                    p.fence("scalar")
                    zx = p.run("scalar", lambda E, z=zrow[par]: E.activation(
                        z, z, AF.Exp, scale=-1.0))
                    zx_par[par] = zx
                    pend = {"par": par, "ncp": ncp, "zx": zx, "n": n, "qt": qt}
                    ui += 1
        flush_norm()

        xlf = xl.bitcast(F32)
        wob = xh
        xbuf = xlf[:, 0:1024]
        gbuf = xlf[:, 1024:2048]
        t1 = xlf[:, 2048:3072]
        obuf = xlf[:, 3072:4096]
        wobv = wob.rearrange("p (u c d) -> p u c d", u=2, c=8)
        wgtbv = wgtb.rearrange("p (u c k) -> p u c k", u=2, c=8)
        wohv = woh.rearrange("(c p) d -> p c d", p=128)
        wolv = wol.rearrange("(c p) d -> p c d", p=128)
        wghv = wgh_scr.rearrange("(c p) s -> p c s", p=128)
        wglv = wgl_scr.rearrange("(c p) s -> p c s", p=128)
        if "W" in DBG_PHASES:
            p.wait("sync", "pe", p.semc["pe"])
            p.wait("sync", "ve", p.semc["ve"])
            p.load(wobv[:, 0, :, :], wohv)
            dwo = p.load(wobv[:, 1, :, :], wolv)
        for tt2 in range(16 if "W" in DBG_PHASES else 0):
            p.wait("sync", "pe", p.semc["pe"])
            p.wait("sync", "ve", p.semc["ve"])
            p.wait("sync", "ds", st_total)
            p.load(wgtbv[:, 0, :, :], wghv[:, :, tt2 * 128:(tt2 + 1) * 128])
            p.load(wgtbv[:, 1, :, :], wglv[:, :, tt2 * 128:(tt2 + 1) * 128])
            p.load(xbuf, x_tm[b, tt2 * 128:(tt2 + 1) * 128, :])
            dg = p.load(gbuf, g_scr[b, tt2 * 128:(tt2 + 1) * 128, :])
            p.wait("tensor", "dl", dg)
            p.wait("tensor", "dl", dwo)
            p.barrier_psum("tensor")
            for c in range(8):
                for half in range(2):
                    for gu, wu in ((0, 0), (0, 1), (1, 0)):
                        p.mm(pD[:, half * 512:(half + 1) * 512],
                             wgtbv[:, gu, c, :],
                             wobv[:, wu, c, half * 512:(half + 1) * 512],
                             c == 0 and gu == 0 and wu == 0, False)
            for half in range(2):
                p.mm(pD[:, half * 512:(half + 1) * 512], onesrow[0:1, 0:128],
                     bos[0:1, half * 512:(half + 1) * 512], False, True,
                     inc="pe" if half == 1 else None)
            peS2 = p.semc["pe"]
            p.wait("vector", "pe", peS2)
            p.wait("vector", "dl", dg)
            p.wait("vector", "ds", st_total)
            p.run("vector", lambda E: E.tensor_tensor(t1, pD[:, :], gbuf, ALU.mult))
            p.fence("vector")
            veO = p.run("vector", lambda E: E.tensor_tensor(
                obuf, t1, xbuf, ALU.add))
            p.wait("gpsimd", "ve", veO)
            st_total = p.store(out[b, tt2 * 128:(tt2 + 1) * 128, :], obuf)

    p.wait("gpsimd", "ds", p.semc["ds"])


def kernel(**inputs):
    x = np.asarray(inputs["L0_attn_residual"], dtype=np.float32)
    Wk = np.asarray(inputs["L0_attn_W_key"], dtype=np.float32)
    Wq = np.asarray(inputs["L0_attn_W_query"], dtype=np.float32)
    Wv = np.asarray(inputs["L0_attn_W_values"], dtype=np.float32)
    Wo = np.asarray(inputs["L0_attn_W_output"], dtype=np.float32)
    Bk = np.asarray(inputs["L0_attn_Bias_key"], dtype=np.float32)
    Bq = np.asarray(inputs["L0_attn_Bias_query"], dtype=np.float32)
    Bv = np.asarray(inputs["L0_attn_Bias_values"], dtype=np.float32)
    Bo = np.asarray(inputs["L0_attn_Bias_output"], dtype=np.float32)
    W1 = np.asarray(inputs["L0_ffn_W1"], dtype=np.float32)
    b1 = np.asarray(inputs["L0_ffn_b1"], dtype=np.float32)
    W2 = np.asarray(inputs["L0_ffn_W2"], dtype=np.float32)
    b2 = np.asarray(inputs["L0_ffn_b2"], dtype=np.float32)

    wk_t = np.ascontiguousarray(Wk.transpose(1, 0, 2).reshape(D, NHEAD * H))
    wq_t = np.ascontiguousarray(Wq.transpose(1, 0, 2).reshape(D, NHEAD * H))
    wv_t = np.ascontiguousarray(Wv.transpose(1, 0, 2).reshape(D, NHEAD * H))
    wout = np.ascontiguousarray(Wo.reshape(NHEAD * H, D))
    bk_m = np.ascontiguousarray(Bk.reshape(8, 128).T)
    bq_m = np.ascontiguousarray(Bq.reshape(8, 128).T)
    bv_row = np.ascontiguousarray(Bv.reshape(1, NHEAD * H))
    bo_row = np.ascontiguousarray(Bo.reshape(1, D))
    b1_m = np.ascontiguousarray(b1.reshape(32, 128).T)
    b2_row = np.ascontiguousarray(b2.reshape(1, D))
    identc = np.eye(128, dtype=np.float32)
    maskinv = (np.arange(128)[None, :] < np.arange(128)[:, None]).astype(np.uint8)

    nc = build_nc()
    in_maps = []
    for c in range(N_CORES):
        xs = x[B_CORE * c: B_CORE * (c + 1)]
        in_maps.append({
            "x_tm": np.ascontiguousarray(xs),
            "x_fm": np.ascontiguousarray(xs.transpose(0, 2, 1)),
            "wk_t": wk_t, "wq_t": wq_t, "wv_t": wv_t, "wout": wout,
            "bk_m": bk_m, "bq_m": bq_m, "bv_row": bv_row, "bo_row": bo_row,
            "w1": W1, "b1_m": b1_m, "w2": W2, "b2_row": b2_row,
            "identc": identc, "maskinv": maskinv,
        })
    res = run_bass_kernel_spmd(nc, in_maps, core_ids=list(range(N_CORES)))
    return np.concatenate([r["out"] for r in res.results], axis=0)


# version 04
# speedup vs baseline: 1.0105x; 1.0105x over previous
import numpy as np
import concourse.bass as bass
import concourse.mybir as mybir
from concourse.bass_utils import run_bass_kernel_spmd
from contextlib import ExitStack

F32 = mybir.dt.float32
BF16 = mybir.dt.bfloat16
U8 = mybir.dt.uint8
AF = mybir.ActivationFunctionType
ALU = mybir.AluOpType
AX = mybir.AxisListType.X

B_CORE = 4
S = 2048
D = 1024
NHEAD = 16
H = 64
DFF = 4096
MAGIC = 12582912.0
N_CORES = 8

ENGS = ("sync", "tensor", "vector", "scalar", "gpsimd")

DBG_PHASES = "QLFAW"
DBG_BATCHES = B_CORE
DBG_TAP = False


class Prog:
    def __init__(self, nc):
        self.nc = nc
        self.ops = {e: [] for e in ENGS}
        self.semh = {}
        self.semc = {}
        self.last_wait = {}

    def sem(self, name, handle):
        self.semh[name] = handle
        self.semc[name] = 0

    def wait(self, eng, sem, val):
        if val is None or val <= 0:
            return
        key = (eng, sem)
        if self.last_wait.get(key, -1) >= val:
            return
        self.last_wait[key] = val
        h = self.semh[sem]
        self.ops[eng].append(lambda E, h=h, v=val: E.wait_ge(h, v))

    OWN = {"tensor": "pe", "vector": "ve", "scalar": "ac"}

    def fence(self, eng):
        own = self.OWN[eng]
        self.wait(eng, own, self.semc[own])

    def run(self, eng, fn, inc=None, n=1):
        if inc is None and eng in ("vector", "scalar"):
            inc = self.OWN[eng]
        tok = None
        if inc is not None:
            self.semc[inc] += n
            tok = self.semc[inc]
            h = self.semh[inc]

            def cl(E, fn=fn, h=h, n=n):
                fn(E).then_inc(h, n)
        else:
            def cl(E, fn=fn):
                fn(E)
        self.ops[eng].append(cl)
        return tok

    def load(self, out_ap, in_ap):
        return self.run("sync", lambda E, o=out_ap, i=in_ap: E.dma_start(out=o, in_=i),
                        inc="dl", n=16)

    def store(self, out_ap, in_ap):
        return self.run("gpsimd", lambda E, o=out_ap, i=in_ap: E.dma_start(out=o, in_=i),
                        inc="ds", n=16)

    def mm(self, out, lhsT, rhs, start, stop, inc=None):
        return self.run("tensor",
                        lambda E, o=out, l=lhsT, r=rhs, s=start, p=stop:
                        E.matmul(o, l, r, start=s, stop=p), inc=inc)

    def barrier_psum(self, *engs):
        for e in engs:
            for sem in ("pe", "ve", "ac"):
                if sem == self.OWN.get(e):
                    continue
                self.wait(e, sem, self.semc[sem])


def build_nc():
    nc = bass.Bass("TRN2", target_bir_lowering=False)

    x_tm = nc.dram_tensor("x_tm", [B_CORE, S, D], F32, kind="ExternalInput").ap()
    x_fm = nc.dram_tensor("x_fm", [B_CORE, D, S], F32, kind="ExternalInput").ap()
    wk_t = nc.dram_tensor("wk_t", [D, NHEAD * H], F32, kind="ExternalInput").ap()
    wq_t = nc.dram_tensor("wq_t", [D, NHEAD * H], F32, kind="ExternalInput").ap()
    wv_t = nc.dram_tensor("wv_t", [D, NHEAD * H], F32, kind="ExternalInput").ap()
    wout = nc.dram_tensor("wout", [NHEAD * H, D], F32, kind="ExternalInput").ap()
    bk_m = nc.dram_tensor("bk_m", [128, 8], F32, kind="ExternalInput").ap()
    bq_m = nc.dram_tensor("bq_m", [128, 8], F32, kind="ExternalInput").ap()
    bv_row = nc.dram_tensor("bv_row", [1, NHEAD * H], F32, kind="ExternalInput").ap()
    bo_row = nc.dram_tensor("bo_row", [1, D], F32, kind="ExternalInput").ap()
    w1 = nc.dram_tensor("w1", [DFF, D], F32, kind="ExternalInput").ap()
    b1_m = nc.dram_tensor("b1_m", [128, 32], F32, kind="ExternalInput").ap()
    w2 = nc.dram_tensor("w2", [D, DFF], F32, kind="ExternalInput").ap()
    b2_row = nc.dram_tensor("b2_row", [1, D], F32, kind="ExternalInput").ap()
    identc = nc.dram_tensor("identc", [128, 128], F32, kind="ExternalInput").ap()
    maskinv = nc.dram_tensor("maskinv", [128, 128], U8, kind="ExternalInput").ap()
    out = nc.dram_tensor("out", [B_CORE, S, D], F32, kind="ExternalOutput").ap()

    kw = {"kind": "ExternalOutput"} if DBG_TAP else {}
    wq1t = nc.dram_tensor("wq1t", [D, DFF], BF16, **kw).ap()
    wq2t = nc.dram_tensor("wq2t", [DFF, D], BF16, **kw).ap()
    g_scr = nc.dram_tensor("g_scr", [B_CORE, S, D], F32, **kw).ap()
    wgh_scr = nc.dram_tensor("wgh_scr", [NHEAD * H, S], BF16, **kw).ap()
    wgl_scr = nc.dram_tensor("wgl_scr", [NHEAD * H, S], BF16, **kw).ap()
    awh = nc.dram_tensor("awh", [3, D, NHEAD * H], BF16).ap()
    awl = nc.dram_tensor("awl", [3, D, NHEAD * H], BF16).ap()
    woh = nc.dram_tensor("woh", [NHEAD * H, D], BF16).ap()
    wol = nc.dram_tensor("wol", [NHEAD * H, D], BF16).ap()

    with ExitStack() as ctx:
        E_ = ctx.enter_context
        xh = E_(nc.sbuf_tensor("xh", [128, 16384], BF16)).ap()
        xl = E_(nc.sbuf_tensor("xl", [128, 16384], BF16)).ap()
        r2 = E_(nc.sbuf_tensor("r2", [128, 8192], F32)).ap()
        kqK = E_(nc.sbuf_tensor("kqK", [128, 4096], F32)).ap()
        kqQ = E_(nc.sbuf_tensor("kqQ", [128, 4096], F32)).ap()
        vsl = E_(nc.sbuf_tensor("vsl", [128, 16 * 4 * 65], F32)).ap()
        srow1 = E_(nc.sbuf_tensor("srow1", [1, 16 * 65], F32)).ap()
        crow1 = E_(nc.sbuf_tensor("crow1", [1, 16 * 65], F32)).ap()
        rows = E_(nc.sbuf_tensor("rows", [1, 1536], F32)).ap()
        wkb = E_(nc.sbuf_tensor("wkb", [128, 2048], BF16)).ap()
        wqb = E_(nc.sbuf_tensor("wqb", [128, 2048], BF16)).ap()
        wvb = E_(nc.sbuf_tensor("wvb", [128, 4096], BF16)).ap()
        wq1b = E_(nc.sbuf_tensor("wq1b", [128, 2048], BF16)).ap()
        wq2b = E_(nc.sbuf_tensor("wq2b", [128, 2048], BF16)).ap()
        wgtb = E_(nc.sbuf_tensor("wgtb", [128, 2048], BF16)).ap()
        ident = E_(nc.sbuf_tensor("ident", [128, 128], F32)).ap()
        mski = E_(nc.sbuf_tensor("mski", [128, 128], U8)).ap()
        onest = E_(nc.sbuf_tensor("onest", [128, 128], F32)).ap()
        onescol = E_(nc.sbuf_tensor("onescol", [128, 1], F32)).ap()
        onesrow = E_(nc.sbuf_tensor("onesrow", [1, 512], F32)).ap()
        b1s = E_(nc.sbuf_tensor("b1s", [128, 32], F32)).ap()
        bks = E_(nc.sbuf_tensor("bks", [128, 8], F32)).ap()
        bqs = E_(nc.sbuf_tensor("bqs", [128, 8], F32)).ap()
        bvs = E_(nc.sbuf_tensor("bvs", [1, 1024], F32)).ap()
        bos = E_(nc.sbuf_tensor("bos", [1, 1024], F32)).ap()
        b2s = E_(nc.sbuf_tensor("b2s", [1, 1024], F32)).ap()
        colt = E_(nc.sbuf_tensor("colt", [128, 8], F32)).ap()

        pA = E_(nc.psum_tensor("pA", [128, 512], F32)).ap()
        pB = E_(nc.psum_tensor("pB", [128, 512], F32)).ap()
        pC = E_(nc.psum_tensor("pC", [128, 1024], F32)).ap()
        pD = E_(nc.psum_tensor("pD", [128, 1024], F32)).ap()
        pE = E_(nc.psum_tensor("pE", [128, 512], F32)).ap()
        pF = E_(nc.psum_tensor("pF", [128, 512], F32)).ap()

        sem_handles = {}
        for name in ("dl", "ds", "pe", "ve", "ac"):
            sem_handles[name] = E_(nc.semaphore(name))

        block = E_(nc.Block())
        p = Prog(nc)
        for name, h in sem_handles.items():
            p.sem(name, h)

        build_program(p, locals())

        @block.sync
        def _(sync):
            for f in p.ops["sync"]:
                f(sync)

        @block.tensor
        def _(tensor):
            for f in p.ops["tensor"]:
                f(tensor)

        @block.vector
        def _(vector):
            for f in p.ops["vector"]:
                f(vector)

        @block.scalar
        def _(scalar):
            for f in p.ops["scalar"]:
                f(scalar)

        @block.gpsimd
        def _(gpsimd):
            for f in p.ops["gpsimd"]:
                f(gpsimd)

    return nc


def build_program(p, t):
    xh, xl, r2 = t["xh"], t["xl"], t["r2"]
    kqK, kqQ, vsl = t["kqK"], t["kqQ"], t["vsl"]
    srow1, crow1, rows = t["srow1"], t["crow1"], t["rows"]
    wkb, wqb, wvb = t["wkb"], t["wqb"], t["wvb"]
    wq1b, wq2b, wgtb = t["wq1b"], t["wq2b"], t["wgtb"]
    gsb = t["kqK"][:, 0:1024]
    hfp = t["kqK"][:, 1024:1280]
    stage = wq1b[:, 1024:1280]
    stg2 = wgtb
    uh = wq1b[0:64, 0:1024]
    ul = wq1b[64:128, 0:1024]
    ustg = wq2b.bitcast(F32)[0:64, 0:512]
    ident, mski, onest = t["ident"], t["mski"], t["onest"]
    onescol, onesrow = t["onescol"], t["onesrow"]
    b1s, bks, bqs, bvs, bos, b2s, colt = (t["b1s"], t["bks"], t["bqs"], t["bvs"],
                                          t["bos"], t["b2s"], t["colt"])
    pA, pB, pC, pD, pE, pF = t["pA"], t["pB"], t["pC"], t["pD"], t["pE"], t["pF"]
    x_tm, x_fm = t["x_tm"], t["x_fm"]
    wk_t, wq_t, wv_t, wout = t["wk_t"], t["wq_t"], t["wv_t"], t["wout"]
    w1, b1_m, w2 = t["w1"], t["b1_m"], t["w2"]
    identc, maskinv, out = t["identc"], t["maskinv"], t["out"]
    wq1t, wq2t, g_scr = t["wq1t"], t["wq2t"], t["g_scr"]
    wgh_scr, wgl_scr = t["wgh_scr"], t["wgl_scr"]
    awh, awl, woh, wol = t["awh"], t["awl"], t["woh"], t["wol"]
    vslv = vsl.rearrange("p (c h e) -> p c h e", c=16, h=4)

    p.load(ident[:, :], identc)
    p.load(mski[:, :], maskinv)
    p.load(b1s[:, :], t["b1_m"])
    p.load(bks[:, :], t["bk_m"])
    p.load(bqs[:, :], t["bq_m"])
    p.load(bvs[:, :], t["bv_row"])
    p.load(bos[:, :], t["bo_row"])
    tok_consts = p.load(b2s[:, :], t["b2_row"])

    p.run("vector", lambda E: E.memset(onest[:, :], 1.0))
    p.run("vector", lambda E: E.memset(onescol[:, :], 1.0))
    p.run("vector", lambda E: E.memset(onesrow[:, :], 1.0))
    p.run("vector", lambda E: E.memset(vslv[:, :, :, 64:65], 1.0))
    p.fence("vector")

    f32buf = r2[:, 0:1024]
    st_aw = None

    def split_stage(src_ap, dst_h, dst_l):
        nonlocal st_aw
        p.wait("sync", "ds", st_aw)
        p.wait("sync", "ve", p.semc["ve"])
        dt_ = p.load(f32buf, src_ap)
        p.wait("vector", "dl", dt_)
        p.wait("vector", "ds", st_aw)
        p.run("vector", lambda E: E.tensor_copy(stg2[:, 0:1024], f32buf))
        p.fence("vector")
        vv = p.run("vector", lambda E: E.tensor_tensor(
            stg2[:, 1024:2048], f32buf, stg2[:, 0:1024], ALU.subtract))
        p.wait("gpsimd", "ve", vv)
        p.store(dst_h, stg2[:, 0:1024])
        st_aw = p.store(dst_l, stg2[:, 1024:2048])

    for mi, wsrc in enumerate((wk_t, wq_t, wv_t)):
        for i in range(8):
            split_stage(wsrc[i * 128:(i + 1) * 128, :],
                        awh[mi, i * 128:(i + 1) * 128, :],
                        awl[mi, i * 128:(i + 1) * 128, :])
    for i in range(8):
        split_stage(wout[i * 128:(i + 1) * 128, :],
                    woh[i * 128:(i + 1) * 128, :],
                    wol[i * 128:(i + 1) * 128, :])

    def quantize(w_dram, wq_dram, n_rows, acc_i, st_prev):
        n_cols = w_dram.shape[1]
        ntile = n_rows // 128
        ncol_t = n_cols // 128
        wbuf = [r2[:, 0:n_cols], r2[:, n_cols:2 * n_cols]]
        acc = colt[:, acc_i:acc_i + 1]
        sc = colt[:, acc_i + 1:acc_i + 2]
        red = colt[:, 6:7]
        p.run("vector", lambda E, a=acc: E.memset(a, 0.0))
        p.fence("vector")
        ve_toks = [None] * ntile
        for i in range(ntile):
            b = wbuf[i % 2]
            if i >= 2:
                p.wait("sync", "ve", ve_toks[i - 2])
            else:
                p.wait("sync", "ve", p.semc["ve"])
                p.wait("sync", "pe", p.semc["pe"])
                p.wait("sync", "ac", p.semc["ac"])
            dt_ = p.load(b, w_dram[i * 128:(i + 1) * 128, :])
            p.wait("scalar", "dl", dt_)
            aca = p.run("scalar", lambda E, b=b: E.activation(b, b, AF.Abs))
            p.wait("vector", "ac", aca)
            p.run("vector", lambda E, b=b, r=red: E.tensor_reduce(
                r, b, AX, ALU.add))
            p.fence("vector")
            ve_toks[i] = p.run("vector", lambda E, a=acc, r=red: E.tensor_tensor(
                a, a, r, ALU.add))
            p.fence("vector")
        p.wait("tensor", "ve", ve_toks[-1])
        p.wait("tensor", "dl", tok_consts)
        pe1 = p.mm(pF[0:1, 0:1], acc, onescol[:, :], True, True, inc="pe")
        p.wait("vector", "pe", pe1)
        sc1 = rows[0:1, 0:1]
        p.run("vector", lambda E, s=sc1: E.tensor_scalar(
            s, pF[0:1, 0:1], 1.0 / (n_rows * n_cols), 1e-5, ALU.mult, ALU.add))
        p.fence("vector")
        ve1 = p.run("vector", lambda E, s=sc1: E.reciprocal(s, s))
        p.wait("tensor", "ve", ve1)
        pe2 = p.mm(pF[0:128, 0:1], onesrow[0:1, 0:128], sc1, True, True, inc="pe")
        p.wait("vector", "pe", pe2)
        ve2 = p.run("vector", lambda E, s=sc: E.tensor_copy(s, pF[0:128, 0:1]))
        pe_cons = [None] * ntile
        st_toks = [st_prev, st_prev]
        last_store = st_prev
        vet_prev = ve2
        for i in range(ntile):
            b = wbuf[i % 2]
            if i == 0:
                p.wait("sync", "ve", ve2)
            else:
                p.wait("sync", "pe", pe_cons[i - 1])
            dt_ = p.load(b, w_dram[i * 128:(i + 1) * 128, :])
            p.wait("vector", "dl", dt_)
            p.run("vector", lambda E, b=b, s=sc: E.tensor_scalar(
                b, b, s, MAGIC, ALU.mult, ALU.add))
            p.fence("vector")
            p.run("vector", lambda E, b=b: E.tensor_scalar(
                b, b, MAGIC, -1.0, ALU.subtract, ALU.max))
            p.fence("vector")
            veq = p.run("vector", lambda E, b=b: E.tensor_scalar(
                b, b, 1.0, None, ALU.min))
            p.wait("tensor", "ve", veq)
            for j in range(ncol_t):
                sg = stage[:, (j % 2) * 128:(j % 2) * 128 + 128]
                p.wait("tensor", "ve", vet_prev)
                pet = p.run("tensor", lambda E, src=b[:, j * 128:(j + 1) * 128]:
                            E.transpose(pE[0:128, 0:128], src, ident[:, :]), inc="pe")
                p.wait("vector", "ds", st_toks[j % 2])
                p.wait("vector", "pe", pet)
                vet = p.run("vector", lambda E, sg=sg: E.tensor_copy(
                    sg, pE[0:128, 0:128]))
                vet_prev = vet
                p.wait("gpsimd", "ve", vet)
                stt = p.store(wq_dram[j * 128:(j + 1) * 128, i * 128:(i + 1) * 128], sg)
                st_toks[j % 2] = stt
                last_store = stt
            pe_cons[i] = p.semc["pe"]
        return last_store

    st_total = st_aw
    if "Q" in DBG_PHASES:
        st_total = quantize(w1, wq1t, DFF, 0, st_total)
        st_total = quantize(w2, wq2t, D, 2, st_total)
    q_done = st_total
    ve_xn = 0

    for b in range(DBG_BATCHES):
        xf = r2[:, 0:4096]
        sq = [r2[:, 4096:4608], r2[:, 4608:5120]]
        mb = r2[:, 5120:5632]
        rb = r2[:, 5632:6144]
        xfv = xf.rearrange("p (dc t) -> p dc t", dc=8)
        x_fmv = x_fm[b].rearrange("(dc p) t -> p dc t", p=128)
        m_row = rows[0:1, 0:512]
        var_row = rows[0:1, 512:1024]
        rstd_row = rows[0:1, 1024:1536]
        for tt in range(4 if "L" in DBG_PHASES else 0):
            p.wait("sync", "pe", p.semc["pe"])
            p.wait("sync", "ve", p.semc["ve"])
            p.wait("sync", "ac", p.semc["ac"])
            dt_ = p.load(xfv, x_fmv[:, :, tt * 512:(tt + 1) * 512])
            p.barrier_psum("tensor")
            p.wait("tensor", "dl", dt_)
            p.wait("scalar", "dl", dt_)
            for dc in range(8):
                p.mm(pA[0:1, 0:512], onescol[:, :], xfv[:, dc, :], dc == 0, dc == 7,
                     inc="pe" if dc == 7 else None)
            sq_mm = [None, None]
            for dc in range(8):
                p.wait("scalar", "pe", sq_mm[dc % 2])
                act = p.run("scalar", lambda E, dc=dc, s=sq[dc % 2]: E.activation(
                    s, xfv[:, dc, :], AF.Square))
                p.wait("tensor", "ac", act)
                sq_mm[dc % 2] = p.mm(pB[0:1, 0:512], onescol[:, :], sq[dc % 2],
                                     dc == 0, dc == 7, inc="pe")
            pe_stats = p.semc["pe"]
            p.wait("vector", "pe", pe_stats)
            p.run("vector", lambda E: E.tensor_scalar(
                m_row, pA[0:1, 0:512], 1.0 / D, None, ALU.mult))
            p.fence("vector")
            p.run("vector", lambda E: E.tensor_tensor(
                var_row, m_row, m_row, ALU.mult))
            p.fence("vector")
            p.run("vector", lambda E: E.scalar_tensor_tensor(
                var_row, pB[0:1, 0:512], 1.0 / D, var_row, ALU.mult, ALU.subtract))
            p.fence("vector")
            ve_v = p.run("vector", lambda E: E.tensor_scalar(
                var_row, var_row, 1e-5, None, ALU.add))
            p.wait("scalar", "ve", ve_v)
            p.run("scalar", lambda E: E.activation(rstd_row, var_row, AF.Ln))
            p.fence("scalar")
            ac_r = p.run("scalar", lambda E: E.activation(
                rstd_row, rstd_row, AF.Exp, scale=-0.5))
            p.wait("tensor", "ve", ve_v)
            p.wait("tensor", "ac", ac_r)
            p.mm(pC[:, 0:512], onesrow[0:1, 0:128], m_row, True, True)
            pe_bc = p.mm(pC[:, 512:1024], onesrow[0:1, 0:128], rstd_row, True, True,
                         inc="pe")
            p.wait("vector", "pe", pe_bc)
            p.run("vector", lambda E: E.tensor_copy(mb, pC[:, 0:512]))
            p.run("vector", lambda E: E.tensor_copy(rb, pC[:, 512:1024]))
            p.wait("vector", "pe", p.semc["pe"])
            p.wait("vector", "ds", st_total if b > 0 else 0)
            p.fence("vector")
            for dc in range(8):
                p.run("vector", lambda E, i=xfv[:, dc, :]: E.scalar_tensor_tensor(
                    i, i, 1.0, mb, ALU.mult, ALU.subtract))
            p.fence("vector")
            for dc in range(8):
                p.run("vector", lambda E, i=xfv[:, dc, :]: E.tensor_tensor(
                    i, i, rb, ALU.mult))
            p.fence("vector")
            for dc in range(8):
                o = xh[:, dc * 2048 + tt * 512: dc * 2048 + (tt + 1) * 512]
                p.run("vector", lambda E, o=o, i=xfv[:, dc, :]: E.tensor_copy(o, i))
            p.fence("vector")
            for dc in range(8):
                oh = xh[:, dc * 2048 + tt * 512: dc * 2048 + (tt + 1) * 512]
                ol = xl[:, dc * 2048 + tt * 512: dc * 2048 + (tt + 1) * 512]
                ve_xn = p.run("vector", lambda E, oh=oh, ol=ol, i=xfv[:, dc, :]:
                              E.tensor_tensor(ol, i, oh, ALU.subtract))

        r2bf = r2.bitcast(BF16)
        hh = r2bf[:, 0:8192]
        hl = r2bf[:, 8192:16384]
        wq1v = wq1t.rearrange("(dc p) f -> p dc f", p=128)
        wq1bv = wq1b.rearrange("p (u dc f) -> p u dc f", u=2, dc=8)
        wq2bv = wq2b.rearrange("p (u f) -> p u f", u=2)
        if "F" in DBG_PHASES:
            p.wait("sync", "ds", q_done)
        d_toks = {}
        pe_wq = {}
        ve_h_prev = 0
        hdone = {}
        rl = {}

        def emit_ffn2(fc, u, gi):
            ac_h, ve_h = hdone[fc]
            p.wait("tensor", "ac", ac_h)
            p.wait("tensor", "ve", ve_h)
            for sub in range(2):
                gp = pC if sub == 0 else pD
                for half in range(2):
                    for hs in (hh, hl):
                        p.mm(gp[:, half * 512:(half + 1) * 512],
                             hs[:, fc * 256 + sub * 128:
                                fc * 256 + sub * 128 + 128],
                             wq2bv[:, u, half * 512:(half + 1) * 512],
                             fc == 0 and hs is hh, False,
                             inc="pe" if (sub == 1 and half == 1 and hs is hl)
                             else None)
            pe_wq[gi] = p.semc["pe"]

        for ts in range(8 if "F" in DBG_PHASES else 0):
            t0 = ts * 256
            for fc in range(32):
                u = fc % 2
                gi = ts * 32 + fc
                if gi == 0:
                    p.wait("sync", "pe", p.semc["pe"])
                    p.wait("sync", "ac", p.semc["ac"])
                    p.wait("sync", "ve", p.semc["ve"])
                    p.wait("sync", "ds", st_total)
                if gi >= 2:
                    p.wait("sync", "pe", pe_wq[gi - 2])
                p.load(wq1bv[:, u, :, :], wq1v[:, :, fc * 128:(fc + 1) * 128])
                d_toks[gi] = p.load(wq2bv[:, u, :], wq2t[fc * 128:(fc + 1) * 128, :])
                if gi == 0:
                    p.wait("tensor", "ve", ve_xn)
                    p.barrier_psum("tensor")
                p.wait("tensor", "dl", d_toks[gi])
                hp = pA if fc % 2 == 0 else pB
                p.wait("tensor", "ac", rl.get(gi - 2, 0))
                for dc in range(8):
                    for cm, xs in ((0, xh), (1, xl)):
                        p.mm(hp[:, 0:256], wq1bv[:, u, dc, :],
                             xs[:, dc * 2048 + t0: dc * 2048 + t0 + 256],
                             dc == 0 and cm == 0, dc == 7 and cm == 1,
                             inc="pe" if (dc == 7 and cm == 1) else None)
                pe_h = p.semc["pe"]
                p.wait("scalar", "pe", pe_h)
                p.wait("scalar", "ve", ve_h_prev)
                rl[gi] = p.run("scalar", lambda E, i=hp[:, 0:256], fc=fc:
                               E.activation(hfp[:, :], i, AF.Relu,
                                            bias=b1s[:, fc:fc + 1]))
                p.fence("scalar")
                ac_h = p.run("scalar", lambda E, o=hh[:, fc * 256:(fc + 1) * 256]:
                             E.activation(o, hfp[:, :], AF.Copy))
                p.wait("vector", "ac", ac_h)
                ve_h = p.run("vector", lambda E, o=hl[:, fc * 256:(fc + 1) * 256],
                             h=hh[:, fc * 256:(fc + 1) * 256]:
                             E.tensor_tensor(o, hfp[:, :], h, ALU.subtract))
                ve_h_prev = ve_h
                hdone[fc] = (ac_h, ve_h)
                if fc > 0:
                    emit_ffn2(fc - 1, (fc - 1) % 2, gi - 1)
            emit_ffn2(31, 31 % 2, ts * 32 + 31)
            for sub in range(2):
                gp = pC if sub == 0 else pD
                for half in range(2):
                    p.mm(gp[:, half * 512:(half + 1) * 512], onesrow[0:1, 0:128],
                         b2s[0:1, half * 512:(half + 1) * 512], False, True,
                         inc="pe" if (sub == 1 and half == 1) else None)
            pe_g = p.semc["pe"]
            pe_wq[ts * 32 + 31] = pe_g
            p.wait("scalar", "pe", pe_g)
            for sub in range(2):
                gp = pC if sub == 0 else pD
                p.wait("scalar", "ds", st_total)
                ac_g = p.run("scalar", lambda E, gp=gp: E.activation(
                    gsb[:, :], gp[:, :], AF.Silu))
                p.wait("gpsimd", "ac", ac_g)
                st_total = p.store(g_scr[b, t0 + sub * 128: t0 + sub * 128 + 128, :],
                                   gsb[:, :])

        esl = r2
        bank_ac = {0: p.semc["ac"], 1: p.semc["ac"]}
        ustg_ds = [st_total, st_total]
        nbufs = wq2b.bitcast(F32)
        nbuf = [nbufs[0:65, 0:512], nbufs[0:65, 512:1024]]
        zrow = [rows[0:1, 0:512], rows[0:1, 512:1024]]
        pend = None
        ui = 0
        prev_fix = p.semc["ve"]
        zx_par = {0: 0, 1: 0}
        zb_par = {0: 0, 1: 0}
        veW_prev = 0

        def flush_norm():
            nonlocal pend, st_total, veW_prev
            if pend is None:
                return
            par0, n0, qt0 = pend["par"], pend["n"], pend["qt"]
            p.wait("tensor", "ac", pend["zx"])
            p.wait("tensor", "ve", veW_prev)
            zb = p.mm(pA[0:64, :], onesrow[0:1, 0:64], zrow[par0], True, True,
                      inc="pe")
            zb_par[par0] = zb
            nb = nbuf[par0]
            p.wait("vector", "pe", zb)
            p.wait("vector", "ds", ustg_ds[par0])
            p.fence("vector")
            uhs = uh[:, par0 * 512:(par0 + 1) * 512]
            uls = ul[:, par0 * 512:(par0 + 1) * 512]
            p.run("vector", lambda E, nb=nb: E.tensor_tensor(
                nb[0:64, :], nb[0:64, :], pA[0:64, :], ALU.mult))
            p.fence("vector")
            p.run("vector", lambda E, o=uhs, nb=nb: E.tensor_copy(o, nb[0:64, :]))
            p.fence("vector")
            veW = p.run("vector", lambda E, o=uls, nb=nb, h=uhs: E.tensor_tensor(
                o, nb[0:64, :], h, ALU.subtract))
            veW_prev = veW
            p.wait("gpsimd", "ve", veW)
            p.store(wgh_scr[64 * n0: 64 * n0 + 64, qt0 * 512:(qt0 + 1) * 512], uhs)
            st_total = p.store(
                wgl_scr[64 * n0: 64 * n0 + 64, qt0 * 512:(qt0 + 1) * 512], uls)
            ustg_ds[par0] = st_total
            pend = None
        awhv = awh.rearrange("m (dc p) n -> m p dc n", p=128)
        awlv = awl.rearrange("m (dc p) n -> m p dc n", p=128)
        wkbv = wkb.rearrange("p (u dc n) -> p u dc n", u=2, dc=8)
        wqbv = wqb.rearrange("p (u dc n) -> p u dc n", u=2, dc=8)
        wvbv = wvb.rearrange("p (u dc n) -> p u dc n", u=2, dc=8)
        for qd in range(4 if "A" in DBG_PHASES else 0):
            for pp in range(2):
                gp_ = 2 * qd + pp
                p.wait("sync", "pe", p.semc["pe"])
                p.load(wkbv[:, 0, :, :], awhv[0, :, :, gp_ * 128:(gp_ + 1) * 128])
                p.load(wkbv[:, 1, :, :], awlv[0, :, :, gp_ * 128:(gp_ + 1) * 128])
                p.load(wqbv[:, 0, :, :], awhv[1, :, :, gp_ * 128:(gp_ + 1) * 128])
                dq = p.load(wqbv[:, 1, :, :], awlv[1, :, :, gp_ * 128:(gp_ + 1) * 128])
                p.wait("tensor", "dl", dq)
                p.wait("tensor", "ve", p.semc["ve"])
                p.barrier_psum("tensor")
                for tt in range(4):
                    for ps, wb in ((pA, wkbv), (pB, wqbv)):
                        for dc in range(8):
                            for wu, xs in ((0, xh), (0, xl), (1, xh)):
                                p.mm(ps[:, :], wb[:, wu, dc, :],
                                     xs[:, dc * 2048 + tt * 512:
                                        dc * 2048 + (tt + 1) * 512],
                                     dc == 0 and wu == 0 and xs is xh,
                                     dc == 7 and wu == 1,
                                     inc="pe" if (dc == 7 and wu == 1) else None)
                    peQ = p.semc["pe"]
                    p.wait("vector", "pe", peQ)
                    ksl = kqK[:, pp * 2048 + tt * 512: pp * 2048 + (tt + 1) * 512]
                    p.run("vector", lambda E, o=ksl, g=gp_: E.tensor_scalar(
                        o, pA[:, :], bks[:, g:g + 1], None, ALU.add))
                    qsl = kqQ[:, pp * 2048 + tt * 512: pp * 2048 + (tt + 1) * 512]
                    veKQ = p.run("vector", lambda E, o=qsl, g=gp_: E.tensor_scalar(
                        o, pB[:, :], bqs[:, g:g + 1], None, ALU.add))
                    p.wait("tensor", "ve", veKQ)
            p.load(wvbv[:, 0, :, :], awhv[2, :, :, qd * 256:(qd + 1) * 256])
            dv = p.load(wvbv[:, 1, :, :], awlv[2, :, :, qd * 256:(qd + 1) * 256])
            p.wait("tensor", "dl", dv)
            for kc in range(16):
                for dc in range(8):
                    for xs, wu in ((xh, 0), (xh, 1), (xl, 0)):
                        p.mm(pF[:, 0:256],
                             xs[:, dc * 2048 + kc * 128: dc * 2048 + (kc + 1) * 128],
                             wvbv[:, wu, dc, :],
                             dc == 0 and xs is xh and wu == 0, False)
                peV = p.mm(pF[:, 0:256], onesrow[0:1, 0:128],
                           bvs[0:1, qd * 256:(qd + 1) * 256], False, True, inc="pe")
                p.wait("vector", "pe", peV)
                veV = p.run("vector", lambda E, kc=kc: E.tensor_copy(
                    vslv[:, kc, :, 0:64],
                    pF[:, 0:256].rearrange("p (h e) -> p h e", h=4)))
                p.wait("tensor", "ve", veV)
            for j in range(4):
                n = 4 * qd + j
                pbase = 64 * (j % 2)
                poff = (j // 2) * 2048
                p.wait("tensor", "ac", bank_ac[0])
                p.wait("tensor", "ve", p.semc["ve"])
                for cb in range(4):
                    for u in range(4):
                        kc = 4 * cb + u
                        p.mm(pC[0:1, u * 65: u * 65 + 65], onescol[:, :],
                             vslv[:, kc, j, :], True, True,
                             inc="pe" if u == 3 else None)
                    peC = p.semc["pe"]
                    p.wait("vector", "pe", peC)
                    vcp = p.run("vector", lambda E, cb=cb: E.tensor_copy(
                        crow1[0:1, cb * 260:(cb + 1) * 260], pC[0:1, 0:260]))
                    p.wait("tensor", "ve", vcp)
                def scp(dst_kc, src_kc):
                    p.fence("vector")
                    p.run("vector", lambda E, d=dst_kc, s=src_kc: E.tensor_copy(
                        srow1[0:1, d * 65:(d + 1) * 65],
                        srow1[0:1, s * 65:(s + 1) * 65]))

                def sacc(dst_kc, src_kcs, copy_first):
                    first = copy_first
                    for kc in src_kcs:
                        p.fence("vector")
                        src = crow1[0:1, kc * 65:(kc + 1) * 65]
                        dsl = srow1[0:1, dst_kc * 65:(dst_kc + 1) * 65]
                        if first:
                            p.run("vector", lambda E, d=dsl, s=src: E.tensor_copy(d, s))
                            first = False
                        else:
                            p.run("vector", lambda E, d=dsl, s=src: E.tensor_tensor(
                                d, d, s, ALU.add))
                sacc(11, [12, 13, 14, 15], copy_first=True)
                scp(7, 11)
                sacc(7, [8, 9, 10, 11], copy_first=False)
                scp(3, 7)
                sacc(3, [4, 5, 6, 7], copy_first=False)
                p.fence("vector")
                ve_sfx = p.semc["ve"]
                p.wait("tensor", "ve", ve_sfx)
                for qt in range(4):
                    nmat = 4 * qt + 4
                    has_tail = nmat < 16
                    par = ui % 2
                    fix_tok = {}
                    exp_tok = {}

                    def emit_pv(kc, last):
                        p.wait("tensor", "ac", exp_tok[kc // 2])
                        p.wait("tensor", "ve", fix_tok.get(kc, 0))
                        p.mm(pE[0:65, :], vslv[:, kc, j, :],
                             esl[:, kc * 512:(kc + 1) * 512], kc == 0, last,
                             inc="pe" if last else None)

                    for g in range(nmat // 2):
                        sp, bank = (pC, 0) if g % 2 == 0 else (pD, 1)
                        p.wait("tensor", "ac", bank_ac[bank])
                        if g == 0 and pend is not None:
                            p.wait("tensor", "ve", pend["ncp"])
                        for u in range(2):
                            kc = 2 * g + u
                            lhs = kqK[pbase:pbase + 64,
                                      poff + kc * 128: poff + (kc + 1) * 128]
                            rhs = kqQ[pbase:pbase + 64,
                                      poff + qt * 512: poff + (qt + 1) * 512]
                            p.mm(sp[:, u * 512:(u + 1) * 512], lhs, rhs, True, True,
                                 inc="pe" if u == 1 else None)
                        p.wait("scalar", "pe", p.semc["pe"])
                        p.wait("scalar", "ve", prev_fix)
                        exp_tok[g] = p.run(
                            "scalar", lambda E, g=g, sp=sp: E.activation(
                                esl[:, 2 * g * 512: 2 * g * 512 + 1024],
                                sp[:, :], AF.Exp, scale=0.125))
                        bank_ac[bank] = exp_tok[g]
                        p.wait("vector", "ac", exp_tok[g])
                        for u in range(2):
                            kc = 2 * g + u
                            j2 = kc - 4 * qt
                            if 0 <= j2 <= 3:
                                tk = None
                                for j3 in range(j2):
                                    tk = p.run("vector", lambda E, kc=kc, j3=j3:
                                               E.memset(
                                                   esl[:, kc * 512 + j3 * 128:
                                                       kc * 512 + (j3 + 1) * 128],
                                                   1.0))
                                tk = p.run("vector", lambda E, kc=kc, j2=j2:
                                           E.copy_predicated(
                                               esl[:, kc * 512 + j2 * 128:
                                                   kc * 512 + (j2 + 1) * 128],
                                               mski[:, :], onest[:, :]))
                                fix_tok[kc] = tk
                        if g >= 1:
                            emit_pv(2 * (g - 1), False)
                            emit_pv(2 * (g - 1) + 1, False)
                    emit_pv(nmat - 2, False)
                    emit_pv(nmat - 1, not has_tail)
                    if has_tail:
                        p.wait("tensor", "ve", ve_sfx)
                        sfx = srow1[0:1, (4 * qt + 3) * 65:(4 * qt + 3) * 65 + 65]
                        p.mm(pE[0:65, :], sfx, onesrow[0:1, 0:512],
                             False, True, inc="pe")
                    peU = p.semc["pe"]
                    flush_norm()
                    p.wait("vector", "pe", peU)
                    p.wait("vector", "ac", zx_par[par])
                    ncp = p.run("vector", lambda E, nb=nbuf[par]: E.tensor_copy(
                        nb, pE[0:65, :]))
                    p.wait("scalar", "ve", ncp)
                    p.wait("scalar", "pe", zb_par[par])
                    p.run("scalar", lambda E, z=zrow[par], nb=nbuf[par]:
                          E.activation(z, nb[64:65, :], AF.Ln))
                    p.fence("scalar")
                    zx = p.run("scalar", lambda E, z=zrow[par]: E.activation(
                        z, z, AF.Exp, scale=-1.0))
                    zx_par[par] = zx
                    pend = {"par": par, "ncp": ncp, "zx": zx, "n": n, "qt": qt}
                    prev_fix = max(fix_tok.values()) if fix_tok else prev_fix
                    ui += 1
        flush_norm()

        xlf = xl.bitcast(F32)
        wob = xh
        xbuf = xlf[:, 0:1024]
        gbuf = xlf[:, 1024:2048]
        t1 = xlf[:, 2048:3072]
        obuf = xlf[:, 3072:4096]
        wobv = wob.rearrange("p (u c d) -> p u c d", u=2, c=8)
        wgtbv = wgtb.rearrange("p (u c k) -> p u c k", u=2, c=8)
        wohv = woh.rearrange("(c p) d -> p c d", p=128)
        wolv = wol.rearrange("(c p) d -> p c d", p=128)
        wghv = wgh_scr.rearrange("(c p) s -> p c s", p=128)
        wglv = wgl_scr.rearrange("(c p) s -> p c s", p=128)
        if "W" in DBG_PHASES:
            p.wait("sync", "pe", p.semc["pe"])
            p.wait("sync", "ve", p.semc["ve"])
            p.load(wobv[:, 0, :, :], wohv)
            dwo = p.load(wobv[:, 1, :, :], wolv)
        for tt2 in range(16 if "W" in DBG_PHASES else 0):
            p.wait("sync", "pe", p.semc["pe"])
            p.wait("sync", "ve", p.semc["ve"])
            p.wait("sync", "ds", st_total)
            p.load(wgtbv[:, 0, :, :], wghv[:, :, tt2 * 128:(tt2 + 1) * 128])
            p.load(wgtbv[:, 1, :, :], wglv[:, :, tt2 * 128:(tt2 + 1) * 128])
            p.load(xbuf, x_tm[b, tt2 * 128:(tt2 + 1) * 128, :])
            dg = p.load(gbuf, g_scr[b, tt2 * 128:(tt2 + 1) * 128, :])
            p.wait("tensor", "dl", dg)
            p.wait("tensor", "dl", dwo)
            p.barrier_psum("tensor")
            for c in range(8):
                for half in range(2):
                    for gu, wu in ((0, 0), (0, 1), (1, 0)):
                        p.mm(pD[:, half * 512:(half + 1) * 512],
                             wgtbv[:, gu, c, :],
                             wobv[:, wu, c, half * 512:(half + 1) * 512],
                             c == 0 and gu == 0 and wu == 0, False)
            for half in range(2):
                p.mm(pD[:, half * 512:(half + 1) * 512], onesrow[0:1, 0:128],
                     bos[0:1, half * 512:(half + 1) * 512], False, True,
                     inc="pe" if half == 1 else None)
            peS2 = p.semc["pe"]
            p.wait("vector", "pe", peS2)
            p.wait("vector", "dl", dg)
            p.wait("vector", "ds", st_total)
            p.run("vector", lambda E: E.tensor_tensor(t1, pD[:, :], gbuf, ALU.mult))
            p.fence("vector")
            veO = p.run("vector", lambda E: E.tensor_tensor(
                obuf, t1, xbuf, ALU.add))
            p.wait("gpsimd", "ve", veO)
            st_total = p.store(out[b, tt2 * 128:(tt2 + 1) * 128, :], obuf)

    p.wait("gpsimd", "ds", p.semc["ds"])


def kernel(**inputs):
    x = np.asarray(inputs["L0_attn_residual"], dtype=np.float32)
    Wk = np.asarray(inputs["L0_attn_W_key"], dtype=np.float32)
    Wq = np.asarray(inputs["L0_attn_W_query"], dtype=np.float32)
    Wv = np.asarray(inputs["L0_attn_W_values"], dtype=np.float32)
    Wo = np.asarray(inputs["L0_attn_W_output"], dtype=np.float32)
    Bk = np.asarray(inputs["L0_attn_Bias_key"], dtype=np.float32)
    Bq = np.asarray(inputs["L0_attn_Bias_query"], dtype=np.float32)
    Bv = np.asarray(inputs["L0_attn_Bias_values"], dtype=np.float32)
    Bo = np.asarray(inputs["L0_attn_Bias_output"], dtype=np.float32)
    W1 = np.asarray(inputs["L0_ffn_W1"], dtype=np.float32)
    b1 = np.asarray(inputs["L0_ffn_b1"], dtype=np.float32)
    W2 = np.asarray(inputs["L0_ffn_W2"], dtype=np.float32)
    b2 = np.asarray(inputs["L0_ffn_b2"], dtype=np.float32)

    wk_t = np.ascontiguousarray(Wk.transpose(1, 0, 2).reshape(D, NHEAD * H))
    wq_t = np.ascontiguousarray(Wq.transpose(1, 0, 2).reshape(D, NHEAD * H))
    wv_t = np.ascontiguousarray(Wv.transpose(1, 0, 2).reshape(D, NHEAD * H))
    wout = np.ascontiguousarray(Wo.reshape(NHEAD * H, D))
    bk_m = np.ascontiguousarray(Bk.reshape(8, 128).T)
    bq_m = np.ascontiguousarray(Bq.reshape(8, 128).T)
    bv_row = np.ascontiguousarray(Bv.reshape(1, NHEAD * H))
    bo_row = np.ascontiguousarray(Bo.reshape(1, D))
    b1_m = np.ascontiguousarray(b1.reshape(32, 128).T)
    b2_row = np.ascontiguousarray(b2.reshape(1, D))
    identc = np.eye(128, dtype=np.float32)
    maskinv = (np.arange(128)[None, :] < np.arange(128)[:, None]).astype(np.uint8)

    nc = build_nc()
    in_maps = []
    for c in range(N_CORES):
        xs = x[B_CORE * c: B_CORE * (c + 1)]
        in_maps.append({
            "x_tm": np.ascontiguousarray(xs),
            "x_fm": np.ascontiguousarray(xs.transpose(0, 2, 1)),
            "wk_t": wk_t, "wq_t": wq_t, "wv_t": wv_t, "wout": wout,
            "bk_m": bk_m, "bq_m": bq_m, "bv_row": bv_row, "bo_row": bo_row,
            "w1": W1, "b1_m": b1_m, "w2": W2, "b2_row": b2_row,
            "identc": identc, "maskinv": maskinv,
        })
    res = run_bass_kernel_spmd(nc, in_maps, core_ids=list(range(N_CORES)))
    return np.concatenate([r["out"] for r in res.results], axis=0)


# version 06
# speedup vs baseline: 1.2702x; 1.2570x over previous
import numpy as np
import concourse.bass as bass
import concourse.mybir as mybir
from concourse.bass_utils import run_bass_kernel_spmd
from contextlib import ExitStack

F32 = mybir.dt.float32
BF16 = mybir.dt.bfloat16
U8 = mybir.dt.uint8
AF = mybir.ActivationFunctionType
ALU = mybir.AluOpType
AX = mybir.AxisListType.X

B_CORE = 4
S = 2048
D = 1024
NHEAD = 16
H = 64
DFF = 4096
MAGIC = 12582912.0
N_CORES = 8

ENGS = ("sync", "tensor", "vector", "scalar", "gpsimd")

DBG_PHASES = "QLFAW"
DBG_BATCHES = B_CORE
DBG_TAP = False


class Prog:
    def __init__(self, nc):
        self.nc = nc
        self.ops = {e: [] for e in ENGS}
        self.semh = {}
        self.semc = {}
        self.last_wait = {}

    def sem(self, name, handle):
        self.semh[name] = handle
        self.semc[name] = 0

    def wait(self, eng, sem, val):
        if val is None or val <= 0:
            return
        key = (eng, sem)
        if self.last_wait.get(key, -1) >= val:
            return
        self.last_wait[key] = val
        h = self.semh[sem]
        self.ops[eng].append(lambda E, h=h, v=val: E.wait_ge(h, v))

    OWN = {"tensor": "pe", "vector": "ve", "scalar": "ac"}

    def fence(self, eng):
        own = self.OWN[eng]
        self.wait(eng, own, self.semc[own])

    def run(self, eng, fn, inc=None, n=1):
        if inc is None and eng in ("vector", "scalar"):
            inc = self.OWN[eng]
        tok = None
        if inc is not None:
            self.semc[inc] += n
            tok = self.semc[inc]
            h = self.semh[inc]

            def cl(E, fn=fn, h=h, n=n):
                fn(E).then_inc(h, n)
        else:
            def cl(E, fn=fn):
                fn(E)
        self.ops[eng].append(cl)
        return tok

    def load(self, out_ap, in_ap):
        return self.run("sync", lambda E, o=out_ap, i=in_ap: E.dma_start(out=o, in_=i),
                        inc="dl", n=16)

    def store(self, out_ap, in_ap):
        return self.run("gpsimd", lambda E, o=out_ap, i=in_ap: E.dma_start(out=o, in_=i),
                        inc="ds", n=16)

    def mm(self, out, lhsT, rhs, start, stop, inc=None):
        return self.run("tensor",
                        lambda E, o=out, l=lhsT, r=rhs, s=start, p=stop:
                        E.matmul(o, l, r, start=s, stop=p), inc=inc)

    def barrier_psum(self, *engs):
        for e in engs:
            for sem in ("pe", "ve", "ac"):
                if sem == self.OWN.get(e):
                    continue
                self.wait(e, sem, self.semc[sem])


def build_nc():
    nc = bass.Bass("TRN2", target_bir_lowering=False)

    x_tm = nc.dram_tensor("x_tm", [B_CORE, S, D], F32, kind="ExternalInput").ap()
    x_fm = nc.dram_tensor("x_fm", [B_CORE, D, S], F32, kind="ExternalInput").ap()
    wk_t = nc.dram_tensor("wk_t", [D, NHEAD * H], F32, kind="ExternalInput").ap()
    wq_t = nc.dram_tensor("wq_t", [D, NHEAD * H], F32, kind="ExternalInput").ap()
    wv_t = nc.dram_tensor("wv_t", [D, NHEAD * H], F32, kind="ExternalInput").ap()
    wout = nc.dram_tensor("wout", [NHEAD * H, D], F32, kind="ExternalInput").ap()
    bk_m = nc.dram_tensor("bk_m", [128, 8], F32, kind="ExternalInput").ap()
    bq_m = nc.dram_tensor("bq_m", [128, 8], F32, kind="ExternalInput").ap()
    bv_row = nc.dram_tensor("bv_row", [1, NHEAD * H], F32, kind="ExternalInput").ap()
    bo_row = nc.dram_tensor("bo_row", [1, D], F32, kind="ExternalInput").ap()
    w1 = nc.dram_tensor("w1", [DFF, D], F32, kind="ExternalInput").ap()
    b1_m = nc.dram_tensor("b1_m", [128, 32], F32, kind="ExternalInput").ap()
    w2 = nc.dram_tensor("w2", [D, DFF], F32, kind="ExternalInput").ap()
    b2_row = nc.dram_tensor("b2_row", [1, D], F32, kind="ExternalInput").ap()
    identc = nc.dram_tensor("identc", [128, 128], F32, kind="ExternalInput").ap()
    maskinv = nc.dram_tensor("maskinv", [128, 128], U8, kind="ExternalInput").ap()
    out = nc.dram_tensor("out", [B_CORE, S, D], F32, kind="ExternalOutput").ap()

    kw = {"kind": "ExternalOutput"} if DBG_TAP else {}
    wq1t = nc.dram_tensor("wq1t", [D, DFF], BF16, **kw).ap()
    wq2t = nc.dram_tensor("wq2t", [DFF, D], BF16, **kw).ap()
    g_scr = nc.dram_tensor("g_scr", [B_CORE, S, D], F32, **kw).ap()
    wgh_scr = nc.dram_tensor("wgh_scr", [NHEAD * H, S], BF16, **kw).ap()
    wgl_scr = nc.dram_tensor("wgl_scr", [NHEAD * H, S], BF16, **kw).ap()
    awh = nc.dram_tensor("awh", [3, D, NHEAD * H], BF16).ap()
    awl = nc.dram_tensor("awl", [3, D, NHEAD * H], BF16).ap()
    woh = nc.dram_tensor("woh", [NHEAD * H, D], BF16).ap()
    wol = nc.dram_tensor("wol", [NHEAD * H, D], BF16).ap()

    with ExitStack() as ctx:
        E_ = ctx.enter_context
        xh = E_(nc.sbuf_tensor("xh", [128, 16384], BF16)).ap()
        xl = E_(nc.sbuf_tensor("xl", [128, 16384], BF16)).ap()
        r2 = E_(nc.sbuf_tensor("r2", [128, 8192], F32)).ap()
        kqK = E_(nc.sbuf_tensor("kqK", [128, 4096], F32)).ap()
        kqQ = E_(nc.sbuf_tensor("kqQ", [128, 4096], F32)).ap()
        vsl = E_(nc.sbuf_tensor("vsl", [128, 16 * 4 * 65], F32)).ap()
        srow1 = E_(nc.sbuf_tensor("srow1", [1, 16 * 65], F32)).ap()
        crow1 = E_(nc.sbuf_tensor("crow1", [1, 16 * 65], F32)).ap()
        rows = E_(nc.sbuf_tensor("rows", [1, 1536], F32)).ap()
        wkb = E_(nc.sbuf_tensor("wkb", [128, 2048], BF16)).ap()
        wqb = E_(nc.sbuf_tensor("wqb", [128, 2048], BF16)).ap()
        wvb = E_(nc.sbuf_tensor("wvb", [128, 4096], BF16)).ap()
        wq1b = E_(nc.sbuf_tensor("wq1b", [128, 4096], BF16)).ap()
        wq2b = E_(nc.sbuf_tensor("wq2b", [128, 3072], BF16)).ap()
        wgtb = E_(nc.sbuf_tensor("wgtb", [128, 2048], BF16)).ap()
        ident = E_(nc.sbuf_tensor("ident", [128, 128], F32)).ap()
        mski = E_(nc.sbuf_tensor("mski", [128, 128], U8)).ap()
        onest = E_(nc.sbuf_tensor("onest", [128, 128], F32)).ap()
        onescol = E_(nc.sbuf_tensor("onescol", [128, 1], F32)).ap()
        onesrow = E_(nc.sbuf_tensor("onesrow", [1, 512], F32)).ap()
        b1s = E_(nc.sbuf_tensor("b1s", [128, 32], F32)).ap()
        bks = E_(nc.sbuf_tensor("bks", [128, 8], F32)).ap()
        bqs = E_(nc.sbuf_tensor("bqs", [128, 8], F32)).ap()
        bvs = E_(nc.sbuf_tensor("bvs", [1, 1024], F32)).ap()
        bos = E_(nc.sbuf_tensor("bos", [1, 1024], F32)).ap()
        b2s = E_(nc.sbuf_tensor("b2s", [1, 1024], F32)).ap()
        colt = E_(nc.sbuf_tensor("colt", [128, 8], F32)).ap()

        pA = E_(nc.psum_tensor("pA", [128, 512], F32)).ap()
        pB = E_(nc.psum_tensor("pB", [128, 512], F32)).ap()
        pC = E_(nc.psum_tensor("pC", [128, 1024], F32)).ap()
        pD = E_(nc.psum_tensor("pD", [128, 1024], F32)).ap()
        pE = E_(nc.psum_tensor("pE", [128, 512], F32)).ap()
        pF = E_(nc.psum_tensor("pF", [128, 512], F32)).ap()

        sem_handles = {}
        for name in ("dl", "ds", "pe", "ve", "ac"):
            sem_handles[name] = E_(nc.semaphore(name))

        block = E_(nc.Block())
        p = Prog(nc)
        for name, h in sem_handles.items():
            p.sem(name, h)

        build_program(p, locals())

        @block.sync
        def _(sync):
            for f in p.ops["sync"]:
                f(sync)

        @block.tensor
        def _(tensor):
            for f in p.ops["tensor"]:
                f(tensor)

        @block.vector
        def _(vector):
            for f in p.ops["vector"]:
                f(vector)

        @block.scalar
        def _(scalar):
            for f in p.ops["scalar"]:
                f(scalar)

        @block.gpsimd
        def _(gpsimd):
            for f in p.ops["gpsimd"]:
                f(gpsimd)

    return nc


def build_program(p, t):
    xh, xl, r2 = t["xh"], t["xl"], t["r2"]
    kqK, kqQ, vsl = t["kqK"], t["kqQ"], t["vsl"]
    srow1, crow1, rows = t["srow1"], t["crow1"], t["rows"]
    wkb, wqb, wvb = t["wkb"], t["wqb"], t["wvb"]
    wq1b, wq2b, wgtb = t["wq1b"], t["wq2b"], t["wgtb"]
    gsb = t["kqK"][:, 0:1024]
    hfp = t["kqK"][:, 1024:1280]
    stage = wq1b[:, 3072:3328]
    stg2 = wgtb
    uh = wq1b[0:64, 0:1024]
    ul = wq1b[64:128, 0:1024]
    ident, mski, onest = t["ident"], t["mski"], t["onest"]
    onescol, onesrow = t["onescol"], t["onesrow"]
    b1s, bks, bqs, bvs, bos, b2s, colt = (t["b1s"], t["bks"], t["bqs"], t["bvs"],
                                          t["bos"], t["b2s"], t["colt"])
    pA, pB, pC, pD, pE, pF = t["pA"], t["pB"], t["pC"], t["pD"], t["pE"], t["pF"]
    x_tm, x_fm = t["x_tm"], t["x_fm"]
    wk_t, wq_t, wv_t, wout = t["wk_t"], t["wq_t"], t["wv_t"], t["wout"]
    w1, b1_m, w2 = t["w1"], t["b1_m"], t["w2"]
    identc, maskinv, out = t["identc"], t["maskinv"], t["out"]
    wq1t, wq2t, g_scr = t["wq1t"], t["wq2t"], t["g_scr"]
    wgh_scr, wgl_scr = t["wgh_scr"], t["wgl_scr"]
    awh, awl, woh, wol = t["awh"], t["awl"], t["woh"], t["wol"]
    vslv = vsl.rearrange("p (c h e) -> p c h e", c=16, h=4)

    p.load(ident[:, :], identc)
    p.load(mski[:, :], maskinv)
    p.load(b1s[:, :], t["b1_m"])
    p.load(bks[:, :], t["bk_m"])
    p.load(bqs[:, :], t["bq_m"])
    p.load(bvs[:, :], t["bv_row"])
    p.load(bos[:, :], t["bo_row"])
    tok_consts = p.load(b2s[:, :], t["b2_row"])

    p.run("vector", lambda E: E.memset(onest[:, :], 1.0))
    p.run("vector", lambda E: E.memset(onescol[:, :], 1.0))
    p.run("vector", lambda E: E.memset(onesrow[:, :], 1.0))
    p.run("vector", lambda E: E.memset(vslv[:, :, :, 64:65], 1.0))
    p.fence("vector")

    f32buf = r2[:, 0:1024]
    st_aw = None

    def split_stage(src_ap, dst_h, dst_l):
        nonlocal st_aw
        p.wait("sync", "ds", st_aw)
        p.wait("sync", "ve", p.semc["ve"])
        dt_ = p.load(f32buf, src_ap)
        p.wait("vector", "dl", dt_)
        p.wait("vector", "ds", st_aw)
        p.run("vector", lambda E: E.tensor_copy(stg2[:, 0:1024], f32buf))
        p.fence("vector")
        vv = p.run("vector", lambda E: E.tensor_tensor(
            stg2[:, 1024:2048], f32buf, stg2[:, 0:1024], ALU.subtract))
        p.wait("gpsimd", "ve", vv)
        p.store(dst_h, stg2[:, 0:1024])
        st_aw = p.store(dst_l, stg2[:, 1024:2048])

    for mi, wsrc in enumerate((wk_t, wq_t, wv_t)):
        for i in range(8):
            split_stage(wsrc[i * 128:(i + 1) * 128, :],
                        awh[mi, i * 128:(i + 1) * 128, :],
                        awl[mi, i * 128:(i + 1) * 128, :])
    for i in range(8):
        split_stage(wout[i * 128:(i + 1) * 128, :],
                    woh[i * 128:(i + 1) * 128, :],
                    wol[i * 128:(i + 1) * 128, :])

    def quantize(w_dram, wq_dram, n_rows, acc_i, st_prev):
        n_cols = w_dram.shape[1]
        ntile = n_rows // 128
        ncol_t = n_cols // 128
        wbuf = [r2[:, 0:n_cols], r2[:, n_cols:2 * n_cols]]
        acc = colt[:, acc_i:acc_i + 1]
        sc = colt[:, acc_i + 1:acc_i + 2]
        red = colt[:, 6:7]
        p.run("vector", lambda E, a=acc: E.memset(a, 0.0))
        p.fence("vector")
        ve_toks = [None] * ntile
        for i in range(ntile):
            b = wbuf[i % 2]
            if i >= 2:
                p.wait("sync", "ve", ve_toks[i - 2])
            else:
                p.wait("sync", "ve", p.semc["ve"])
                p.wait("sync", "pe", p.semc["pe"])
                p.wait("sync", "ac", p.semc["ac"])
            dt_ = p.load(b, w_dram[i * 128:(i + 1) * 128, :])
            p.wait("scalar", "dl", dt_)
            aca = p.run("scalar", lambda E, b=b: E.activation(b, b, AF.Abs))
            p.wait("vector", "ac", aca)
            p.run("vector", lambda E, b=b, r=red: E.tensor_reduce(
                r, b, AX, ALU.add))
            p.fence("vector")
            ve_toks[i] = p.run("vector", lambda E, a=acc, r=red: E.tensor_tensor(
                a, a, r, ALU.add))
            p.fence("vector")
        p.wait("tensor", "ve", ve_toks[-1])
        p.wait("tensor", "dl", tok_consts)
        pe1 = p.mm(pF[0:1, 0:1], acc, onescol[:, :], True, True, inc="pe")
        p.wait("vector", "pe", pe1)
        sc1 = rows[0:1, 0:1]
        p.run("vector", lambda E, s=sc1: E.tensor_scalar(
            s, pF[0:1, 0:1], 1.0 / (n_rows * n_cols), 1e-5, ALU.mult, ALU.add))
        p.fence("vector")
        ve1 = p.run("vector", lambda E, s=sc1: E.reciprocal(s, s))
        p.wait("tensor", "ve", ve1)
        pe2 = p.mm(pF[0:128, 0:1], onesrow[0:1, 0:128], sc1, True, True, inc="pe")
        p.wait("vector", "pe", pe2)
        ve2 = p.run("vector", lambda E, s=sc: E.tensor_copy(s, pF[0:128, 0:1]))
        pe_cons = [None] * ntile
        st_toks = [st_prev, st_prev]
        last_store = st_prev
        vet_prev = ve2
        for i in range(ntile):
            b = wbuf[i % 2]
            if i == 0:
                p.wait("sync", "ve", ve2)
            else:
                p.wait("sync", "pe", pe_cons[i - 1])
            dt_ = p.load(b, w_dram[i * 128:(i + 1) * 128, :])
            p.wait("vector", "dl", dt_)
            p.run("vector", lambda E, b=b, s=sc: E.tensor_scalar(
                b, b, s, MAGIC, ALU.mult, ALU.add))
            p.fence("vector")
            p.run("vector", lambda E, b=b: E.tensor_scalar(
                b, b, MAGIC, -1.0, ALU.subtract, ALU.max))
            p.fence("vector")
            veq = p.run("vector", lambda E, b=b: E.tensor_scalar(
                b, b, 1.0, None, ALU.min))
            p.wait("tensor", "ve", veq)
            for j in range(ncol_t):
                sg = stage[:, (j % 2) * 128:(j % 2) * 128 + 128]
                p.wait("tensor", "ve", vet_prev)
                pet = p.run("tensor", lambda E, src=b[:, j * 128:(j + 1) * 128]:
                            E.transpose(pE[0:128, 0:128], src, ident[:, :]), inc="pe")
                p.wait("vector", "ds", st_toks[j % 2])
                p.wait("vector", "pe", pet)
                vet = p.run("vector", lambda E, sg=sg: E.tensor_copy(
                    sg, pE[0:128, 0:128]))
                vet_prev = vet
                p.wait("gpsimd", "ve", vet)
                stt = p.store(wq_dram[j * 128:(j + 1) * 128, i * 128:(i + 1) * 128], sg)
                st_toks[j % 2] = stt
                last_store = stt
            pe_cons[i] = p.semc["pe"]
        return last_store

    st_total = st_aw
    if "Q" in DBG_PHASES:
        st_total = quantize(w1, wq1t, DFF, 0, st_total)
        st_total = quantize(w2, wq2t, D, 2, st_total)
    q_done = st_total
    ve_xn = 0

    for b in range(DBG_BATCHES):
        xf = r2[:, 0:4096]
        sq = [r2[:, 4096:4608], r2[:, 4608:5120]]
        mb = r2[:, 5120:5632]
        rb = r2[:, 5632:6144]
        xfv = xf.rearrange("p (dc t) -> p dc t", dc=8)
        x_fmv = x_fm[b].rearrange("(dc p) t -> p dc t", p=128)
        m_row = rows[0:1, 0:512]
        var_row = rows[0:1, 512:1024]
        rstd_row = rows[0:1, 1024:1536]
        for tt in range(4 if "L" in DBG_PHASES else 0):
            p.wait("sync", "pe", p.semc["pe"])
            p.wait("sync", "ve", p.semc["ve"])
            p.wait("sync", "ac", p.semc["ac"])
            dt_ = p.load(xfv, x_fmv[:, :, tt * 512:(tt + 1) * 512])
            p.barrier_psum("tensor")
            p.wait("tensor", "dl", dt_)
            p.wait("scalar", "dl", dt_)
            for dc in range(8):
                p.mm(pA[0:1, 0:512], onescol[:, :], xfv[:, dc, :], dc == 0, dc == 7,
                     inc="pe" if dc == 7 else None)
            sq_mm = [None, None]
            for dc in range(8):
                p.wait("scalar", "pe", sq_mm[dc % 2])
                act = p.run("scalar", lambda E, dc=dc, s=sq[dc % 2]: E.activation(
                    s, xfv[:, dc, :], AF.Square))
                p.wait("tensor", "ac", act)
                sq_mm[dc % 2] = p.mm(pB[0:1, 0:512], onescol[:, :], sq[dc % 2],
                                     dc == 0, dc == 7, inc="pe")
            pe_stats = p.semc["pe"]
            p.wait("vector", "pe", pe_stats)
            p.run("vector", lambda E: E.tensor_scalar(
                m_row, pA[0:1, 0:512], 1.0 / D, None, ALU.mult))
            p.fence("vector")
            p.run("vector", lambda E: E.tensor_tensor(
                var_row, m_row, m_row, ALU.mult))
            p.fence("vector")
            p.run("vector", lambda E: E.scalar_tensor_tensor(
                var_row, pB[0:1, 0:512], 1.0 / D, var_row, ALU.mult, ALU.subtract))
            p.fence("vector")
            ve_v = p.run("vector", lambda E: E.tensor_scalar(
                var_row, var_row, 1e-5, None, ALU.add))
            p.wait("scalar", "ve", ve_v)
            p.run("scalar", lambda E: E.activation(rstd_row, var_row, AF.Ln))
            p.fence("scalar")
            ac_r = p.run("scalar", lambda E: E.activation(
                rstd_row, rstd_row, AF.Exp, scale=-0.5))
            p.wait("tensor", "ve", ve_v)
            p.wait("tensor", "ac", ac_r)
            p.mm(pC[:, 0:512], onesrow[0:1, 0:128], m_row, True, True)
            pe_bc = p.mm(pC[:, 512:1024], onesrow[0:1, 0:128], rstd_row, True, True,
                         inc="pe")
            p.wait("vector", "pe", pe_bc)
            p.run("vector", lambda E: E.tensor_copy(mb, pC[:, 0:512]))
            p.run("vector", lambda E: E.tensor_copy(rb, pC[:, 512:1024]))
            p.wait("vector", "pe", p.semc["pe"])
            p.wait("vector", "ds", st_total if b > 0 else 0)
            p.fence("vector")
            for dc in range(8):
                p.run("vector", lambda E, i=xfv[:, dc, :]: E.scalar_tensor_tensor(
                    i, i, 1.0, mb, ALU.mult, ALU.subtract))
            p.fence("vector")
            for dc in range(8):
                p.run("vector", lambda E, i=xfv[:, dc, :]: E.tensor_tensor(
                    i, i, rb, ALU.mult))
            p.fence("vector")
            for dc in range(8):
                o = xh[:, dc * 2048 + tt * 512: dc * 2048 + (tt + 1) * 512]
                p.run("vector", lambda E, o=o, i=xfv[:, dc, :]: E.tensor_copy(o, i))
            p.fence("vector")
            for dc in range(8):
                oh = xh[:, dc * 2048 + tt * 512: dc * 2048 + (tt + 1) * 512]
                ol = xl[:, dc * 2048 + tt * 512: dc * 2048 + (tt + 1) * 512]
                ve_xn = p.run("vector", lambda E, oh=oh, ol=ol, i=xfv[:, dc, :]:
                              E.tensor_tensor(ol, i, oh, ALU.subtract))

        r2bf = r2.bitcast(BF16)
        hh = r2bf[:, 0:8192]
        hl = r2bf[:, 8192:16384]
        wq1v = wq1t.rearrange("(dc p) f -> p dc f", p=128)
        wq1bv = wq1b.rearrange("p (u dc f) -> p u dc f", u=4, dc=8)
        wq2bv = wq2b.rearrange("p (u f) -> p u f", u=3)
        if "F" in DBG_PHASES:
            p.wait("sync", "ds", q_done)
        d_toks = {}
        pe_wq = {}
        ve_h_prev = 0
        hdone = {}
        rl = {}

        def emit_ffn2(fc, u, gi):
            ac_h, ve_h = hdone[fc]
            p.wait("tensor", "ac", ac_h)
            p.wait("tensor", "ve", ve_h)
            for sub in range(2):
                gp = pC if sub == 0 else pD
                for half in range(2):
                    for hs in (hh, hl):
                        p.mm(gp[:, half * 512:(half + 1) * 512],
                             hs[:, fc * 256 + sub * 128:
                                fc * 256 + sub * 128 + 128],
                             wq2bv[:, u, half * 512:(half + 1) * 512],
                             fc == 0 and hs is hh, False,
                             inc="pe" if (sub == 1 and half == 1 and hs is hl)
                             else None)
            pe_wq[gi] = p.semc["pe"]

        for ts in range(8 if "F" in DBG_PHASES else 0):
            t0 = ts * 256
            for fc in range(32):
                gi = ts * 32 + fc
                u = gi % 3
                if gi == 0:
                    p.wait("sync", "pe", p.semc["pe"])
                    p.wait("sync", "ac", p.semc["ac"])
                    p.wait("sync", "ve", p.semc["ve"])
                    p.wait("sync", "ds", st_total)
                if gi >= 3:
                    p.wait("sync", "pe", pe_wq[gi - 3])
                p.load(wq1bv[:, u, :, :], wq1v[:, :, fc * 128:(fc + 1) * 128])
                d_toks[gi] = p.load(wq2bv[:, u, :], wq2t[fc * 128:(fc + 1) * 128, :])
                if gi == 0:
                    p.wait("tensor", "ve", ve_xn)
                    p.barrier_psum("tensor")
                p.wait("tensor", "dl", d_toks[gi])
                hp = pA if fc % 2 == 0 else pB
                p.wait("tensor", "ac", rl.get(gi - 2, 0))
                for dc in range(8):
                    for cm, xs in ((0, xh), (1, xl)):
                        p.mm(hp[:, 0:256], wq1bv[:, u, dc, :],
                             xs[:, dc * 2048 + t0: dc * 2048 + t0 + 256],
                             dc == 0 and cm == 0, dc == 7 and cm == 1,
                             inc="pe" if (dc == 7 and cm == 1) else None)
                pe_h = p.semc["pe"]
                p.wait("scalar", "pe", pe_h)
                p.wait("scalar", "ve", ve_h_prev)
                rl[gi] = p.run("scalar", lambda E, i=hp[:, 0:256], fc=fc:
                               E.activation(hfp[:, :], i, AF.Relu,
                                            bias=b1s[:, fc:fc + 1]))
                p.fence("scalar")
                ac_h = p.run("scalar", lambda E, o=hh[:, fc * 256:(fc + 1) * 256]:
                             E.activation(o, hfp[:, :], AF.Copy))
                p.wait("vector", "ac", ac_h)
                ve_h = p.run("vector", lambda E, o=hl[:, fc * 256:(fc + 1) * 256],
                             h=hh[:, fc * 256:(fc + 1) * 256]:
                             E.tensor_tensor(o, hfp[:, :], h, ALU.subtract))
                ve_h_prev = ve_h
                hdone[fc] = (ac_h, ve_h)
                if fc > 0:
                    emit_ffn2(fc - 1, (gi - 1) % 3, gi - 1)
            emit_ffn2(31, (ts * 32 + 31) % 3, ts * 32 + 31)
            for sub in range(2):
                gp = pC if sub == 0 else pD
                for half in range(2):
                    p.mm(gp[:, half * 512:(half + 1) * 512], onesrow[0:1, 0:128],
                         b2s[0:1, half * 512:(half + 1) * 512], False, True,
                         inc="pe" if (sub == 1 and half == 1) else None)
            pe_g = p.semc["pe"]
            pe_wq[ts * 32 + 31] = pe_g
            p.wait("scalar", "pe", pe_g)
            for sub in range(2):
                gp = pC if sub == 0 else pD
                p.wait("scalar", "ds", st_total)
                ac_g = p.run("scalar", lambda E, gp=gp: E.activation(
                    gsb[:, :], gp[:, :], AF.Silu))
                p.wait("gpsimd", "ac", ac_g)
                st_total = p.store(g_scr[b, t0 + sub * 128: t0 + sub * 128 + 128, :],
                                   gsb[:, :])

        esl = r2
        bank_ac = {0: p.semc["ac"], 1: p.semc["ac"]}
        ustg_ds = [st_total, st_total]
        nbufs = wq2b.bitcast(F32)
        nbuf = [nbufs[0:65, 0:512], nbufs[0:65, 512:1024]]
        zrow = [rows[0:1, 0:512], rows[0:1, 512:1024]]
        pend = None
        ui = 0
        prev_fix = p.semc["ve"]
        zx_par = {0: 0, 1: 0}
        zb_par = {0: 0, 1: 0}
        veW_prev = 0

        def flush_norm():
            nonlocal pend, st_total, veW_prev
            if pend is None:
                return
            par0, n0, qt0 = pend["par"], pend["n"], pend["qt"]
            p.wait("tensor", "ac", pend["zx"])
            p.wait("tensor", "ve", veW_prev)
            zb = p.mm(pA[0:64, :], onesrow[0:1, 0:64], zrow[par0], True, True,
                      inc="pe")
            zb_par[par0] = zb
            nb = nbuf[par0]
            p.wait("vector", "pe", zb)
            p.wait("vector", "ds", ustg_ds[par0])
            p.fence("vector")
            uhs = uh[:, par0 * 512:(par0 + 1) * 512]
            uls = ul[:, par0 * 512:(par0 + 1) * 512]
            p.run("vector", lambda E, nb=nb: E.tensor_tensor(
                nb[0:64, :], nb[0:64, :], pA[0:64, :], ALU.mult))
            p.fence("vector")
            p.run("vector", lambda E, o=uhs, nb=nb: E.tensor_copy(o, nb[0:64, :]))
            p.fence("vector")
            veW = p.run("vector", lambda E, o=uls, nb=nb, h=uhs: E.tensor_tensor(
                o, nb[0:64, :], h, ALU.subtract))
            veW_prev = veW
            p.wait("gpsimd", "ve", veW)
            p.store(wgh_scr[64 * n0: 64 * n0 + 64, qt0 * 512:(qt0 + 1) * 512], uhs)
            st_total = p.store(
                wgl_scr[64 * n0: 64 * n0 + 64, qt0 * 512:(qt0 + 1) * 512], uls)
            ustg_ds[par0] = st_total
            pend = None
        awhv = awh.rearrange("m (dc p) n -> m p dc n", p=128)
        awlv = awl.rearrange("m (dc p) n -> m p dc n", p=128)
        wkbv = wkb.rearrange("p (u dc n) -> p u dc n", u=2, dc=8)
        wqbv = wqb.rearrange("p (u dc n) -> p u dc n", u=2, dc=8)
        wvbv = wvb.rearrange("p (u dc n) -> p u dc n", u=2, dc=8)
        for qd in range(4 if "A" in DBG_PHASES else 0):
            for pp in range(2):
                gp_ = 2 * qd + pp
                p.wait("sync", "pe", p.semc["pe"])
                p.load(wkbv[:, 0, :, :], awhv[0, :, :, gp_ * 128:(gp_ + 1) * 128])
                p.load(wkbv[:, 1, :, :], awlv[0, :, :, gp_ * 128:(gp_ + 1) * 128])
                p.load(wqbv[:, 0, :, :], awhv[1, :, :, gp_ * 128:(gp_ + 1) * 128])
                dq = p.load(wqbv[:, 1, :, :], awlv[1, :, :, gp_ * 128:(gp_ + 1) * 128])
                p.wait("tensor", "dl", dq)
                p.wait("tensor", "ve", p.semc["ve"])
                p.barrier_psum("tensor")
                for tt in range(4):
                    for ps, wb in ((pA, wkbv), (pB, wqbv)):
                        for dc in range(8):
                            for wu, xs in ((0, xh), (0, xl), (1, xh)):
                                p.mm(ps[:, :], wb[:, wu, dc, :],
                                     xs[:, dc * 2048 + tt * 512:
                                        dc * 2048 + (tt + 1) * 512],
                                     dc == 0 and wu == 0 and xs is xh,
                                     dc == 7 and wu == 1,
                                     inc="pe" if (dc == 7 and wu == 1) else None)
                    peQ = p.semc["pe"]
                    p.wait("vector", "pe", peQ)
                    ksl = kqK[:, pp * 2048 + tt * 512: pp * 2048 + (tt + 1) * 512]
                    p.run("vector", lambda E, o=ksl, g=gp_: E.tensor_scalar(
                        o, pA[:, :], bks[:, g:g + 1], None, ALU.add))
                    qsl = kqQ[:, pp * 2048 + tt * 512: pp * 2048 + (tt + 1) * 512]
                    veKQ = p.run("vector", lambda E, o=qsl, g=gp_: E.tensor_scalar(
                        o, pB[:, :], bqs[:, g:g + 1], None, ALU.add))
                    p.wait("tensor", "ve", veKQ)
            p.load(wvbv[:, 0, :, :], awhv[2, :, :, qd * 256:(qd + 1) * 256])
            dv = p.load(wvbv[:, 1, :, :], awlv[2, :, :, qd * 256:(qd + 1) * 256])
            p.wait("tensor", "dl", dv)
            for kc in range(16):
                for dc in range(8):
                    for xs, wu in ((xh, 0), (xh, 1), (xl, 0)):
                        p.mm(pF[:, 0:256],
                             xs[:, dc * 2048 + kc * 128: dc * 2048 + (kc + 1) * 128],
                             wvbv[:, wu, dc, :],
                             dc == 0 and xs is xh and wu == 0, False)
                peV = p.mm(pF[:, 0:256], onesrow[0:1, 0:128],
                           bvs[0:1, qd * 256:(qd + 1) * 256], False, True, inc="pe")
                p.wait("vector", "pe", peV)
                veV = p.run("vector", lambda E, kc=kc: E.tensor_copy(
                    vslv[:, kc, :, 0:64],
                    pF[:, 0:256].rearrange("p (h e) -> p h e", h=4)))
                p.wait("tensor", "ve", veV)
            for j in range(4):
                n = 4 * qd + j
                pbase = 64 * (j % 2)
                poff = (j // 2) * 2048
                p.wait("tensor", "ac", bank_ac[0])
                p.wait("tensor", "ve", p.semc["ve"])
                for cb in range(4):
                    for u in range(4):
                        kc = 4 * cb + u
                        p.mm(pC[0:1, u * 65: u * 65 + 65], onescol[:, :],
                             vslv[:, kc, j, :], True, True,
                             inc="pe" if u == 3 else None)
                    peC = p.semc["pe"]
                    p.wait("vector", "pe", peC)
                    vcp = p.run("vector", lambda E, cb=cb: E.tensor_copy(
                        crow1[0:1, cb * 260:(cb + 1) * 260], pC[0:1, 0:260]))
                    p.wait("tensor", "ve", vcp)
                def scp(dst_kc, src_kc):
                    p.fence("vector")
                    p.run("vector", lambda E, d=dst_kc, s=src_kc: E.tensor_copy(
                        srow1[0:1, d * 65:(d + 1) * 65],
                        srow1[0:1, s * 65:(s + 1) * 65]))

                def sacc(dst_kc, src_kcs, copy_first):
                    first = copy_first
                    for kc in src_kcs:
                        p.fence("vector")
                        src = crow1[0:1, kc * 65:(kc + 1) * 65]
                        dsl = srow1[0:1, dst_kc * 65:(dst_kc + 1) * 65]
                        if first:
                            p.run("vector", lambda E, d=dsl, s=src: E.tensor_copy(d, s))
                            first = False
                        else:
                            p.run("vector", lambda E, d=dsl, s=src: E.tensor_tensor(
                                d, d, s, ALU.add))
                sacc(11, [12, 13, 14, 15], copy_first=True)
                scp(7, 11)
                sacc(7, [8, 9, 10, 11], copy_first=False)
                scp(3, 7)
                sacc(3, [4, 5, 6, 7], copy_first=False)
                p.fence("vector")
                ve_sfx = p.semc["ve"]
                p.wait("tensor", "ve", ve_sfx)
                for qt in range(4):
                    nmat = 4 * qt + 4
                    has_tail = nmat < 16
                    par = ui % 2
                    fix_tok = {}
                    exp_tok = {}

                    def emit_pv(kc, last):
                        p.wait("tensor", "ac", exp_tok[kc // 2])
                        p.wait("tensor", "ve", fix_tok.get(kc, 0))
                        p.mm(pE[0:65, :], vslv[:, kc, j, :],
                             esl[:, kc * 512:(kc + 1) * 512], kc == 0, last,
                             inc="pe" if last else None)

                    for g in range(nmat // 2):
                        sp, bank = (pC, 0) if g % 2 == 0 else (pD, 1)
                        p.wait("tensor", "ac", bank_ac[bank])
                        if g == 0 and pend is not None:
                            p.wait("tensor", "ve", pend["ncp"])
                        for u in range(2):
                            kc = 2 * g + u
                            lhs = kqK[pbase:pbase + 64,
                                      poff + kc * 128: poff + (kc + 1) * 128]
                            rhs = kqQ[pbase:pbase + 64,
                                      poff + qt * 512: poff + (qt + 1) * 512]
                            p.mm(sp[:, u * 512:(u + 1) * 512], lhs, rhs, True, True,
                                 inc="pe" if u == 1 else None)
                        p.wait("scalar", "pe", p.semc["pe"])
                        p.wait("scalar", "ve", prev_fix)
                        exp_tok[g] = p.run(
                            "scalar", lambda E, g=g, sp=sp: E.activation(
                                esl[:, 2 * g * 512: 2 * g * 512 + 1024],
                                sp[:, :], AF.Exp, scale=0.125))
                        bank_ac[bank] = exp_tok[g]
                        p.wait("vector", "ac", exp_tok[g])
                        for u in range(2):
                            kc = 2 * g + u
                            j2 = kc - 4 * qt
                            if 0 <= j2 <= 3:
                                tk = None
                                for j3 in range(j2):
                                    tk = p.run("vector", lambda E, kc=kc, j3=j3:
                                               E.memset(
                                                   esl[:, kc * 512 + j3 * 128:
                                                       kc * 512 + (j3 + 1) * 128],
                                                   1.0))
                                tk = p.run("vector", lambda E, kc=kc, j2=j2:
                                           E.copy_predicated(
                                               esl[:, kc * 512 + j2 * 128:
                                                   kc * 512 + (j2 + 1) * 128],
                                               mski[:, :], onest[:, :]))
                                fix_tok[kc] = tk
                        if g >= 1:
                            emit_pv(2 * (g - 1), False)
                            emit_pv(2 * (g - 1) + 1, False)
                    emit_pv(nmat - 2, False)
                    emit_pv(nmat - 1, not has_tail)
                    if has_tail:
                        p.wait("tensor", "ve", ve_sfx)
                        sfx = srow1[0:1, (4 * qt + 3) * 65:(4 * qt + 3) * 65 + 65]
                        p.mm(pE[0:65, :], sfx, onesrow[0:1, 0:512],
                             False, True, inc="pe")
                    peU = p.semc["pe"]
                    flush_norm()
                    p.wait("vector", "pe", peU)
                    p.wait("vector", "ac", zx_par[par])
                    ncp = p.run("vector", lambda E, nb=nbuf[par]: E.tensor_copy(
                        nb, pE[0:65, :]))
                    p.wait("scalar", "ve", ncp)
                    p.wait("scalar", "pe", zb_par[par])
                    p.run("scalar", lambda E, z=zrow[par], nb=nbuf[par]:
                          E.activation(z, nb[64:65, :], AF.Ln))
                    p.fence("scalar")
                    zx = p.run("scalar", lambda E, z=zrow[par]: E.activation(
                        z, z, AF.Exp, scale=-1.0))
                    zx_par[par] = zx
                    pend = {"par": par, "ncp": ncp, "zx": zx, "n": n, "qt": qt}
                    prev_fix = max(fix_tok.values()) if fix_tok else prev_fix
                    ui += 1
        flush_norm()

        xlf = xl.bitcast(F32)
        wob = xh
        xbuf = xlf[:, 0:1024]
        gbuf = xlf[:, 1024:2048]
        t1 = xlf[:, 2048:3072]
        obuf = xlf[:, 3072:4096]
        wobv = wob.rearrange("p (u c d) -> p u c d", u=2, c=8)
        wgtbv = wgtb.rearrange("p (u c k) -> p u c k", u=2, c=8)
        wohv = woh.rearrange("(c p) d -> p c d", p=128)
        wolv = wol.rearrange("(c p) d -> p c d", p=128)
        wghv = wgh_scr.rearrange("(c p) s -> p c s", p=128)
        wglv = wgl_scr.rearrange("(c p) s -> p c s", p=128)
        if "W" in DBG_PHASES:
            p.wait("sync", "pe", p.semc["pe"])
            p.wait("sync", "ve", p.semc["ve"])
            p.load(wobv[:, 0, :, :], wohv)
            dwo = p.load(wobv[:, 1, :, :], wolv)
        for tt2 in range(16 if "W" in DBG_PHASES else 0):
            p.wait("sync", "pe", p.semc["pe"])
            p.wait("sync", "ve", p.semc["ve"])
            p.wait("sync", "ds", st_total)
            p.load(wgtbv[:, 0, :, :], wghv[:, :, tt2 * 128:(tt2 + 1) * 128])
            p.load(wgtbv[:, 1, :, :], wglv[:, :, tt2 * 128:(tt2 + 1) * 128])
            p.load(xbuf, x_tm[b, tt2 * 128:(tt2 + 1) * 128, :])
            dg = p.load(gbuf, g_scr[b, tt2 * 128:(tt2 + 1) * 128, :])
            p.wait("tensor", "dl", dg)
            p.wait("tensor", "dl", dwo)
            p.barrier_psum("tensor")
            for c in range(8):
                for half in range(2):
                    for gu, wu in ((0, 0), (0, 1), (1, 0)):
                        p.mm(pD[:, half * 512:(half + 1) * 512],
                             wgtbv[:, gu, c, :],
                             wobv[:, wu, c, half * 512:(half + 1) * 512],
                             c == 0 and gu == 0 and wu == 0, False)
            for half in range(2):
                p.mm(pD[:, half * 512:(half + 1) * 512], onesrow[0:1, 0:128],
                     bos[0:1, half * 512:(half + 1) * 512], False, True,
                     inc="pe" if half == 1 else None)
            peS2 = p.semc["pe"]
            p.wait("vector", "pe", peS2)
            p.wait("vector", "dl", dg)
            p.wait("vector", "ds", st_total)
            p.run("vector", lambda E: E.tensor_tensor(t1, pD[:, :], gbuf, ALU.mult))
            p.fence("vector")
            veO = p.run("vector", lambda E: E.tensor_tensor(
                obuf, t1, xbuf, ALU.add))
            p.wait("gpsimd", "ve", veO)
            st_total = p.store(out[b, tt2 * 128:(tt2 + 1) * 128, :], obuf)

    p.wait("gpsimd", "ds", p.semc["ds"])


def kernel(**inputs):
    x = np.asarray(inputs["L0_attn_residual"], dtype=np.float32)
    Wk = np.asarray(inputs["L0_attn_W_key"], dtype=np.float32)
    Wq = np.asarray(inputs["L0_attn_W_query"], dtype=np.float32)
    Wv = np.asarray(inputs["L0_attn_W_values"], dtype=np.float32)
    Wo = np.asarray(inputs["L0_attn_W_output"], dtype=np.float32)
    Bk = np.asarray(inputs["L0_attn_Bias_key"], dtype=np.float32)
    Bq = np.asarray(inputs["L0_attn_Bias_query"], dtype=np.float32)
    Bv = np.asarray(inputs["L0_attn_Bias_values"], dtype=np.float32)
    Bo = np.asarray(inputs["L0_attn_Bias_output"], dtype=np.float32)
    W1 = np.asarray(inputs["L0_ffn_W1"], dtype=np.float32)
    b1 = np.asarray(inputs["L0_ffn_b1"], dtype=np.float32)
    W2 = np.asarray(inputs["L0_ffn_W2"], dtype=np.float32)
    b2 = np.asarray(inputs["L0_ffn_b2"], dtype=np.float32)

    wk_t = np.ascontiguousarray(Wk.transpose(1, 0, 2).reshape(D, NHEAD * H))
    wq_t = np.ascontiguousarray(Wq.transpose(1, 0, 2).reshape(D, NHEAD * H))
    wv_t = np.ascontiguousarray(Wv.transpose(1, 0, 2).reshape(D, NHEAD * H))
    wout = np.ascontiguousarray(Wo.reshape(NHEAD * H, D))
    bk_m = np.ascontiguousarray(Bk.reshape(8, 128).T)
    bq_m = np.ascontiguousarray(Bq.reshape(8, 128).T)
    bv_row = np.ascontiguousarray(Bv.reshape(1, NHEAD * H))
    bo_row = np.ascontiguousarray(Bo.reshape(1, D))
    b1_m = np.ascontiguousarray(b1.reshape(32, 128).T)
    b2_row = np.ascontiguousarray(b2.reshape(1, D))
    identc = np.eye(128, dtype=np.float32)
    maskinv = (np.arange(128)[None, :] < np.arange(128)[:, None]).astype(np.uint8)

    nc = build_nc()
    in_maps = []
    for c in range(N_CORES):
        xs = x[B_CORE * c: B_CORE * (c + 1)]
        in_maps.append({
            "x_tm": np.ascontiguousarray(xs),
            "x_fm": np.ascontiguousarray(xs.transpose(0, 2, 1)),
            "wk_t": wk_t, "wq_t": wq_t, "wv_t": wv_t, "wout": wout,
            "bk_m": bk_m, "bq_m": bq_m, "bv_row": bv_row, "bo_row": bo_row,
            "w1": W1, "b1_m": b1_m, "w2": W2, "b2_row": b2_row,
            "identc": identc, "maskinv": maskinv,
        })
    res = run_bass_kernel_spmd(nc, in_maps, core_ids=list(range(N_CORES)))
    return np.concatenate([r["out"] for r in res.results], axis=0)


# version 07
# speedup vs baseline: 1.3837x; 1.0894x over previous
import numpy as np
import concourse.bass as bass
import concourse.mybir as mybir
from concourse.bass_utils import run_bass_kernel_spmd
from contextlib import ExitStack

F32 = mybir.dt.float32
BF16 = mybir.dt.bfloat16
U8 = mybir.dt.uint8
AF = mybir.ActivationFunctionType
ALU = mybir.AluOpType
AX = mybir.AxisListType.X

B_CORE = 4
S = 2048
D = 1024
NHEAD = 16
H = 64
DFF = 4096
MAGIC = 12582912.0
N_CORES = 8

ENGS = ("sync", "tensor", "vector", "scalar", "gpsimd")

DBG_PHASES = "QLFAW"
DBG_BATCHES = B_CORE
DBG_TAP = False


class Prog:
    def __init__(self, nc):
        self.nc = nc
        self.ops = {e: [] for e in ENGS}
        self.semh = {}
        self.semc = {}
        self.last_wait = {}

    def sem(self, name, handle):
        self.semh[name] = handle
        self.semc[name] = 0

    def wait(self, eng, sem, val):
        if val is None or val <= 0:
            return
        key = (eng, sem)
        if self.last_wait.get(key, -1) >= val:
            return
        self.last_wait[key] = val
        h = self.semh[sem]
        self.ops[eng].append(lambda E, h=h, v=val: E.wait_ge(h, v))

    OWN = {"tensor": "pe", "vector": "ve", "scalar": "ac"}

    def fence(self, eng):
        own = self.OWN[eng]
        self.wait(eng, own, self.semc[own])

    def run(self, eng, fn, inc=None, n=1):
        if inc is None and eng in ("vector", "scalar"):
            inc = self.OWN[eng]
        tok = None
        if inc is not None:
            self.semc[inc] += n
            tok = self.semc[inc]
            h = self.semh[inc]

            def cl(E, fn=fn, h=h, n=n):
                fn(E).then_inc(h, n)
        else:
            def cl(E, fn=fn):
                fn(E)
        self.ops[eng].append(cl)
        return tok

    def load(self, out_ap, in_ap):
        return self.run("sync", lambda E, o=out_ap, i=in_ap: E.dma_start(out=o, in_=i),
                        inc="dl", n=16)

    def store(self, out_ap, in_ap):
        return self.run("gpsimd", lambda E, o=out_ap, i=in_ap: E.dma_start(out=o, in_=i),
                        inc="ds", n=16)

    def mm(self, out, lhsT, rhs, start, stop, inc=None):
        return self.run("tensor",
                        lambda E, o=out, l=lhsT, r=rhs, s=start, p=stop:
                        E.matmul(o, l, r, start=s, stop=p), inc=inc)

    def barrier_psum(self, *engs):
        for e in engs:
            for sem in ("pe", "ve", "ac"):
                if sem == self.OWN.get(e):
                    continue
                self.wait(e, sem, self.semc[sem])


def build_nc():
    nc = bass.Bass("TRN2", target_bir_lowering=False)

    x_tm = nc.dram_tensor("x_tm", [B_CORE, S, D], F32, kind="ExternalInput").ap()
    x_fm = nc.dram_tensor("x_fm", [B_CORE, D, S], F32, kind="ExternalInput").ap()
    wk_t = nc.dram_tensor("wk_t", [D, NHEAD * H], F32, kind="ExternalInput").ap()
    wq_t = nc.dram_tensor("wq_t", [D, NHEAD * H], F32, kind="ExternalInput").ap()
    wv_t = nc.dram_tensor("wv_t", [D, NHEAD * H], F32, kind="ExternalInput").ap()
    wout = nc.dram_tensor("wout", [NHEAD * H, D], F32, kind="ExternalInput").ap()
    bk_m = nc.dram_tensor("bk_m", [128, 8], F32, kind="ExternalInput").ap()
    bq_m = nc.dram_tensor("bq_m", [128, 8], F32, kind="ExternalInput").ap()
    bv_row = nc.dram_tensor("bv_row", [1, NHEAD * H], F32, kind="ExternalInput").ap()
    bo_row = nc.dram_tensor("bo_row", [1, D], F32, kind="ExternalInput").ap()
    w1 = nc.dram_tensor("w1", [DFF, D], F32, kind="ExternalInput").ap()
    b1_m = nc.dram_tensor("b1_m", [128, 32], F32, kind="ExternalInput").ap()
    w2 = nc.dram_tensor("w2", [D, DFF], F32, kind="ExternalInput").ap()
    b2_row = nc.dram_tensor("b2_row", [1, D], F32, kind="ExternalInput").ap()
    identc = nc.dram_tensor("identc", [128, 128], F32, kind="ExternalInput").ap()
    maskinv = nc.dram_tensor("maskinv", [128, 128], U8, kind="ExternalInput").ap()
    out = nc.dram_tensor("out", [B_CORE, S, D], F32, kind="ExternalOutput").ap()

    kw = {"kind": "ExternalOutput"} if DBG_TAP else {}
    wq1t = nc.dram_tensor("wq1t", [D, DFF], BF16, **kw).ap()
    wq2t = nc.dram_tensor("wq2t", [DFF, D], BF16, **kw).ap()
    g_scr = nc.dram_tensor("g_scr", [B_CORE, S, D], F32, **kw).ap()
    wgh_scr = nc.dram_tensor("wgh_scr", [NHEAD * H, S], BF16, **kw).ap()
    wgl_scr = nc.dram_tensor("wgl_scr", [NHEAD * H, S], BF16, **kw).ap()
    awh = nc.dram_tensor("awh", [3, D, NHEAD * H], BF16).ap()
    awl = nc.dram_tensor("awl", [3, D, NHEAD * H], BF16).ap()
    woh = nc.dram_tensor("woh", [NHEAD * H, D], BF16).ap()
    wol = nc.dram_tensor("wol", [NHEAD * H, D], BF16).ap()

    with ExitStack() as ctx:
        E_ = ctx.enter_context
        xh = E_(nc.sbuf_tensor("xh", [128, 16384], BF16)).ap()
        xl = E_(nc.sbuf_tensor("xl", [128, 16384], BF16)).ap()
        r2 = E_(nc.sbuf_tensor("r2", [128, 8192], F32)).ap()
        kqK = E_(nc.sbuf_tensor("kqK", [128, 4096], F32)).ap()
        kqQ = E_(nc.sbuf_tensor("kqQ", [128, 4096], F32)).ap()
        vsl = E_(nc.sbuf_tensor("vsl", [128, 16 * 4 * 65], F32)).ap()
        srow1 = E_(nc.sbuf_tensor("srow1", [1, 16 * 65], F32)).ap()
        crow1 = E_(nc.sbuf_tensor("crow1", [1, 16 * 65], F32)).ap()
        rows = E_(nc.sbuf_tensor("rows", [1, 1536], F32)).ap()
        wkb = E_(nc.sbuf_tensor("wkb", [128, 2048], BF16)).ap()
        wqb = E_(nc.sbuf_tensor("wqb", [128, 2048], BF16)).ap()
        wvb = E_(nc.sbuf_tensor("wvb", [128, 4096], BF16)).ap()
        wq1b = E_(nc.sbuf_tensor("wq1b", [128, 4096], BF16)).ap()
        wq2b = E_(nc.sbuf_tensor("wq2b", [128, 3072], BF16)).ap()
        wgtb = E_(nc.sbuf_tensor("wgtb", [128, 2048], BF16)).ap()
        ident = E_(nc.sbuf_tensor("ident", [128, 128], F32)).ap()
        mski = E_(nc.sbuf_tensor("mski", [128, 128], U8)).ap()
        onest = E_(nc.sbuf_tensor("onest", [128, 128], F32)).ap()
        onescol = E_(nc.sbuf_tensor("onescol", [128, 1], F32)).ap()
        onesrow = E_(nc.sbuf_tensor("onesrow", [1, 512], F32)).ap()
        b1s = E_(nc.sbuf_tensor("b1s", [128, 32], F32)).ap()
        bks = E_(nc.sbuf_tensor("bks", [128, 8], F32)).ap()
        bqs = E_(nc.sbuf_tensor("bqs", [128, 8], F32)).ap()
        bvs = E_(nc.sbuf_tensor("bvs", [1, 1024], F32)).ap()
        bos = E_(nc.sbuf_tensor("bos", [1, 1024], F32)).ap()
        b2s = E_(nc.sbuf_tensor("b2s", [1, 1024], F32)).ap()
        colt = E_(nc.sbuf_tensor("colt", [128, 8], F32)).ap()

        pA = E_(nc.psum_tensor("pA", [128, 512], F32)).ap()
        pB = E_(nc.psum_tensor("pB", [128, 512], F32)).ap()
        pC = E_(nc.psum_tensor("pC", [128, 1024], F32)).ap()
        pD = E_(nc.psum_tensor("pD", [128, 1024], F32)).ap()
        pE = E_(nc.psum_tensor("pE", [128, 512], F32)).ap()
        pF = E_(nc.psum_tensor("pF", [128, 512], F32)).ap()

        sem_handles = {}
        for name in ("dl", "ds", "pe", "ve", "ac"):
            sem_handles[name] = E_(nc.semaphore(name))

        block = E_(nc.Block())
        p = Prog(nc)
        for name, h in sem_handles.items():
            p.sem(name, h)

        build_program(p, locals())

        @block.sync
        def _(sync):
            for f in p.ops["sync"]:
                f(sync)

        @block.tensor
        def _(tensor):
            for f in p.ops["tensor"]:
                f(tensor)

        @block.vector
        def _(vector):
            for f in p.ops["vector"]:
                f(vector)

        @block.scalar
        def _(scalar):
            for f in p.ops["scalar"]:
                f(scalar)

        @block.gpsimd
        def _(gpsimd):
            for f in p.ops["gpsimd"]:
                f(gpsimd)

    return nc


def build_program(p, t):
    xh, xl, r2 = t["xh"], t["xl"], t["r2"]
    kqK, kqQ, vsl = t["kqK"], t["kqQ"], t["vsl"]
    srow1, crow1, rows = t["srow1"], t["crow1"], t["rows"]
    wkb, wqb, wvb = t["wkb"], t["wqb"], t["wvb"]
    wq1b, wq2b, wgtb = t["wq1b"], t["wq2b"], t["wgtb"]
    gsb = t["kqK"][:, 0:1024]
    hfp = t["kqK"][:, 1024:1280]
    stage = wq1b[:, 3072:3328]
    stg2 = wgtb
    uh = wq1b[0:64, 0:1024]
    ul = wq1b[64:128, 0:1024]
    ident, mski, onest = t["ident"], t["mski"], t["onest"]
    onescol, onesrow = t["onescol"], t["onesrow"]
    b1s, bks, bqs, bvs, bos, b2s, colt = (t["b1s"], t["bks"], t["bqs"], t["bvs"],
                                          t["bos"], t["b2s"], t["colt"])
    pA, pB, pC, pD, pE, pF = t["pA"], t["pB"], t["pC"], t["pD"], t["pE"], t["pF"]
    x_tm, x_fm = t["x_tm"], t["x_fm"]
    wk_t, wq_t, wv_t, wout = t["wk_t"], t["wq_t"], t["wv_t"], t["wout"]
    w1, b1_m, w2 = t["w1"], t["b1_m"], t["w2"]
    identc, maskinv, out = t["identc"], t["maskinv"], t["out"]
    wq1t, wq2t, g_scr = t["wq1t"], t["wq2t"], t["g_scr"]
    wgh_scr, wgl_scr = t["wgh_scr"], t["wgl_scr"]
    awh, awl, woh, wol = t["awh"], t["awl"], t["woh"], t["wol"]
    vslv = vsl.rearrange("p (c h e) -> p c h e", c=16, h=4)

    p.load(ident[:, :], identc)
    p.load(mski[:, :], maskinv)
    p.load(b1s[:, :], t["b1_m"])
    p.load(bks[:, :], t["bk_m"])
    p.load(bqs[:, :], t["bq_m"])
    p.load(bvs[:, :], t["bv_row"])
    p.load(bos[:, :], t["bo_row"])
    tok_consts = p.load(b2s[:, :], t["b2_row"])

    p.run("vector", lambda E: E.memset(onest[:, :], 1.0))
    p.run("vector", lambda E: E.memset(onescol[:, :], 1.0))
    p.run("vector", lambda E: E.memset(onesrow[:, :], 1.0))
    p.run("vector", lambda E: E.memset(vslv[:, :, :, 64:65], 1.0))
    p.fence("vector")

    f32buf = r2[:, 0:1024]
    st_aw = None

    def split_stage(src_ap, dst_h, dst_l):
        nonlocal st_aw
        p.wait("sync", "ds", st_aw)
        p.wait("sync", "ve", p.semc["ve"])
        dt_ = p.load(f32buf, src_ap)
        p.wait("vector", "dl", dt_)
        p.wait("vector", "ds", st_aw)
        p.run("vector", lambda E: E.tensor_copy(stg2[:, 0:1024], f32buf))
        p.fence("vector")
        vv = p.run("vector", lambda E: E.tensor_tensor(
            stg2[:, 1024:2048], f32buf, stg2[:, 0:1024], ALU.subtract))
        p.wait("gpsimd", "ve", vv)
        p.store(dst_h, stg2[:, 0:1024])
        st_aw = p.store(dst_l, stg2[:, 1024:2048])

    for mi, wsrc in enumerate((wk_t, wq_t, wv_t)):
        for i in range(8):
            split_stage(wsrc[i * 128:(i + 1) * 128, :],
                        awh[mi, i * 128:(i + 1) * 128, :],
                        awl[mi, i * 128:(i + 1) * 128, :])
    for i in range(8):
        split_stage(wout[i * 128:(i + 1) * 128, :],
                    woh[i * 128:(i + 1) * 128, :],
                    wol[i * 128:(i + 1) * 128, :])

    def quantize(w_dram, wq_dram, n_rows, acc_i, st_prev):
        n_cols = w_dram.shape[1]
        ntile = n_rows // 128
        ncol_t = n_cols // 128
        wbuf = [r2[:, 0:n_cols], r2[:, n_cols:2 * n_cols]]
        acc = colt[:, acc_i:acc_i + 1]
        sc = colt[:, acc_i + 1:acc_i + 2]
        red = colt[:, 6:7]
        p.run("vector", lambda E, a=acc: E.memset(a, 0.0))
        p.fence("vector")
        ve_toks = [None] * ntile
        for i in range(ntile):
            b = wbuf[i % 2]
            if i >= 2:
                p.wait("sync", "ve", ve_toks[i - 2])
            else:
                p.wait("sync", "ve", p.semc["ve"])
                p.wait("sync", "pe", p.semc["pe"])
                p.wait("sync", "ac", p.semc["ac"])
            dt_ = p.load(b, w_dram[i * 128:(i + 1) * 128, :])
            p.wait("scalar", "dl", dt_)
            aca = p.run("scalar", lambda E, b=b: E.activation(b, b, AF.Abs))
            p.wait("vector", "ac", aca)
            p.run("vector", lambda E, b=b, r=red: E.tensor_reduce(
                r, b, AX, ALU.add))
            p.fence("vector")
            ve_toks[i] = p.run("vector", lambda E, a=acc, r=red: E.tensor_tensor(
                a, a, r, ALU.add))
            p.fence("vector")
        p.wait("tensor", "ve", ve_toks[-1])
        p.wait("tensor", "dl", tok_consts)
        pe1 = p.mm(pF[0:1, 0:1], acc, onescol[:, :], True, True, inc="pe")
        p.wait("vector", "pe", pe1)
        sc1 = rows[0:1, 0:1]
        p.run("vector", lambda E, s=sc1: E.tensor_scalar(
            s, pF[0:1, 0:1], 1.0 / (n_rows * n_cols), 1e-5, ALU.mult, ALU.add))
        p.fence("vector")
        ve1 = p.run("vector", lambda E, s=sc1: E.reciprocal(s, s))
        p.wait("tensor", "ve", ve1)
        pe2 = p.mm(pF[0:128, 0:1], onesrow[0:1, 0:128], sc1, True, True, inc="pe")
        p.wait("vector", "pe", pe2)
        ve2 = p.run("vector", lambda E, s=sc: E.tensor_copy(s, pF[0:128, 0:1]))
        pe_cons = [None] * ntile
        st_toks = [st_prev, st_prev]
        last_store = st_prev
        vet_prev = ve2
        for i in range(ntile):
            b = wbuf[i % 2]
            if i == 0:
                p.wait("sync", "ve", ve2)
            else:
                p.wait("sync", "pe", pe_cons[i - 1])
            dt_ = p.load(b, w_dram[i * 128:(i + 1) * 128, :])
            p.wait("vector", "dl", dt_)
            p.run("vector", lambda E, b=b, s=sc: E.tensor_scalar(
                b, b, s, MAGIC, ALU.mult, ALU.add))
            p.fence("vector")
            p.run("vector", lambda E, b=b: E.tensor_scalar(
                b, b, MAGIC, -1.0, ALU.subtract, ALU.max))
            p.fence("vector")
            veq = p.run("vector", lambda E, b=b: E.tensor_scalar(
                b, b, 1.0, None, ALU.min))
            p.wait("tensor", "ve", veq)
            for j in range(ncol_t):
                sg = stage[:, (j % 2) * 128:(j % 2) * 128 + 128]
                p.wait("tensor", "ve", vet_prev)
                pet = p.run("tensor", lambda E, src=b[:, j * 128:(j + 1) * 128]:
                            E.transpose(pE[0:128, 0:128], src, ident[:, :]), inc="pe")
                p.wait("vector", "ds", st_toks[j % 2])
                p.wait("vector", "pe", pet)
                vet = p.run("vector", lambda E, sg=sg: E.tensor_copy(
                    sg, pE[0:128, 0:128]))
                vet_prev = vet
                p.wait("gpsimd", "ve", vet)
                stt = p.store(wq_dram[j * 128:(j + 1) * 128, i * 128:(i + 1) * 128], sg)
                st_toks[j % 2] = stt
                last_store = stt
            pe_cons[i] = p.semc["pe"]
        return last_store

    st_total = st_aw
    if "Q" in DBG_PHASES:
        st_total = quantize(w1, wq1t, DFF, 0, st_total)
        st_total = quantize(w2, wq2t, D, 2, st_total)
    q_done = st_total
    ve_xn = 0

    for b in range(DBG_BATCHES):
        xf = r2[:, 0:4096]
        sq = [r2[:, 4096:4608], r2[:, 4608:5120]]
        mb = r2[:, 5120:5632]
        rb = r2[:, 5632:6144]
        xfv = xf.rearrange("p (dc t) -> p dc t", dc=8)
        x_fmv = x_fm[b].rearrange("(dc p) t -> p dc t", p=128)
        m_row = rows[0:1, 0:512]
        var_row = rows[0:1, 512:1024]
        rstd_row = rows[0:1, 1024:1536]
        for tt in range(4 if "L" in DBG_PHASES else 0):
            p.wait("sync", "pe", p.semc["pe"])
            p.wait("sync", "ve", p.semc["ve"])
            p.wait("sync", "ac", p.semc["ac"])
            dt_ = p.load(xfv, x_fmv[:, :, tt * 512:(tt + 1) * 512])
            p.barrier_psum("tensor")
            p.wait("tensor", "dl", dt_)
            p.wait("scalar", "dl", dt_)
            for dc in range(8):
                p.mm(pA[0:1, 0:512], onescol[:, :], xfv[:, dc, :], dc == 0, dc == 7,
                     inc="pe" if dc == 7 else None)
            sq_mm = [None, None]
            for dc in range(8):
                p.wait("scalar", "pe", sq_mm[dc % 2])
                act = p.run("scalar", lambda E, dc=dc, s=sq[dc % 2]: E.activation(
                    s, xfv[:, dc, :], AF.Square))
                p.wait("tensor", "ac", act)
                sq_mm[dc % 2] = p.mm(pB[0:1, 0:512], onescol[:, :], sq[dc % 2],
                                     dc == 0, dc == 7, inc="pe")
            pe_stats = p.semc["pe"]
            p.wait("vector", "pe", pe_stats)
            p.run("vector", lambda E: E.tensor_scalar(
                m_row, pA[0:1, 0:512], 1.0 / D, None, ALU.mult))
            p.fence("vector")
            p.run("vector", lambda E: E.tensor_tensor(
                var_row, m_row, m_row, ALU.mult))
            p.fence("vector")
            p.run("vector", lambda E: E.scalar_tensor_tensor(
                var_row, pB[0:1, 0:512], 1.0 / D, var_row, ALU.mult, ALU.subtract))
            p.fence("vector")
            ve_v = p.run("vector", lambda E: E.tensor_scalar(
                var_row, var_row, 1e-5, None, ALU.add))
            p.wait("scalar", "ve", ve_v)
            p.run("scalar", lambda E: E.activation(rstd_row, var_row, AF.Ln))
            p.fence("scalar")
            ac_r = p.run("scalar", lambda E: E.activation(
                rstd_row, rstd_row, AF.Exp, scale=-0.5))
            p.wait("tensor", "ve", ve_v)
            p.wait("tensor", "ac", ac_r)
            p.mm(pC[:, 0:512], onesrow[0:1, 0:128], m_row, True, True)
            pe_bc = p.mm(pC[:, 512:1024], onesrow[0:1, 0:128], rstd_row, True, True,
                         inc="pe")
            p.wait("vector", "pe", pe_bc)
            p.run("vector", lambda E: E.tensor_copy(mb, pC[:, 0:512]))
            p.run("vector", lambda E: E.tensor_copy(rb, pC[:, 512:1024]))
            p.wait("vector", "pe", p.semc["pe"])
            p.wait("vector", "ds", st_total if b > 0 else 0)
            p.fence("vector")
            for dc in range(8):
                p.run("vector", lambda E, i=xfv[:, dc, :]: E.scalar_tensor_tensor(
                    i, i, 1.0, mb, ALU.mult, ALU.subtract))
            p.fence("vector")
            for dc in range(8):
                p.run("vector", lambda E, i=xfv[:, dc, :]: E.tensor_tensor(
                    i, i, rb, ALU.mult))
            p.fence("vector")
            for dc in range(8):
                o = xh[:, dc * 2048 + tt * 512: dc * 2048 + (tt + 1) * 512]
                p.run("vector", lambda E, o=o, i=xfv[:, dc, :]: E.tensor_copy(o, i))
            p.fence("vector")
            for dc in range(8):
                oh = xh[:, dc * 2048 + tt * 512: dc * 2048 + (tt + 1) * 512]
                ol = xl[:, dc * 2048 + tt * 512: dc * 2048 + (tt + 1) * 512]
                ve_xn = p.run("vector", lambda E, oh=oh, ol=ol, i=xfv[:, dc, :]:
                              E.tensor_tensor(ol, i, oh, ALU.subtract))

        r2bf = r2.bitcast(BF16)
        hh = r2bf[:, 0:8192]
        hl = r2bf[:, 8192:16384]
        wq1v = wq1t.rearrange("(dc p) f -> p dc f", p=128)
        wq1bv = wq1b.rearrange("p (u dc f) -> p u dc f", u=4, dc=8)
        wq2bv = wq2b.rearrange("p (u f) -> p u f", u=3)
        if "F" in DBG_PHASES:
            p.wait("sync", "ds", q_done)
        d_toks = {}
        pe_wq = {}
        ve_h_prev = 0
        hdone = {}
        rl = {}

        def emit_ffn2(fc, u, gi):
            ac_h, ve_h = hdone[fc]
            p.wait("tensor", "ac", ac_h)
            p.wait("tensor", "ve", ve_h)
            for sub in range(2):
                gp = pC if sub == 0 else pD
                for half in range(2):
                    for hs in (hh, hl):
                        p.mm(gp[:, half * 512:(half + 1) * 512],
                             hs[:, fc * 256 + sub * 128:
                                fc * 256 + sub * 128 + 128],
                             wq2bv[:, u, half * 512:(half + 1) * 512],
                             fc == 0 and hs is hh, False,
                             inc="pe" if (sub == 1 and half == 1 and hs is hl)
                             else None)
            pe_wq[gi] = p.semc["pe"]

        for ts in range(8 if "F" in DBG_PHASES else 0):
            t0 = ts * 256
            for fc in range(32):
                gi = ts * 32 + fc
                u = gi % 3
                if gi == 0:
                    p.wait("sync", "pe", p.semc["pe"])
                    p.wait("sync", "ac", p.semc["ac"])
                    p.wait("sync", "ve", p.semc["ve"])
                    p.wait("sync", "ds", st_total)
                if gi >= 3:
                    p.wait("sync", "pe", pe_wq[gi - 3])
                p.load(wq1bv[:, u, :, :], wq1v[:, :, fc * 128:(fc + 1) * 128])
                d_toks[gi] = p.load(wq2bv[:, u, :], wq2t[fc * 128:(fc + 1) * 128, :])
                if gi == 0:
                    p.wait("tensor", "ve", ve_xn)
                    p.barrier_psum("tensor")
                p.wait("tensor", "dl", d_toks[gi])
                hp = pA if fc % 2 == 0 else pB
                p.wait("tensor", "ac", rl.get(gi - 2, 0))
                for dc in range(8):
                    for cm, xs in ((0, xh), (1, xl)):
                        p.mm(hp[:, 0:256], wq1bv[:, u, dc, :],
                             xs[:, dc * 2048 + t0: dc * 2048 + t0 + 256],
                             dc == 0 and cm == 0, dc == 7 and cm == 1,
                             inc="pe" if (dc == 7 and cm == 1) else None)
                pe_h = p.semc["pe"]
                p.wait("scalar", "pe", pe_h)
                p.wait("scalar", "ve", ve_h_prev)
                rl[gi] = p.run("scalar", lambda E, i=hp[:, 0:256], fc=fc:
                               E.activation(hfp[:, :], i, AF.Relu,
                                            bias=b1s[:, fc:fc + 1]))
                p.fence("scalar")
                ac_h = p.run("scalar", lambda E, o=hh[:, fc * 256:(fc + 1) * 256]:
                             E.activation(o, hfp[:, :], AF.Copy))
                p.wait("vector", "ac", ac_h)
                ve_h = p.run("vector", lambda E, o=hl[:, fc * 256:(fc + 1) * 256],
                             h=hh[:, fc * 256:(fc + 1) * 256]:
                             E.tensor_tensor(o, hfp[:, :], h, ALU.subtract))
                ve_h_prev = ve_h
                hdone[fc] = (ac_h, ve_h)
                if fc > 0:
                    emit_ffn2(fc - 1, (gi - 1) % 3, gi - 1)
            emit_ffn2(31, (ts * 32 + 31) % 3, ts * 32 + 31)
            for sub in range(2):
                gp = pC if sub == 0 else pD
                for half in range(2):
                    p.mm(gp[:, half * 512:(half + 1) * 512], onesrow[0:1, 0:128],
                         b2s[0:1, half * 512:(half + 1) * 512], False, True,
                         inc="pe" if (sub == 1 and half == 1) else None)
            pe_g = p.semc["pe"]
            pe_wq[ts * 32 + 31] = pe_g
            p.wait("scalar", "pe", pe_g)
            for sub in range(2):
                gp = pC if sub == 0 else pD
                p.wait("scalar", "ds", st_total)
                ac_g = p.run("scalar", lambda E, gp=gp: E.activation(
                    gsb[:, :], gp[:, :], AF.Silu))
                p.wait("gpsimd", "ac", ac_g)
                st_total = p.store(g_scr[b, t0 + sub * 128: t0 + sub * 128 + 128, :],
                                   gsb[:, :])

        esl = r2
        bank_ac = {0: p.semc["ac"], 1: p.semc["ac"]}
        ustg_ds = [st_total, st_total]
        nbufs = wq2b.bitcast(F32)
        nbuf = [nbufs[0:65, 0:512], nbufs[0:65, 512:1024]]
        zrow = [rows[0:1, 0:512], rows[0:1, 512:1024]]
        pend = None
        ui = 0
        prev_fix = p.semc["ve"]
        zx_par = {0: 0, 1: 0}
        zb_par = {0: 0, 1: 0}
        veW_prev = 0

        def flush_norm():
            nonlocal pend, st_total, veW_prev
            if pend is None:
                return
            par0, n0, qt0 = pend["par"], pend["n"], pend["qt"]
            p.wait("tensor", "ac", pend["zx"])
            p.wait("tensor", "ve", veW_prev)
            zb = p.mm(pA[0:64, :], onesrow[0:1, 0:64], zrow[par0], True, True,
                      inc="pe")
            zb_par[par0] = zb
            nb = nbuf[par0]
            p.wait("vector", "pe", zb)
            p.wait("vector", "ds", ustg_ds[par0])
            p.fence("vector")
            uhs = uh[:, par0 * 512:(par0 + 1) * 512]
            uls = ul[:, par0 * 512:(par0 + 1) * 512]
            p.run("vector", lambda E, nb=nb: E.tensor_tensor(
                nb[0:64, :], nb[0:64, :], pA[0:64, :], ALU.mult))
            p.fence("vector")
            p.run("vector", lambda E, o=uhs, nb=nb: E.tensor_copy(o, nb[0:64, :]))
            p.fence("vector")
            veW = p.run("vector", lambda E, o=uls, nb=nb, h=uhs: E.tensor_tensor(
                o, nb[0:64, :], h, ALU.subtract))
            veW_prev = veW
            p.wait("gpsimd", "ve", veW)
            p.store(wgh_scr[64 * n0: 64 * n0 + 64, qt0 * 512:(qt0 + 1) * 512], uhs)
            st_total = p.store(
                wgl_scr[64 * n0: 64 * n0 + 64, qt0 * 512:(qt0 + 1) * 512], uls)
            ustg_ds[par0] = st_total
            pend = None
        awhv = awh.rearrange("m (dc p) n -> m p dc n", p=128)
        awlv = awl.rearrange("m (dc p) n -> m p dc n", p=128)
        wkbv = wkb.rearrange("p (u dc n) -> p u dc n", u=2, dc=8)
        wqbv = wqb.rearrange("p (u dc n) -> p u dc n", u=2, dc=8)
        wvbv = wvb.rearrange("p (u dc n) -> p u dc n", u=2, dc=8)
        for qd in range(4 if "A" in DBG_PHASES else 0):
            p.load(wvbv[:, 0, :, :], awhv[2, :, :, qd * 256:(qd + 1) * 256])
            dv = p.load(wvbv[:, 1, :, :], awlv[2, :, :, qd * 256:(qd + 1) * 256])
            p.wait("tensor", "dl", dv)
            p.wait("tensor", "ve", p.semc["ve"])
            p.barrier_psum("tensor")
            for kc in range(16):
                for dc in range(8):
                    for xs, wu in ((xh, 0), (xh, 1), (xl, 0)):
                        p.mm(pF[:, 0:256],
                             xs[:, dc * 2048 + kc * 128: dc * 2048 + (kc + 1) * 128],
                             wvbv[:, wu, dc, :],
                             dc == 0 and xs is xh and wu == 0, False)
                peV = p.mm(pF[:, 0:256], onesrow[0:1, 0:128],
                           bvs[0:1, qd * 256:(qd + 1) * 256], False, True, inc="pe")
                p.wait("vector", "pe", peV)
                veV = p.run("vector", lambda E, kc=kc: E.tensor_copy(
                    vslv[:, kc, :, 0:64],
                    pF[:, 0:256].rearrange("p (h e) -> p h e", h=4)))
                p.wait("tensor", "ve", veV)
            kqKb = kqK.bitcast(BF16)
            kqQb = kqQ.bitcast(BF16)
            ktmp = wgtb.bitcast(F32)[:, 0:512]
            qtmp = wgtb.bitcast(F32)[:, 512:1024]
            klt = wq1b[:, 1024:1536]
            for pp in range(2):
                gp_ = 2 * qd + pp
                p.wait("sync", "pe", p.semc["pe"])
                p.load(wkbv[:, 0, :, :], awhv[0, :, :, gp_ * 128:(gp_ + 1) * 128])
                p.load(wkbv[:, 1, :, :], awlv[0, :, :, gp_ * 128:(gp_ + 1) * 128])
                p.load(wqbv[:, 0, :, :], awhv[1, :, :, gp_ * 128:(gp_ + 1) * 128])
                dq = p.load(wqbv[:, 1, :, :], awlv[1, :, :, gp_ * 128:(gp_ + 1) * 128])
                p.wait("tensor", "dl", dq)
                p.wait("tensor", "ve", p.semc["ve"])
                p.wait("tensor", "ds", p.semc["ds"])
                p.barrier_psum("tensor")
                pr_ds = 0
                for tt in range(4):
                    ttsl = slice(tt * 512, (tt + 1) * 512)
                    for ps, wb in ((pA, wkbv), (pB, wqbv)):
                        for dc in range(8):
                            for wu, xs in ((0, xh), (0, xl), (1, xh)):
                                p.mm(ps[:, :], wb[:, wu, dc, :],
                                     xs[:, dc * 2048 + tt * 512:
                                        dc * 2048 + (tt + 1) * 512],
                                     dc == 0 and wu == 0 and xs is xh,
                                     dc == 7 and wu == 1,
                                     inc="pe" if (dc == 7 and wu == 1) else None)
                    peQ = p.semc["pe"]
                    p.wait("vector", "pe", peQ)
                    p.wait("vector", "ds", pr_ds)
                    p.run("vector", lambda E, g=gp_: E.tensor_scalar(
                        ktmp, pA[:, :], bks[:, g:g + 1], None, ALU.add))
                    p.run("vector", lambda E, g=gp_: E.tensor_scalar(
                        qtmp, pB[:, :], bqs[:, g:g + 1], None, ALU.add))
                    p.fence("vector")
                    for hs in range(2):
                        R = slice(64 * hs, 64 * hs + 64)
                        kP_h = kqKb[:, hs * 2048:hs * 2048 + 2048]
                        qhh_h = kqQb[:, hs * 2048:hs * 2048 + 2048]
                        qll_h = kqQb[:, 4096 + hs * 2048:4096 + hs * 2048 + 2048]
                        p.run("vector", lambda E, o=kP_h[R, ttsl], i=ktmp[R, :]:
                              E.tensor_copy(o, i))
                        p.run("vector", lambda E, o=qhh_h[R, ttsl], i=qtmp[R, :]:
                              E.tensor_copy(o, i))
                    p.fence("vector")
                    for hs in range(2):
                        R = slice(64 * hs, 64 * hs + 64)
                        kP_h = kqKb[:, hs * 2048:hs * 2048 + 2048]
                        qhh_h = kqQb[:, hs * 2048:hs * 2048 + 2048]
                        qll_h = kqQb[:, 4096 + hs * 2048:4096 + hs * 2048 + 2048]
                        p.run("vector", lambda E, o=klt[R, :], i=ktmp[R, :],
                              h=kP_h[R, ttsl]: E.tensor_tensor(o, i, h, ALU.subtract))
                        p.run("vector", lambda E, o=qll_h[R, ttsl], i=qtmp[R, :],
                              h=qhh_h[R, ttsl]: E.tensor_tensor(o, i, h, ALU.subtract))
                    ve_ev = p.run("vector", lambda E: E.engine_nop())
                    p.wait("gpsimd", "ve", ve_ev)
                    for hs in range(2):
                        R = slice(64 * hs, 64 * hs + 64)
                        Rp = slice(64 * (1 - hs), 64 * (1 - hs) + 64)
                        kP_h = kqKb[:, hs * 2048:hs * 2048 + 2048]
                        qhh_h = kqQb[:, hs * 2048:hs * 2048 + 2048]
                        qll_h = kqQb[:, 4096 + hs * 2048:4096 + hs * 2048 + 2048]
                        p.store(kP_h[Rp, ttsl], klt[R, :])
                        p.store(qhh_h[Rp, ttsl], qhh_h[R, ttsl])
                        pr_ds = p.store(qll_h[Rp, ttsl], qll_h[R, ttsl])
                    p.wait("tensor", "ve", ve_ev)
                pair_ds = pr_ds
                for hs in range(2):
                    j = 2 * pp + hs
                    n = 4 * qd + j
                    kP = kqKb[:, hs * 2048:hs * 2048 + 2048]
                    qhh = kqQb[:, hs * 2048:hs * 2048 + 2048]
                    qll = kqQb[:, 4096 + hs * 2048:4096 + hs * 2048 + 2048]
                    p.wait("tensor", "ds", pair_ds)
                    p.wait("tensor", "ac", bank_ac[0])
                    p.wait("tensor", "ve", p.semc["ve"])
                    for cb in range(4):
                        for u in range(4):
                            kc = 4 * cb + u
                            p.mm(pC[0:1, u * 65: u * 65 + 65], onescol[:, :],
                                 vslv[:, kc, j, :], True, True,
                                 inc="pe" if u == 3 else None)
                        peC = p.semc["pe"]
                        p.wait("vector", "pe", peC)
                        vcp = p.run("vector", lambda E, cb=cb: E.tensor_copy(
                            crow1[0:1, cb * 260:(cb + 1) * 260], pC[0:1, 0:260]))
                        p.wait("tensor", "ve", vcp)
                    def scp(dst_kc, src_kc):
                        p.fence("vector")
                        p.run("vector", lambda E, d=dst_kc, s=src_kc: E.tensor_copy(
                            srow1[0:1, d * 65:(d + 1) * 65],
                            srow1[0:1, s * 65:(s + 1) * 65]))

                    def sacc(dst_kc, src_kcs, copy_first):
                        first = copy_first
                        for kc in src_kcs:
                            p.fence("vector")
                            src = crow1[0:1, kc * 65:(kc + 1) * 65]
                            dsl = srow1[0:1, dst_kc * 65:(dst_kc + 1) * 65]
                            if first:
                                p.run("vector", lambda E, d=dsl, s=src: E.tensor_copy(d, s))
                                first = False
                            else:
                                p.run("vector", lambda E, d=dsl, s=src: E.tensor_tensor(
                                    d, d, s, ALU.add))
                    sacc(11, [12, 13, 14, 15], copy_first=True)
                    scp(7, 11)
                    sacc(7, [8, 9, 10, 11], copy_first=False)
                    scp(3, 7)
                    sacc(3, [4, 5, 6, 7], copy_first=False)
                    p.fence("vector")
                    ve_sfx = p.semc["ve"]
                    p.wait("tensor", "ve", ve_sfx)
                    for qt in range(4):
                        nmat = 4 * qt + 4
                        has_tail = nmat < 16
                        par = ui % 2
                        fix_tok = {}
                        exp_tok = {}

                        def emit_pv(kc, last):
                            p.wait("tensor", "ac", exp_tok[kc // 2])
                            p.wait("tensor", "ve", fix_tok.get(kc, 0))
                            p.mm(pE[0:65, :], vslv[:, kc, j, :],
                                 esl[:, kc * 512:(kc + 1) * 512], kc == 0, last,
                                 inc="pe" if last else None)

                        for g in range(nmat // 2):
                            sp, bank = (pC, 0) if g % 2 == 0 else (pD, 1)
                            p.wait("tensor", "ac", bank_ac[bank])
                            if g == 0 and pend is not None:
                                p.wait("tensor", "ve", pend["ncp"])
                            for u in range(2):
                                kc = 2 * g + u
                                lhs = kP[:, kc * 128:(kc + 1) * 128]
                                p.mm(sp[:, u * 512:(u + 1) * 512], lhs,
                                     qhh[:, qt * 512:(qt + 1) * 512], True, False)
                                p.mm(sp[:, u * 512:(u + 1) * 512], lhs,
                                     qll[:, qt * 512:(qt + 1) * 512], False, True,
                                     inc="pe" if u == 1 else None)
                            p.wait("scalar", "pe", p.semc["pe"])
                            p.wait("scalar", "ve", prev_fix)
                            exp_tok[g] = p.run(
                                "scalar", lambda E, g=g, sp=sp: E.activation(
                                    esl[:, 2 * g * 512: 2 * g * 512 + 1024],
                                    sp[:, :], AF.Exp, scale=0.125))
                            bank_ac[bank] = exp_tok[g]
                            p.wait("vector", "ac", exp_tok[g])
                            for u in range(2):
                                kc = 2 * g + u
                                j2 = kc - 4 * qt
                                if 0 <= j2 <= 3:
                                    tk = None
                                    for j3 in range(j2):
                                        tk = p.run("vector", lambda E, kc=kc, j3=j3:
                                                   E.memset(
                                                       esl[:, kc * 512 + j3 * 128:
                                                           kc * 512 + (j3 + 1) * 128],
                                                       1.0))
                                    tk = p.run("vector", lambda E, kc=kc, j2=j2:
                                               E.copy_predicated(
                                                   esl[:, kc * 512 + j2 * 128:
                                                       kc * 512 + (j2 + 1) * 128],
                                                   mski[:, :], onest[:, :]))
                                    fix_tok[kc] = tk
                            if g >= 1:
                                emit_pv(2 * (g - 1), False)
                                emit_pv(2 * (g - 1) + 1, False)
                        emit_pv(nmat - 2, False)
                        emit_pv(nmat - 1, not has_tail)
                        if has_tail:
                            p.wait("tensor", "ve", ve_sfx)
                            sfx = srow1[0:1, (4 * qt + 3) * 65:(4 * qt + 3) * 65 + 65]
                            p.mm(pE[0:65, :], sfx, onesrow[0:1, 0:512],
                                 False, True, inc="pe")
                        peU = p.semc["pe"]
                        flush_norm()
                        p.wait("vector", "pe", peU)
                        p.wait("vector", "ac", zx_par[par])
                        ncp = p.run("vector", lambda E, nb=nbuf[par]: E.tensor_copy(
                            nb, pE[0:65, :]))
                        p.wait("scalar", "ve", ncp)
                        p.wait("scalar", "pe", zb_par[par])
                        p.run("scalar", lambda E, z=zrow[par], nb=nbuf[par]:
                              E.activation(z, nb[64:65, :], AF.Ln))
                        p.fence("scalar")
                        zx = p.run("scalar", lambda E, z=zrow[par]: E.activation(
                            z, z, AF.Exp, scale=-1.0))
                        zx_par[par] = zx
                        pend = {"par": par, "ncp": ncp, "zx": zx, "n": n, "qt": qt}
                        prev_fix = max(fix_tok.values()) if fix_tok else prev_fix
                        ui += 1
        flush_norm()

        xlf = xl.bitcast(F32)
        wob = xh
        xbuf = xlf[:, 0:1024]
        gbuf = xlf[:, 1024:2048]
        t1 = xlf[:, 2048:3072]
        obuf = xlf[:, 3072:4096]
        wobv = wob.rearrange("p (u c d) -> p u c d", u=2, c=8)
        wgtbv = wgtb.rearrange("p (u c k) -> p u c k", u=2, c=8)
        wohv = woh.rearrange("(c p) d -> p c d", p=128)
        wolv = wol.rearrange("(c p) d -> p c d", p=128)
        wghv = wgh_scr.rearrange("(c p) s -> p c s", p=128)
        wglv = wgl_scr.rearrange("(c p) s -> p c s", p=128)
        if "W" in DBG_PHASES:
            p.wait("sync", "pe", p.semc["pe"])
            p.wait("sync", "ve", p.semc["ve"])
            p.load(wobv[:, 0, :, :], wohv)
            dwo = p.load(wobv[:, 1, :, :], wolv)
        for tt2 in range(16 if "W" in DBG_PHASES else 0):
            p.wait("sync", "pe", p.semc["pe"])
            p.wait("sync", "ve", p.semc["ve"])
            p.wait("sync", "ds", st_total)
            p.load(wgtbv[:, 0, :, :], wghv[:, :, tt2 * 128:(tt2 + 1) * 128])
            p.load(wgtbv[:, 1, :, :], wglv[:, :, tt2 * 128:(tt2 + 1) * 128])
            p.load(xbuf, x_tm[b, tt2 * 128:(tt2 + 1) * 128, :])
            dg = p.load(gbuf, g_scr[b, tt2 * 128:(tt2 + 1) * 128, :])
            p.wait("tensor", "dl", dg)
            p.wait("tensor", "dl", dwo)
            p.barrier_psum("tensor")
            for c in range(8):
                for half in range(2):
                    for gu, wu in ((0, 0), (0, 1), (1, 0)):
                        p.mm(pD[:, half * 512:(half + 1) * 512],
                             wgtbv[:, gu, c, :],
                             wobv[:, wu, c, half * 512:(half + 1) * 512],
                             c == 0 and gu == 0 and wu == 0, False)
            for half in range(2):
                p.mm(pD[:, half * 512:(half + 1) * 512], onesrow[0:1, 0:128],
                     bos[0:1, half * 512:(half + 1) * 512], False, True,
                     inc="pe" if half == 1 else None)
            peS2 = p.semc["pe"]
            p.wait("vector", "pe", peS2)
            p.wait("vector", "dl", dg)
            p.wait("vector", "ds", st_total)
            p.run("vector", lambda E: E.tensor_tensor(t1, pD[:, :], gbuf, ALU.mult))
            p.fence("vector")
            veO = p.run("vector", lambda E: E.tensor_tensor(
                obuf, t1, xbuf, ALU.add))
            p.wait("gpsimd", "ve", veO)
            st_total = p.store(out[b, tt2 * 128:(tt2 + 1) * 128, :], obuf)

    p.wait("gpsimd", "ds", p.semc["ds"])


def kernel(**inputs):
    x = np.asarray(inputs["L0_attn_residual"], dtype=np.float32)
    Wk = np.asarray(inputs["L0_attn_W_key"], dtype=np.float32)
    Wq = np.asarray(inputs["L0_attn_W_query"], dtype=np.float32)
    Wv = np.asarray(inputs["L0_attn_W_values"], dtype=np.float32)
    Wo = np.asarray(inputs["L0_attn_W_output"], dtype=np.float32)
    Bk = np.asarray(inputs["L0_attn_Bias_key"], dtype=np.float32)
    Bq = np.asarray(inputs["L0_attn_Bias_query"], dtype=np.float32)
    Bv = np.asarray(inputs["L0_attn_Bias_values"], dtype=np.float32)
    Bo = np.asarray(inputs["L0_attn_Bias_output"], dtype=np.float32)
    W1 = np.asarray(inputs["L0_ffn_W1"], dtype=np.float32)
    b1 = np.asarray(inputs["L0_ffn_b1"], dtype=np.float32)
    W2 = np.asarray(inputs["L0_ffn_W2"], dtype=np.float32)
    b2 = np.asarray(inputs["L0_ffn_b2"], dtype=np.float32)

    wk_t = np.ascontiguousarray(Wk.transpose(1, 0, 2).reshape(D, NHEAD * H))
    wq_t = np.ascontiguousarray(Wq.transpose(1, 0, 2).reshape(D, NHEAD * H))
    wv_t = np.ascontiguousarray(Wv.transpose(1, 0, 2).reshape(D, NHEAD * H))
    wout = np.ascontiguousarray(Wo.reshape(NHEAD * H, D))
    bk_m = np.ascontiguousarray(Bk.reshape(8, 128).T)
    bq_m = np.ascontiguousarray(Bq.reshape(8, 128).T)
    bv_row = np.ascontiguousarray(Bv.reshape(1, NHEAD * H))
    bo_row = np.ascontiguousarray(Bo.reshape(1, D))
    b1_m = np.ascontiguousarray(b1.reshape(32, 128).T)
    b2_row = np.ascontiguousarray(b2.reshape(1, D))
    identc = np.eye(128, dtype=np.float32)
    maskinv = (np.arange(128)[None, :] < np.arange(128)[:, None]).astype(np.uint8)

    nc = build_nc()
    in_maps = []
    for c in range(N_CORES):
        xs = x[B_CORE * c: B_CORE * (c + 1)]
        in_maps.append({
            "x_tm": np.ascontiguousarray(xs),
            "x_fm": np.ascontiguousarray(xs.transpose(0, 2, 1)),
            "wk_t": wk_t, "wq_t": wq_t, "wv_t": wv_t, "wout": wout,
            "bk_m": bk_m, "bq_m": bq_m, "bv_row": bv_row, "bo_row": bo_row,
            "w1": W1, "b1_m": b1_m, "w2": W2, "b2_row": b2_row,
            "identc": identc, "maskinv": maskinv,
        })
    res = run_bass_kernel_spmd(nc, in_maps, core_ids=list(range(N_CORES)))
    return np.concatenate([r["out"] for r in res.results], axis=0)


# version 08
# speedup vs baseline: 1.5123x; 1.0929x over previous
import numpy as np
import concourse.bass as bass
import concourse.mybir as mybir
from concourse.bass_utils import run_bass_kernel_spmd
from contextlib import ExitStack

F32 = mybir.dt.float32
BF16 = mybir.dt.bfloat16
U8 = mybir.dt.uint8
AF = mybir.ActivationFunctionType
ALU = mybir.AluOpType
AX = mybir.AxisListType.X

B_CORE = 4
S = 2048
D = 1024
NHEAD = 16
H = 64
DFF = 4096
MAGIC = 12582912.0
N_CORES = 8

ENGS = ("sync", "tensor", "vector", "scalar", "gpsimd")

DBG_PHASES = "QLFAW"
DBG_BATCHES = B_CORE
DBG_TAP = False


class Prog:
    def __init__(self, nc):
        self.nc = nc
        self.ops = {e: [] for e in ENGS}
        self.semh = {}
        self.semc = {}
        self.last_wait = {}

    def sem(self, name, handle):
        self.semh[name] = handle
        self.semc[name] = 0

    def wait(self, eng, sem, val):
        if val is None or val <= 0:
            return
        key = (eng, sem)
        if self.last_wait.get(key, -1) >= val:
            return
        self.last_wait[key] = val
        h = self.semh[sem]
        self.ops[eng].append(lambda E, h=h, v=val: E.wait_ge(h, v))

    OWN = {"tensor": "pe", "vector": "ve", "scalar": "ac"}

    def fence(self, eng):
        own = self.OWN[eng]
        self.wait(eng, own, self.semc[own])

    def run(self, eng, fn, inc=None, n=1):
        if inc is None and eng in ("vector", "scalar"):
            inc = self.OWN[eng]
        tok = None
        if inc is not None:
            self.semc[inc] += n
            tok = self.semc[inc]
            h = self.semh[inc]

            def cl(E, fn=fn, h=h, n=n):
                fn(E).then_inc(h, n)
        else:
            def cl(E, fn=fn):
                fn(E)
        self.ops[eng].append(cl)
        return tok

    def load(self, out_ap, in_ap):
        return self.run("sync", lambda E, o=out_ap, i=in_ap: E.dma_start(out=o, in_=i),
                        inc="dl", n=16)

    def store(self, out_ap, in_ap):
        return self.run("gpsimd", lambda E, o=out_ap, i=in_ap: E.dma_start(out=o, in_=i),
                        inc="ds", n=16)

    def mm(self, out, lhsT, rhs, start, stop, inc=None):
        return self.run("tensor",
                        lambda E, o=out, l=lhsT, r=rhs, s=start, p=stop:
                        E.matmul(o, l, r, start=s, stop=p), inc=inc)

    def barrier_psum(self, *engs):
        for e in engs:
            for sem in ("pe", "ve", "ac"):
                if sem == self.OWN.get(e):
                    continue
                self.wait(e, sem, self.semc[sem])


def build_nc():
    nc = bass.Bass("TRN2", target_bir_lowering=False)

    x_tm = nc.dram_tensor("x_tm", [B_CORE, S, D], F32, kind="ExternalInput").ap()
    x_fm = nc.dram_tensor("x_fm", [B_CORE, D, S], F32, kind="ExternalInput").ap()
    wk_t = nc.dram_tensor("wk_t", [D, NHEAD * H], F32, kind="ExternalInput").ap()
    wq_t = nc.dram_tensor("wq_t", [D, NHEAD * H], F32, kind="ExternalInput").ap()
    wv_t = nc.dram_tensor("wv_t", [D, NHEAD * H], F32, kind="ExternalInput").ap()
    wout = nc.dram_tensor("wout", [NHEAD * H, D], F32, kind="ExternalInput").ap()
    bk_m = nc.dram_tensor("bk_m", [128, 8], F32, kind="ExternalInput").ap()
    bq_m = nc.dram_tensor("bq_m", [128, 8], F32, kind="ExternalInput").ap()
    bv_row = nc.dram_tensor("bv_row", [1, NHEAD * H], F32, kind="ExternalInput").ap()
    bo_row = nc.dram_tensor("bo_row", [1, D], F32, kind="ExternalInput").ap()
    w1 = nc.dram_tensor("w1", [DFF, D], F32, kind="ExternalInput").ap()
    b1_m = nc.dram_tensor("b1_m", [128, 32], F32, kind="ExternalInput").ap()
    w2 = nc.dram_tensor("w2", [D, DFF], F32, kind="ExternalInput").ap()
    b2_row = nc.dram_tensor("b2_row", [1, D], F32, kind="ExternalInput").ap()
    identc = nc.dram_tensor("identc", [128, 128], F32, kind="ExternalInput").ap()
    maskinv = nc.dram_tensor("maskinv", [128, 128], U8, kind="ExternalInput").ap()
    out = nc.dram_tensor("out", [B_CORE, S, D], F32, kind="ExternalOutput").ap()

    kw = {"kind": "ExternalOutput"} if DBG_TAP else {}
    wq1t = nc.dram_tensor("wq1t", [D, DFF], BF16, **kw).ap()
    wq2t = nc.dram_tensor("wq2t", [DFF, D], BF16, **kw).ap()
    g_scr = nc.dram_tensor("g_scr", [B_CORE, S, D], F32, **kw).ap()
    wgh_scr = nc.dram_tensor("wgh_scr", [NHEAD * H, S], BF16, **kw).ap()
    wgl_scr = nc.dram_tensor("wgl_scr", [NHEAD * H, S], BF16, **kw).ap()
    awh = nc.dram_tensor("awh", [3, D, NHEAD * H], BF16).ap()
    awl = nc.dram_tensor("awl", [3, D, NHEAD * H], BF16).ap()
    woh = nc.dram_tensor("woh", [NHEAD * H, D], BF16).ap()
    wol = nc.dram_tensor("wol", [NHEAD * H, D], BF16).ap()

    with ExitStack() as ctx:
        E_ = ctx.enter_context
        xh = E_(nc.sbuf_tensor("xh", [128, 16384], BF16)).ap()
        xl = E_(nc.sbuf_tensor("xl", [128, 16384], BF16)).ap()
        r2 = E_(nc.sbuf_tensor("r2", [128, 8192], F32)).ap()
        kqK = E_(nc.sbuf_tensor("kqK", [128, 4096], F32)).ap()
        kqQ = E_(nc.sbuf_tensor("kqQ", [128, 4096], F32)).ap()
        vsl = E_(nc.sbuf_tensor("vsl", [128, 16 * 4 * 65], F32)).ap()
        srow1 = E_(nc.sbuf_tensor("srow1", [1, 16 * 65], F32)).ap()
        crow1 = E_(nc.sbuf_tensor("crow1", [1, 16 * 65], F32)).ap()
        rows = E_(nc.sbuf_tensor("rows", [1, 2048], F32)).ap()
        wkb = E_(nc.sbuf_tensor("wkb", [128, 2048], BF16)).ap()
        wqb = E_(nc.sbuf_tensor("wqb", [128, 2048], BF16)).ap()
        wvb = E_(nc.sbuf_tensor("wvb", [128, 4096], BF16)).ap()
        wq1b = E_(nc.sbuf_tensor("wq1b", [128, 4096], BF16)).ap()
        wq2b = E_(nc.sbuf_tensor("wq2b", [128, 3072], BF16)).ap()
        wgtb = E_(nc.sbuf_tensor("wgtb", [128, 4096], BF16)).ap()
        ident = E_(nc.sbuf_tensor("ident", [128, 128], F32)).ap()
        mski = E_(nc.sbuf_tensor("mski", [128, 128], U8)).ap()
        onest = E_(nc.sbuf_tensor("onest", [128, 128], F32)).ap()
        onescol = E_(nc.sbuf_tensor("onescol", [128, 1], F32)).ap()
        onesrow = E_(nc.sbuf_tensor("onesrow", [1, 512], F32)).ap()
        b1s = E_(nc.sbuf_tensor("b1s", [128, 32], F32)).ap()
        bks = E_(nc.sbuf_tensor("bks", [128, 8], F32)).ap()
        bqs = E_(nc.sbuf_tensor("bqs", [128, 8], F32)).ap()
        colt = E_(nc.sbuf_tensor("colt", [128, 8], F32)).ap()

        pA = E_(nc.psum_tensor("pA", [128, 512], F32)).ap()
        pB = E_(nc.psum_tensor("pB", [128, 512], F32)).ap()
        pC = E_(nc.psum_tensor("pC", [128, 1024], F32)).ap()
        pD = E_(nc.psum_tensor("pD", [128, 1024], F32)).ap()
        pE = E_(nc.psum_tensor("pE", [128, 512], F32)).ap()
        pF = E_(nc.psum_tensor("pF", [128, 512], F32)).ap()

        sem_handles = {}
        for name in ("dl", "ds", "pe", "ve", "ac"):
            sem_handles[name] = E_(nc.semaphore(name))

        block = E_(nc.Block())
        p = Prog(nc)
        for name, h in sem_handles.items():
            p.sem(name, h)

        build_program(p, locals())

        @block.sync
        def _(sync):
            for f in p.ops["sync"]:
                f(sync)

        @block.tensor
        def _(tensor):
            for f in p.ops["tensor"]:
                f(tensor)

        @block.vector
        def _(vector):
            for f in p.ops["vector"]:
                f(vector)

        @block.scalar
        def _(scalar):
            for f in p.ops["scalar"]:
                f(scalar)

        @block.gpsimd
        def _(gpsimd):
            for f in p.ops["gpsimd"]:
                f(gpsimd)

    return nc


def build_program(p, t):
    xh, xl, r2 = t["xh"], t["xl"], t["r2"]
    kqK, kqQ, vsl = t["kqK"], t["kqQ"], t["vsl"]
    srow1, crow1, rows = t["srow1"], t["crow1"], t["rows"]
    wkb, wqb, wvb = t["wkb"], t["wqb"], t["wvb"]
    wq1b, wq2b, wgtb = t["wq1b"], t["wq2b"], t["wgtb"]
    gsb = t["kqK"][:, 0:1024]
    hfp = t["kqK"][:, 1024:1280]
    stage = wq1b[:, 3072:3328]
    stg2 = wgtb
    uh = wq1b[0:64, 0:1024]
    ul = wq1b[64:128, 0:1024]
    ident, mski, onest = t["ident"], t["mski"], t["onest"]
    onescol, onesrow = t["onescol"], t["onesrow"]
    b1s, bks, bqs, colt = t["b1s"], t["bks"], t["bqs"], t["colt"]
    bvs = t["rows"][0:1, 1024:2048]
    bos = t["rows"][0:1, 1024:2048]
    b2s = t["rows"][0:1, 1024:2048]
    pA, pB, pC, pD, pE, pF = t["pA"], t["pB"], t["pC"], t["pD"], t["pE"], t["pF"]
    x_tm, x_fm = t["x_tm"], t["x_fm"]
    wk_t, wq_t, wv_t, wout = t["wk_t"], t["wq_t"], t["wv_t"], t["wout"]
    w1, b1_m, w2 = t["w1"], t["b1_m"], t["w2"]
    identc, maskinv, out = t["identc"], t["maskinv"], t["out"]
    wq1t, wq2t, g_scr = t["wq1t"], t["wq2t"], t["g_scr"]
    wgh_scr, wgl_scr = t["wgh_scr"], t["wgl_scr"]
    awh, awl, woh, wol = t["awh"], t["awl"], t["woh"], t["wol"]
    vslv = vsl.rearrange("p (c h e) -> p c h e", c=16, h=4)

    p.load(ident[:, :], identc)
    p.load(mski[:, :], maskinv)
    p.load(b1s[:, :], t["b1_m"])
    p.load(bks[:, :], t["bk_m"])
    tok_consts = p.load(bqs[:, :], t["bq_m"])

    p.run("vector", lambda E: E.memset(onest[:, :], 1.0))
    p.run("vector", lambda E: E.memset(onescol[:, :], 1.0))
    p.run("vector", lambda E: E.memset(onesrow[:, :], 1.0))
    p.run("vector", lambda E: E.memset(vslv[:, :, :, 64:65], 1.0))
    p.fence("vector")

    f32buf = r2[:, 0:1024]
    st_aw = None

    def split_stage(src_ap, dst_h, dst_l):
        nonlocal st_aw
        p.wait("sync", "ds", st_aw)
        p.wait("sync", "ve", p.semc["ve"])
        dt_ = p.load(f32buf, src_ap)
        p.wait("vector", "dl", dt_)
        p.wait("vector", "ds", st_aw)
        p.run("vector", lambda E: E.tensor_copy(stg2[:, 0:1024], f32buf))
        p.fence("vector")
        vv = p.run("vector", lambda E: E.tensor_tensor(
            stg2[:, 1024:2048], f32buf, stg2[:, 0:1024], ALU.subtract))
        p.wait("gpsimd", "ve", vv)
        p.store(dst_h, stg2[:, 0:1024])
        st_aw = p.store(dst_l, stg2[:, 1024:2048])

    for mi, wsrc in enumerate((wk_t, wq_t, wv_t)):
        for i in range(8):
            split_stage(wsrc[i * 128:(i + 1) * 128, :],
                        awh[mi, i * 128:(i + 1) * 128, :],
                        awl[mi, i * 128:(i + 1) * 128, :])
    for i in range(8):
        split_stage(wout[i * 128:(i + 1) * 128, :],
                    woh[i * 128:(i + 1) * 128, :],
                    wol[i * 128:(i + 1) * 128, :])

    def quantize(w_dram, wq_dram, n_rows, acc_i, st_prev):
        n_cols = w_dram.shape[1]
        ntile = n_rows // 128
        ncol_t = n_cols // 128
        wbuf = [r2[:, 0:n_cols], r2[:, n_cols:2 * n_cols]]
        acc = colt[:, acc_i:acc_i + 1]
        sc = colt[:, acc_i + 1:acc_i + 2]
        red = colt[:, 6:7]
        p.run("vector", lambda E, a=acc: E.memset(a, 0.0))
        p.fence("vector")
        ve_toks = [None] * ntile
        for i in range(ntile):
            b = wbuf[i % 2]
            if i >= 2:
                p.wait("sync", "ve", ve_toks[i - 2])
            else:
                p.wait("sync", "ve", p.semc["ve"])
                p.wait("sync", "pe", p.semc["pe"])
                p.wait("sync", "ac", p.semc["ac"])
            dt_ = p.load(b, w_dram[i * 128:(i + 1) * 128, :])
            p.wait("scalar", "dl", dt_)
            aca = p.run("scalar", lambda E, b=b: E.activation(b, b, AF.Abs))
            p.wait("vector", "ac", aca)
            p.run("vector", lambda E, b=b, r=red: E.tensor_reduce(
                r, b, AX, ALU.add))
            p.fence("vector")
            ve_toks[i] = p.run("vector", lambda E, a=acc, r=red: E.tensor_tensor(
                a, a, r, ALU.add))
            p.fence("vector")
        p.wait("tensor", "ve", ve_toks[-1])
        p.wait("tensor", "dl", tok_consts)
        pe1 = p.mm(pF[0:1, 0:1], acc, onescol[:, :], True, True, inc="pe")
        p.wait("vector", "pe", pe1)
        sc1 = rows[0:1, 0:1]
        p.run("vector", lambda E, s=sc1: E.tensor_scalar(
            s, pF[0:1, 0:1], 1.0 / (n_rows * n_cols), 1e-5, ALU.mult, ALU.add))
        p.fence("vector")
        ve1 = p.run("vector", lambda E, s=sc1: E.reciprocal(s, s))
        p.wait("tensor", "ve", ve1)
        pe2 = p.mm(pF[0:128, 0:1], onesrow[0:1, 0:128], sc1, True, True, inc="pe")
        p.wait("vector", "pe", pe2)
        ve2 = p.run("vector", lambda E, s=sc: E.tensor_copy(s, pF[0:128, 0:1]))
        pe_cons = [None] * ntile
        st_toks = [st_prev, st_prev]
        last_store = st_prev
        vet_prev = ve2
        for i in range(ntile):
            b = wbuf[i % 2]
            if i == 0:
                p.wait("sync", "ve", ve2)
            else:
                p.wait("sync", "pe", pe_cons[i - 1])
            dt_ = p.load(b, w_dram[i * 128:(i + 1) * 128, :])
            p.wait("vector", "dl", dt_)
            p.run("vector", lambda E, b=b, s=sc: E.tensor_scalar(
                b, b, s, MAGIC, ALU.mult, ALU.add))
            p.fence("vector")
            p.run("vector", lambda E, b=b: E.tensor_scalar(
                b, b, MAGIC, -1.0, ALU.subtract, ALU.max))
            p.fence("vector")
            veq = p.run("vector", lambda E, b=b: E.tensor_scalar(
                b, b, 1.0, None, ALU.min))
            p.wait("tensor", "ve", veq)
            for j in range(ncol_t):
                sg = stage[:, (j % 2) * 128:(j % 2) * 128 + 128]
                p.wait("tensor", "ve", vet_prev)
                pet = p.run("tensor", lambda E, src=b[:, j * 128:(j + 1) * 128]:
                            E.transpose(pE[0:128, 0:128], src, ident[:, :]), inc="pe")
                p.wait("vector", "ds", st_toks[j % 2])
                p.wait("vector", "pe", pet)
                vet = p.run("vector", lambda E, sg=sg: E.tensor_copy(
                    sg, pE[0:128, 0:128]))
                vet_prev = vet
                p.wait("gpsimd", "ve", vet)
                stt = p.store(wq_dram[j * 128:(j + 1) * 128, i * 128:(i + 1) * 128], sg)
                st_toks[j % 2] = stt
                last_store = stt
            pe_cons[i] = p.semc["pe"]
        return last_store

    st_total = st_aw
    if "Q" in DBG_PHASES:
        st_total = quantize(w1, wq1t, DFF, 0, st_total)
        st_total = quantize(w2, wq2t, D, 2, st_total)
    q_done = st_total
    ve_xn = 0

    for b in range(DBG_BATCHES):
        xf = r2[:, 0:4096]
        sq = [r2[:, 4096:4608], r2[:, 4608:5120]]
        mb = r2[:, 5120:5632]
        rb = r2[:, 5632:6144]
        xfv = xf.rearrange("p (dc t) -> p dc t", dc=8)
        x_fmv = x_fm[b].rearrange("(dc p) t -> p dc t", p=128)
        m_row = rows[0:1, 0:512]
        var_row = rows[0:1, 512:1024]
        rstd_row = rows[0:1, 1024:1536]
        for tt in range(4 if "L" in DBG_PHASES else 0):
            p.wait("sync", "pe", p.semc["pe"])
            p.wait("sync", "ve", p.semc["ve"])
            p.wait("sync", "ac", p.semc["ac"])
            dt_ = p.load(xfv, x_fmv[:, :, tt * 512:(tt + 1) * 512])
            p.barrier_psum("tensor")
            p.wait("tensor", "dl", dt_)
            p.wait("scalar", "dl", dt_)
            for dc in range(8):
                p.mm(pA[0:1, 0:512], onescol[:, :], xfv[:, dc, :], dc == 0, dc == 7,
                     inc="pe" if dc == 7 else None)
            sq_mm = [None, None]
            for dc in range(8):
                p.wait("scalar", "pe", sq_mm[dc % 2])
                act = p.run("scalar", lambda E, dc=dc, s=sq[dc % 2]: E.activation(
                    s, xfv[:, dc, :], AF.Square))
                p.wait("tensor", "ac", act)
                sq_mm[dc % 2] = p.mm(pB[0:1, 0:512], onescol[:, :], sq[dc % 2],
                                     dc == 0, dc == 7, inc="pe")
            pe_stats = p.semc["pe"]
            p.wait("vector", "pe", pe_stats)
            p.run("vector", lambda E: E.tensor_scalar(
                m_row, pA[0:1, 0:512], 1.0 / D, None, ALU.mult))
            p.fence("vector")
            p.run("vector", lambda E: E.tensor_tensor(
                var_row, m_row, m_row, ALU.mult))
            p.fence("vector")
            p.run("vector", lambda E: E.scalar_tensor_tensor(
                var_row, pB[0:1, 0:512], 1.0 / D, var_row, ALU.mult, ALU.subtract))
            p.fence("vector")
            ve_v = p.run("vector", lambda E: E.tensor_scalar(
                var_row, var_row, 1e-5, None, ALU.add))
            p.wait("scalar", "ve", ve_v)
            p.run("scalar", lambda E: E.activation(rstd_row, var_row, AF.Ln))
            p.fence("scalar")
            ac_r = p.run("scalar", lambda E: E.activation(
                rstd_row, rstd_row, AF.Exp, scale=-0.5))
            p.wait("tensor", "ve", ve_v)
            p.wait("tensor", "ac", ac_r)
            p.mm(pC[:, 0:512], onesrow[0:1, 0:128], m_row, True, True)
            pe_bc = p.mm(pC[:, 512:1024], onesrow[0:1, 0:128], rstd_row, True, True,
                         inc="pe")
            p.wait("vector", "pe", pe_bc)
            p.run("vector", lambda E: E.tensor_copy(mb, pC[:, 0:512]))
            p.run("vector", lambda E: E.tensor_copy(rb, pC[:, 512:1024]))
            p.wait("vector", "pe", p.semc["pe"])
            p.wait("vector", "ds", st_total if b > 0 else 0)
            p.fence("vector")
            for dc in range(8):
                p.run("vector", lambda E, i=xfv[:, dc, :]: E.scalar_tensor_tensor(
                    i, i, 1.0, mb, ALU.mult, ALU.subtract))
            p.fence("vector")
            for dc in range(8):
                p.run("vector", lambda E, i=xfv[:, dc, :]: E.tensor_tensor(
                    i, i, rb, ALU.mult))
            p.fence("vector")
            for dc in range(8):
                o = xh[:, dc * 2048 + tt * 512: dc * 2048 + (tt + 1) * 512]
                p.run("vector", lambda E, o=o, i=xfv[:, dc, :]: E.tensor_copy(o, i))
            p.fence("vector")
            for dc in range(8):
                oh = xh[:, dc * 2048 + tt * 512: dc * 2048 + (tt + 1) * 512]
                ol = xl[:, dc * 2048 + tt * 512: dc * 2048 + (tt + 1) * 512]
                ve_xn = p.run("vector", lambda E, oh=oh, ol=ol, i=xfv[:, dc, :]:
                              E.tensor_tensor(ol, i, oh, ALU.subtract))

        r2bf = r2.bitcast(BF16)
        hh = r2bf[:, 0:8192]
        hl = r2bf[:, 8192:16384]
        wq1v = wq1t.rearrange("(dc p) f -> p dc f", p=128)
        wq1bv = wq1b.rearrange("p (u dc f) -> p u dc f", u=4, dc=8)
        wq2bv = wq2b.rearrange("p (u f) -> p u f", u=3)
        if "F" in DBG_PHASES:
            p.wait("sync", "ds", q_done)
            p.wait("sync", "ac", p.semc["ac"])
            p.wait("sync", "pe", p.semc["pe"])
            d_b2 = p.load(b2s, t["b2_row"])
            p.wait("tensor", "dl", d_b2)
        d_toks = {}
        pe_wq = {}
        ve_h_prev = 0
        hdone = {}
        rl = {}

        def emit_ffn2(fc, u, gi):
            ac_h, ve_h = hdone[fc]
            p.wait("tensor", "ac", ac_h)
            p.wait("tensor", "ve", ve_h)
            for sub in range(2):
                gp = pC if sub == 0 else pD
                for half in range(2):
                    for hs in (hh, hl):
                        p.mm(gp[:, half * 512:(half + 1) * 512],
                             hs[:, fc * 256 + sub * 128:
                                fc * 256 + sub * 128 + 128],
                             wq2bv[:, u, half * 512:(half + 1) * 512],
                             fc == 0 and hs is hh, False,
                             inc="pe" if (sub == 1 and half == 1 and hs is hl)
                             else None)
            pe_wq[gi] = p.semc["pe"]

        for ts in range(8 if "F" in DBG_PHASES else 0):
            t0 = ts * 256
            for fc in range(32):
                gi = ts * 32 + fc
                u = gi % 3
                if gi == 0:
                    p.wait("sync", "pe", p.semc["pe"])
                    p.wait("sync", "ac", p.semc["ac"])
                    p.wait("sync", "ve", p.semc["ve"])
                    p.wait("sync", "ds", st_total)
                if gi >= 3:
                    p.wait("sync", "pe", pe_wq[gi - 3])
                p.load(wq1bv[:, u, :, :], wq1v[:, :, fc * 128:(fc + 1) * 128])
                d_toks[gi] = p.load(wq2bv[:, u, :], wq2t[fc * 128:(fc + 1) * 128, :])
                if gi == 0:
                    p.wait("tensor", "ve", ve_xn)
                    p.barrier_psum("tensor")
                p.wait("tensor", "dl", d_toks[gi])
                hp = pA if fc % 2 == 0 else pB
                p.wait("tensor", "ac", rl.get(gi - 2, 0))
                for dc in range(8):
                    for cm, xs in ((0, xh), (1, xl)):
                        p.mm(hp[:, 0:256], wq1bv[:, u, dc, :],
                             xs[:, dc * 2048 + t0: dc * 2048 + t0 + 256],
                             dc == 0 and cm == 0, dc == 7 and cm == 1,
                             inc="pe" if (dc == 7 and cm == 1) else None)
                pe_h = p.semc["pe"]
                p.wait("scalar", "pe", pe_h)
                p.wait("scalar", "ve", ve_h_prev)
                rl[gi] = p.run("scalar", lambda E, i=hp[:, 0:256], fc=fc:
                               E.activation(hfp[:, :], i, AF.Relu,
                                            bias=b1s[:, fc:fc + 1]))
                p.fence("scalar")
                ac_h = p.run("scalar", lambda E, o=hh[:, fc * 256:(fc + 1) * 256]:
                             E.activation(o, hfp[:, :], AF.Copy))
                p.wait("vector", "ac", ac_h)
                ve_h = p.run("vector", lambda E, o=hl[:, fc * 256:(fc + 1) * 256],
                             h=hh[:, fc * 256:(fc + 1) * 256]:
                             E.tensor_tensor(o, hfp[:, :], h, ALU.subtract))
                ve_h_prev = ve_h
                hdone[fc] = (ac_h, ve_h)
                if fc > 0:
                    emit_ffn2(fc - 1, (gi - 1) % 3, gi - 1)
            emit_ffn2(31, (ts * 32 + 31) % 3, ts * 32 + 31)
            for sub in range(2):
                gp = pC if sub == 0 else pD
                for half in range(2):
                    p.mm(gp[:, half * 512:(half + 1) * 512], onesrow[0:1, 0:128],
                         b2s[0:1, half * 512:(half + 1) * 512], False, True,
                         inc="pe" if (sub == 1 and half == 1) else None)
            pe_g = p.semc["pe"]
            pe_wq[ts * 32 + 31] = pe_g
            p.wait("scalar", "pe", pe_g)
            for sub in range(2):
                gp = pC if sub == 0 else pD
                p.wait("scalar", "ds", st_total)
                ac_g = p.run("scalar", lambda E, gp=gp: E.activation(
                    gsb[:, :], gp[:, :], AF.Silu))
                p.wait("gpsimd", "ac", ac_g)
                st_total = p.store(g_scr[b, t0 + sub * 128: t0 + sub * 128 + 128, :],
                                   gsb[:, :])

        esl = r2
        bank_ac = {0: p.semc["ac"], 1: p.semc["ac"]}
        ustg_ds = [st_total, st_total]
        nbufs = wq2b.bitcast(F32)
        nbuf = [nbufs[0:65, 0:512], nbufs[0:65, 512:1024]]
        zrow = [rows[0:1, 0:512], rows[0:1, 512:1024]]
        pend = None
        ui = 0
        prev_fix = p.semc["ve"]
        zx_par = {0: 0, 1: 0}
        zb_par = {0: 0, 1: 0}
        veW_prev = 0

        def flush_norm():
            nonlocal pend, st_total, veW_prev
            if pend is None:
                return
            par0, n0, qt0 = pend["par"], pend["n"], pend["qt"]
            p.wait("tensor", "ac", pend["zx"])
            p.wait("tensor", "ve", veW_prev)
            zb = p.mm(pA[0:64, :], onesrow[0:1, 0:64], zrow[par0], True, True,
                      inc="pe")
            zb_par[par0] = zb
            nb = nbuf[par0]
            p.wait("vector", "pe", zb)
            p.wait("vector", "ds", ustg_ds[par0])
            p.fence("vector")
            uhs = uh[:, par0 * 512:(par0 + 1) * 512]
            uls = ul[:, par0 * 512:(par0 + 1) * 512]
            p.run("vector", lambda E, nb=nb: E.tensor_tensor(
                nb[0:64, :], nb[0:64, :], pA[0:64, :], ALU.mult))
            p.fence("vector")
            p.run("vector", lambda E, o=uhs, nb=nb: E.tensor_copy(o, nb[0:64, :]))
            p.fence("vector")
            veW = p.run("vector", lambda E, o=uls, nb=nb, h=uhs: E.tensor_tensor(
                o, nb[0:64, :], h, ALU.subtract))
            veW_prev = veW
            p.wait("gpsimd", "ve", veW)
            p.store(wgh_scr[64 * n0: 64 * n0 + 64, qt0 * 512:(qt0 + 1) * 512], uhs)
            st_total = p.store(
                wgl_scr[64 * n0: 64 * n0 + 64, qt0 * 512:(qt0 + 1) * 512], uls)
            ustg_ds[par0] = st_total
            pend = None
        awhv = awh.rearrange("m (dc p) n -> m p dc n", p=128)
        awlv = awl.rearrange("m (dc p) n -> m p dc n", p=128)
        wkbv = wkb.rearrange("p (u dc n) -> p u dc n", u=2, dc=8)
        wqbv = wqb.rearrange("p (u dc n) -> p u dc n", u=2, dc=8)
        wvbv = wvb.rearrange("p (u dc n) -> p u dc n", u=2, dc=8)
        for qd in range(4 if "A" in DBG_PHASES else 0):
            if qd == 0:
                p.wait("sync", "ac", p.semc["ac"])
                p.wait("sync", "ve", p.semc["ve"])
                p.wait("sync", "pe", p.semc["pe"])
                d_bv = p.load(bvs, t["bv_row"])
                p.wait("tensor", "dl", d_bv)
            p.load(wvbv[:, 0, :, :], awhv[2, :, :, qd * 256:(qd + 1) * 256])
            dv = p.load(wvbv[:, 1, :, :], awlv[2, :, :, qd * 256:(qd + 1) * 256])
            p.wait("tensor", "dl", dv)
            p.wait("tensor", "ve", p.semc["ve"])
            p.barrier_psum("tensor")
            for kc in range(16):
                for dc in range(8):
                    for xs, wu in ((xh, 0), (xh, 1), (xl, 0)):
                        p.mm(pF[:, 0:256],
                             xs[:, dc * 2048 + kc * 128: dc * 2048 + (kc + 1) * 128],
                             wvbv[:, wu, dc, :],
                             dc == 0 and xs is xh and wu == 0, False)
                peV = p.mm(pF[:, 0:256], onesrow[0:1, 0:128],
                           bvs[0:1, qd * 256:(qd + 1) * 256], False, True, inc="pe")
                p.wait("vector", "pe", peV)
                veV = p.run("vector", lambda E, kc=kc: E.tensor_copy(
                    vslv[:, kc, :, 0:64],
                    pF[:, 0:256].rearrange("p (h e) -> p h e", h=4)))
                p.wait("tensor", "ve", veV)
            kqKb = kqK.bitcast(BF16)
            kqQb = kqQ.bitcast(BF16)
            ktmp = wgtb.bitcast(F32)[:, 0:512]
            qtmp = wgtb.bitcast(F32)[:, 512:1024]
            klt = wq1b[:, 1024:1536]
            for pp in range(2):
                gp_ = 2 * qd + pp
                p.wait("sync", "pe", p.semc["pe"])
                p.load(wkbv[:, 0, :, :], awhv[0, :, :, gp_ * 128:(gp_ + 1) * 128])
                p.load(wkbv[:, 1, :, :], awlv[0, :, :, gp_ * 128:(gp_ + 1) * 128])
                p.load(wqbv[:, 0, :, :], awhv[1, :, :, gp_ * 128:(gp_ + 1) * 128])
                dq = p.load(wqbv[:, 1, :, :], awlv[1, :, :, gp_ * 128:(gp_ + 1) * 128])
                p.wait("tensor", "dl", dq)
                p.wait("tensor", "ve", p.semc["ve"])
                p.wait("tensor", "ds", p.semc["ds"])
                p.barrier_psum("tensor")
                pr_ds = 0
                for tt in range(4):
                    ttsl = slice(tt * 512, (tt + 1) * 512)
                    for ps, wb in ((pA, wkbv), (pB, wqbv)):
                        for dc in range(8):
                            for wu, xs in ((0, xh), (0, xl), (1, xh)):
                                p.mm(ps[:, :], wb[:, wu, dc, :],
                                     xs[:, dc * 2048 + tt * 512:
                                        dc * 2048 + (tt + 1) * 512],
                                     dc == 0 and wu == 0 and xs is xh,
                                     dc == 7 and wu == 1,
                                     inc="pe" if (dc == 7 and wu == 1) else None)
                    peQ = p.semc["pe"]
                    p.wait("vector", "pe", peQ)
                    p.wait("vector", "ds", pr_ds)
                    p.run("vector", lambda E, g=gp_: E.tensor_scalar(
                        ktmp, pA[:, :], bks[:, g:g + 1], None, ALU.add))
                    ve_ts = p.run("vector", lambda E, g=gp_: E.tensor_scalar(
                        qtmp, pB[:, :], bqs[:, g:g + 1], None, ALU.add))
                    p.fence("vector")
                    for hs in range(2):
                        R = slice(64 * hs, 64 * hs + 64)
                        kP_h = kqKb[:, hs * 2048:hs * 2048 + 2048]
                        qhh_h = kqQb[:, hs * 2048:hs * 2048 + 2048]
                        qll_h = kqQb[:, 4096 + hs * 2048:4096 + hs * 2048 + 2048]
                        p.run("vector", lambda E, o=kP_h[R, ttsl], i=ktmp[R, :]:
                              E.tensor_copy(o, i))
                        p.run("vector", lambda E, o=qhh_h[R, ttsl], i=qtmp[R, :]:
                              E.tensor_copy(o, i))
                    p.fence("vector")
                    for hs in range(2):
                        R = slice(64 * hs, 64 * hs + 64)
                        kP_h = kqKb[:, hs * 2048:hs * 2048 + 2048]
                        qhh_h = kqQb[:, hs * 2048:hs * 2048 + 2048]
                        qll_h = kqQb[:, 4096 + hs * 2048:4096 + hs * 2048 + 2048]
                        p.run("vector", lambda E, o=klt[R, :], i=ktmp[R, :],
                              h=kP_h[R, ttsl]: E.tensor_tensor(o, i, h, ALU.subtract))
                        p.run("vector", lambda E, o=qll_h[R, ttsl], i=qtmp[R, :],
                              h=qhh_h[R, ttsl]: E.tensor_tensor(o, i, h, ALU.subtract))
                    ve_ev = p.run("vector", lambda E: E.engine_nop())
                    p.wait("gpsimd", "ve", ve_ev)
                    for hs in range(2):
                        R = slice(64 * hs, 64 * hs + 64)
                        Rp = slice(64 * (1 - hs), 64 * (1 - hs) + 64)
                        kP_h = kqKb[:, hs * 2048:hs * 2048 + 2048]
                        qhh_h = kqQb[:, hs * 2048:hs * 2048 + 2048]
                        qll_h = kqQb[:, 4096 + hs * 2048:4096 + hs * 2048 + 2048]
                        p.store(kP_h[Rp, ttsl], klt[R, :])
                        p.store(qhh_h[Rp, ttsl], qhh_h[R, ttsl])
                        pr_ds = p.store(qll_h[Rp, ttsl], qll_h[R, ttsl])
                    p.wait("tensor", "ve", ve_ts)
                pair_ds = pr_ds
                for hs in range(2):
                    j = 2 * pp + hs
                    n = 4 * qd + j
                    kP = kqKb[:, hs * 2048:hs * 2048 + 2048]
                    qhh = kqQb[:, hs * 2048:hs * 2048 + 2048]
                    qll = kqQb[:, 4096 + hs * 2048:4096 + hs * 2048 + 2048]
                    p.wait("tensor", "ds", pair_ds)
                    p.wait("tensor", "ac", bank_ac[0])
                    p.wait("tensor", "ve", p.semc["ve"])
                    for cb in range(4):
                        for u in range(4):
                            kc = 4 * cb + u
                            p.mm(pC[0:1, u * 65: u * 65 + 65], onescol[:, :],
                                 vslv[:, kc, j, :], True, True,
                                 inc="pe" if u == 3 else None)
                        peC = p.semc["pe"]
                        p.wait("vector", "pe", peC)
                        vcp = p.run("vector", lambda E, cb=cb: E.tensor_copy(
                            crow1[0:1, cb * 260:(cb + 1) * 260], pC[0:1, 0:260]))
                        p.wait("tensor", "ve", vcp)
                    def scp(dst_kc, src_kc):
                        p.fence("vector")
                        p.run("vector", lambda E, d=dst_kc, s=src_kc: E.tensor_copy(
                            srow1[0:1, d * 65:(d + 1) * 65],
                            srow1[0:1, s * 65:(s + 1) * 65]))

                    def sacc(dst_kc, src_kcs, copy_first):
                        first = copy_first
                        for kc in src_kcs:
                            p.fence("vector")
                            src = crow1[0:1, kc * 65:(kc + 1) * 65]
                            dsl = srow1[0:1, dst_kc * 65:(dst_kc + 1) * 65]
                            if first:
                                p.run("vector", lambda E, d=dsl, s=src: E.tensor_copy(d, s))
                                first = False
                            else:
                                p.run("vector", lambda E, d=dsl, s=src: E.tensor_tensor(
                                    d, d, s, ALU.add))
                    sacc(11, [12, 13, 14, 15], copy_first=True)
                    scp(7, 11)
                    sacc(7, [8, 9, 10, 11], copy_first=False)
                    scp(3, 7)
                    sacc(3, [4, 5, 6, 7], copy_first=False)
                    p.fence("vector")
                    ve_sfx = p.semc["ve"]
                    p.wait("tensor", "ve", vcp)
                    for qt in range(4):
                        nmat = 4 * qt + 4
                        has_tail = nmat < 16
                        par = ui % 2
                        fix_tok = {}
                        exp_tok = {}

                        def emit_pv(kc, last):
                            p.wait("tensor", "ac", exp_tok[kc // 2])
                            p.wait("tensor", "ve", fix_tok.get(kc, 0))
                            p.mm(pE[0:65, :], vslv[:, kc, j, :],
                                 esl[:, kc * 512:(kc + 1) * 512], kc == 0, last,
                                 inc="pe" if last else None)

                        for g in range(nmat // 2):
                            sp, bank = (pC, 0) if g % 2 == 0 else (pD, 1)
                            p.wait("tensor", "ac", bank_ac[bank])
                            if g == 0 and pend is not None:
                                p.wait("tensor", "ve", pend["ncp"])
                            for u in range(2):
                                kc = 2 * g + u
                                lhs = kP[:, kc * 128:(kc + 1) * 128]
                                p.mm(sp[:, u * 512:(u + 1) * 512], lhs,
                                     qhh[:, qt * 512:(qt + 1) * 512], True, False)
                                p.mm(sp[:, u * 512:(u + 1) * 512], lhs,
                                     qll[:, qt * 512:(qt + 1) * 512], False, True,
                                     inc="pe" if u == 1 else None)
                            p.wait("scalar", "pe", p.semc["pe"])
                            p.wait("scalar", "ve", prev_fix)
                            exp_tok[g] = p.run(
                                "scalar", lambda E, g=g, sp=sp: E.activation(
                                    esl[:, 2 * g * 512: 2 * g * 512 + 1024],
                                    sp[:, :], AF.Exp, scale=0.125))
                            bank_ac[bank] = exp_tok[g]
                            p.wait("vector", "ac", exp_tok[g])
                            for u in range(2):
                                kc = 2 * g + u
                                j2 = kc - 4 * qt
                                if 0 <= j2 <= 3:
                                    tk = None
                                    for j3 in range(j2):
                                        tk = p.run("vector", lambda E, kc=kc, j3=j3:
                                                   E.memset(
                                                       esl[:, kc * 512 + j3 * 128:
                                                           kc * 512 + (j3 + 1) * 128],
                                                       1.0))
                                    tk = p.run("vector", lambda E, kc=kc, j2=j2:
                                               E.copy_predicated(
                                                   esl[:, kc * 512 + j2 * 128:
                                                       kc * 512 + (j2 + 1) * 128],
                                                   mski[:, :], onest[:, :]))
                                    fix_tok[kc] = tk
                            if g >= 1:
                                emit_pv(2 * (g - 1), False)
                                emit_pv(2 * (g - 1) + 1, False)
                        emit_pv(nmat - 2, False)
                        emit_pv(nmat - 1, not has_tail)
                        if has_tail:
                            p.wait("tensor", "ve", ve_sfx)
                            sfx = srow1[0:1, (4 * qt + 3) * 65:(4 * qt + 3) * 65 + 65]
                            p.mm(pE[0:65, :], sfx, onesrow[0:1, 0:512],
                                 False, True, inc="pe")
                        peU = p.semc["pe"]
                        flush_norm()
                        p.wait("vector", "pe", peU)
                        p.wait("vector", "ac", zx_par[par])
                        ncp = p.run("vector", lambda E, nb=nbuf[par]: E.tensor_copy(
                            nb, pE[0:65, :]))
                        p.wait("scalar", "ve", ncp)
                        p.wait("scalar", "pe", zb_par[par])
                        p.run("scalar", lambda E, z=zrow[par], nb=nbuf[par]:
                              E.activation(z, nb[64:65, :], AF.Ln))
                        p.fence("scalar")
                        zx = p.run("scalar", lambda E, z=zrow[par]: E.activation(
                            z, z, AF.Exp, scale=-1.0))
                        zx_par[par] = zx
                        pend = {"par": par, "ncp": ncp, "zx": zx, "n": n, "qt": qt}
                        prev_fix = max(fix_tok.values()) if fix_tok else prev_fix
                        ui += 1
        flush_norm()

        xlf = xl.bitcast(F32)
        wob = xh
        xbuf = [xlf[:, 0:1024], xlf[:, 4096:5120]]
        gbuf = [xlf[:, 1024:2048], xlf[:, 5120:6144]]
        obuf = [xlf[:, 3072:4096], xlf[:, 6144:7168]]
        t1 = xlf[:, 2048:3072]
        wobv = wob.rearrange("p (u c d) -> p u c d", u=2, c=8)
        wgtbv = wgtb.rearrange("p (bb u c k) -> p bb u c k", bb=2, u=2, c=8)
        wohv = woh.rearrange("(c p) d -> p c d", p=128)
        wolv = wol.rearrange("(c p) d -> p c d", p=128)
        wghv = wgh_scr.rearrange("(c p) s -> p c s", p=128)
        wglv = wgl_scr.rearrange("(c p) s -> p c s", p=128)
        if "W" in DBG_PHASES:
            p.wait("sync", "pe", p.semc["pe"])
            p.wait("sync", "ve", p.semc["ve"])
            p.wait("sync", "ds", st_total)
            p.wait("sync", "ac", p.semc["ac"])
            d_bo = p.load(bos, t["bo_row"])
            p.wait("tensor", "dl", d_bo)
            p.load(wobv[:, 0, :, :], wohv)
            dwo = p.load(wobv[:, 1, :, :], wolv)
            dtok = {}
            pe_hist = {}
            ve_hist = {}
            st_hist = {}

            def w_loads(i):
                bb = i % 2
                p.wait("sync", "pe", pe_hist.get(i - 2, 0))
                p.wait("sync", "ve", ve_hist.get(i - 2, 0))
                p.wait("sync", "ds", st_hist.get(i - 2, 0))
                p.load(wgtbv[:, bb, 0, :, :], wghv[:, :, i * 128:(i + 1) * 128])
                p.load(wgtbv[:, bb, 1, :, :], wglv[:, :, i * 128:(i + 1) * 128])
                p.load(xbuf[bb], x_tm[b, i * 128:(i + 1) * 128, :])
                dtok[i] = p.load(gbuf[bb], g_scr[b, i * 128:(i + 1) * 128, :])

            w_loads(0)
        for tt2 in range(16 if "W" in DBG_PHASES else 0):
            bb = tt2 % 2
            if tt2 + 1 < 16:
                w_loads(tt2 + 1)
            p.wait("tensor", "dl", dtok[tt2])
            p.wait("tensor", "dl", dwo)
            p.barrier_psum("tensor")
            for c in range(8):
                for half in range(2):
                    for gu, wu in ((0, 0), (0, 1), (1, 0)):
                        p.mm(pD[:, half * 512:(half + 1) * 512],
                             wgtbv[:, bb, gu, c, :],
                             wobv[:, wu, c, half * 512:(half + 1) * 512],
                             c == 0 and gu == 0 and wu == 0, False)
            for half in range(2):
                p.mm(pD[:, half * 512:(half + 1) * 512], onesrow[0:1, 0:128],
                     bos[0:1, half * 512:(half + 1) * 512], False, True,
                     inc="pe" if half == 1 else None)
            peS2 = p.semc["pe"]
            pe_hist[tt2] = peS2
            p.wait("vector", "pe", peS2)
            p.wait("vector", "dl", dtok[tt2])
            p.wait("vector", "ds", st_hist.get(tt2 - 2, 0))
            p.run("vector", lambda E, bb=bb: E.tensor_tensor(
                t1, pD[:, :], gbuf[bb], ALU.mult))
            p.fence("vector")
            veO = p.run("vector", lambda E, bb=bb: E.tensor_tensor(
                obuf[bb], t1, xbuf[bb], ALU.add))
            ve_hist[tt2] = veO
            p.wait("gpsimd", "ve", veO)
            st_total = p.store(out[b, tt2 * 128:(tt2 + 1) * 128, :], obuf[bb])
            st_hist[tt2] = st_total

    p.wait("gpsimd", "ds", p.semc["ds"])


def kernel(**inputs):
    x = np.asarray(inputs["L0_attn_residual"], dtype=np.float32)
    Wk = np.asarray(inputs["L0_attn_W_key"], dtype=np.float32)
    Wq = np.asarray(inputs["L0_attn_W_query"], dtype=np.float32)
    Wv = np.asarray(inputs["L0_attn_W_values"], dtype=np.float32)
    Wo = np.asarray(inputs["L0_attn_W_output"], dtype=np.float32)
    Bk = np.asarray(inputs["L0_attn_Bias_key"], dtype=np.float32)
    Bq = np.asarray(inputs["L0_attn_Bias_query"], dtype=np.float32)
    Bv = np.asarray(inputs["L0_attn_Bias_values"], dtype=np.float32)
    Bo = np.asarray(inputs["L0_attn_Bias_output"], dtype=np.float32)
    W1 = np.asarray(inputs["L0_ffn_W1"], dtype=np.float32)
    b1 = np.asarray(inputs["L0_ffn_b1"], dtype=np.float32)
    W2 = np.asarray(inputs["L0_ffn_W2"], dtype=np.float32)
    b2 = np.asarray(inputs["L0_ffn_b2"], dtype=np.float32)

    wk_t = np.ascontiguousarray(Wk.transpose(1, 0, 2).reshape(D, NHEAD * H))
    wq_t = np.ascontiguousarray(Wq.transpose(1, 0, 2).reshape(D, NHEAD * H))
    wv_t = np.ascontiguousarray(Wv.transpose(1, 0, 2).reshape(D, NHEAD * H))
    wout = np.ascontiguousarray(Wo.reshape(NHEAD * H, D))
    bk_m = np.ascontiguousarray(Bk.reshape(8, 128).T)
    bq_m = np.ascontiguousarray(Bq.reshape(8, 128).T)
    bv_row = np.ascontiguousarray(Bv.reshape(1, NHEAD * H))
    bo_row = np.ascontiguousarray(Bo.reshape(1, D))
    b1_m = np.ascontiguousarray(b1.reshape(32, 128).T)
    b2_row = np.ascontiguousarray(b2.reshape(1, D))
    identc = np.eye(128, dtype=np.float32)
    maskinv = (np.arange(128)[None, :] < np.arange(128)[:, None]).astype(np.uint8)

    nc = build_nc()
    in_maps = []
    for c in range(N_CORES):
        xs = x[B_CORE * c: B_CORE * (c + 1)]
        in_maps.append({
            "x_tm": np.ascontiguousarray(xs),
            "x_fm": np.ascontiguousarray(xs.transpose(0, 2, 1)),
            "wk_t": wk_t, "wq_t": wq_t, "wv_t": wv_t, "wout": wout,
            "bk_m": bk_m, "bq_m": bq_m, "bv_row": bv_row, "bo_row": bo_row,
            "w1": W1, "b1_m": b1_m, "w2": W2, "b2_row": b2_row,
            "identc": identc, "maskinv": maskinv,
        })
    res = run_bass_kernel_spmd(nc, in_maps, core_ids=list(range(N_CORES)))
    return np.concatenate([r["out"] for r in res.results], axis=0)


# version 15
# speedup vs baseline: 1.5508x; 1.0255x over previous
import numpy as np
import concourse.bass as bass
import concourse.mybir as mybir
from concourse.bass_utils import run_bass_kernel_spmd
from contextlib import ExitStack

F32 = mybir.dt.float32
BF16 = mybir.dt.bfloat16
U8 = mybir.dt.uint8
AF = mybir.ActivationFunctionType
ALU = mybir.AluOpType
AX = mybir.AxisListType.X

B_CORE = 4
S = 2048
D = 1024
NHEAD = 16
H = 64
DFF = 4096
MAGIC = 12582912.0
N_CORES = 8

ENGS = ("sync", "tensor", "vector", "scalar", "gpsimd")

DBG_PHASES = "QLFAW"
DBG_BATCHES = B_CORE
DBG_TAP = False


class Prog:
    def __init__(self, nc):
        self.nc = nc
        self.ops = {e: [] for e in ENGS}
        self.semh = {}
        self.semc = {}
        self.last_wait = {}

    def sem(self, name, handle):
        self.semh[name] = handle
        self.semc[name] = 0

    def wait(self, eng, sem, val):
        if val is None or val <= 0:
            return
        key = (eng, sem)
        if self.last_wait.get(key, -1) >= val:
            return
        self.last_wait[key] = val
        h = self.semh[sem]
        self.ops[eng].append(lambda E, h=h, v=val: E.wait_ge(h, v))

    OWN = {"tensor": "pe", "vector": "ve", "scalar": "ac"}

    def fence(self, eng):
        own = self.OWN[eng]
        self.wait(eng, own, self.semc[own])

    def run(self, eng, fn, inc=None, n=1):
        if inc is None and eng in ("vector", "scalar"):
            inc = self.OWN[eng]
        tok = None
        if inc is not None:
            self.semc[inc] += n
            tok = self.semc[inc]
            h = self.semh[inc]

            def cl(E, fn=fn, h=h, n=n):
                fn(E).then_inc(h, n)
        else:
            def cl(E, fn=fn):
                fn(E)
        self.ops[eng].append(cl)
        return tok

    def load(self, out_ap, in_ap):
        return self.run("sync", lambda E, o=out_ap, i=in_ap: E.dma_start(out=o, in_=i),
                        inc="dl", n=16)

    def store(self, out_ap, in_ap):
        return self.run("gpsimd", lambda E, o=out_ap, i=in_ap: E.dma_start(out=o, in_=i),
                        inc="ds", n=16)

    def mm(self, out, lhsT, rhs, start, stop, inc=None):
        return self.run("tensor",
                        lambda E, o=out, l=lhsT, r=rhs, s=start, p=stop:
                        E.matmul(o, l, r, start=s, stop=p), inc=inc)

    def barrier_psum(self, *engs):
        for e in engs:
            for sem in ("pe", "ve", "ac"):
                if sem == self.OWN.get(e):
                    continue
                self.wait(e, sem, self.semc[sem])


def build_nc():
    nc = bass.Bass("TRN2", target_bir_lowering=False)

    x_tm = nc.dram_tensor("x_tm", [B_CORE, S, D], F32, kind="ExternalInput").ap()
    x_fm = nc.dram_tensor("x_fm", [B_CORE, D, S], F32, kind="ExternalInput").ap()
    wk_t = nc.dram_tensor("wk_t", [D, NHEAD * H], F32, kind="ExternalInput").ap()
    wq_t = nc.dram_tensor("wq_t", [D, NHEAD * H], F32, kind="ExternalInput").ap()
    wv_t = nc.dram_tensor("wv_t", [D, NHEAD * H], F32, kind="ExternalInput").ap()
    wout = nc.dram_tensor("wout", [NHEAD * H, D], F32, kind="ExternalInput").ap()
    bk_m = nc.dram_tensor("bk_m", [128, 8], F32, kind="ExternalInput").ap()
    bq_m = nc.dram_tensor("bq_m", [128, 8], F32, kind="ExternalInput").ap()
    bv_row = nc.dram_tensor("bv_row", [1, NHEAD * H], F32, kind="ExternalInput").ap()
    bo_row = nc.dram_tensor("bo_row", [1, D], F32, kind="ExternalInput").ap()
    w1 = nc.dram_tensor("w1", [DFF, D], F32, kind="ExternalInput").ap()
    b1_m = nc.dram_tensor("b1_m", [128, 32], F32, kind="ExternalInput").ap()
    w2 = nc.dram_tensor("w2", [D, DFF], F32, kind="ExternalInput").ap()
    b2_row = nc.dram_tensor("b2_row", [1, D], F32, kind="ExternalInput").ap()
    identc = nc.dram_tensor("identc", [128, 128], F32, kind="ExternalInput").ap()
    maskinv = nc.dram_tensor("maskinv", [128, 128], U8, kind="ExternalInput").ap()
    out = nc.dram_tensor("out", [B_CORE, S, D], F32, kind="ExternalOutput").ap()

    kw = {"kind": "ExternalOutput"} if DBG_TAP else {}
    wq1t = nc.dram_tensor("wq1t", [D, DFF], BF16, **kw).ap()
    wq2t = nc.dram_tensor("wq2t", [DFF, D], BF16, **kw).ap()
    g_scr = nc.dram_tensor("g_scr", [B_CORE, S, D], F32, **kw).ap()
    wgh_scr = nc.dram_tensor("wgh_scr", [NHEAD * H, S], BF16, **kw).ap()
    awh = nc.dram_tensor("awh", [3, D, NHEAD * H], BF16).ap()
    woh = nc.dram_tensor("woh", [NHEAD * H, D], BF16).ap()

    with ExitStack() as ctx:
        E_ = ctx.enter_context
        xh = E_(nc.sbuf_tensor("xh", [128, 16384], BF16)).ap()
        xl = E_(nc.sbuf_tensor("xl", [128, 16384], BF16)).ap()
        r2 = E_(nc.sbuf_tensor("r2", [128, 8192], F32)).ap()
        kqK = E_(nc.sbuf_tensor("kqK", [128, 4096], F32)).ap()
        kqQ = E_(nc.sbuf_tensor("kqQ", [128, 4096], F32)).ap()
        vsl = E_(nc.sbuf_tensor("vsl", [128, 16 * 4 * 65], F32)).ap()
        srow1 = E_(nc.sbuf_tensor("srow1", [1, 16 * 65], F32)).ap()
        crow1 = E_(nc.sbuf_tensor("crow1", [1, 16 * 65], F32)).ap()
        rows = E_(nc.sbuf_tensor("rows", [1, 2048], F32)).ap()
        wkb = E_(nc.sbuf_tensor("wkb", [128, 2048], BF16)).ap()
        wqb = E_(nc.sbuf_tensor("wqb", [128, 2048], BF16)).ap()
        wvb = E_(nc.sbuf_tensor("wvb", [128, 4096], BF16)).ap()
        wq1b = E_(nc.sbuf_tensor("wq1b", [128, 4096], BF16)).ap()
        wq2b = E_(nc.sbuf_tensor("wq2b", [128, 3072], BF16)).ap()
        wgtb = E_(nc.sbuf_tensor("wgtb", [128, 4096], BF16)).ap()
        ident = E_(nc.sbuf_tensor("ident", [128, 128], F32)).ap()
        mski = E_(nc.sbuf_tensor("mski", [128, 128], U8)).ap()
        onest = E_(nc.sbuf_tensor("onest", [128, 128], F32)).ap()
        onescol = E_(nc.sbuf_tensor("onescol", [128, 1], F32)).ap()
        onesrow = E_(nc.sbuf_tensor("onesrow", [1, 512], F32)).ap()
        b1s = E_(nc.sbuf_tensor("b1s", [128, 32], F32)).ap()
        bks = E_(nc.sbuf_tensor("bks", [128, 8], F32)).ap()
        bqs = E_(nc.sbuf_tensor("bqs", [128, 8], F32)).ap()
        colt = E_(nc.sbuf_tensor("colt", [128, 8], F32)).ap()

        pA = E_(nc.psum_tensor("pA", [128, 512], F32)).ap()
        pB = E_(nc.psum_tensor("pB", [128, 512], F32)).ap()
        pC = E_(nc.psum_tensor("pC", [128, 1024], F32)).ap()
        pD = E_(nc.psum_tensor("pD", [128, 1024], F32)).ap()
        pE = E_(nc.psum_tensor("pE", [128, 512], F32)).ap()
        pF = E_(nc.psum_tensor("pF", [128, 512], F32)).ap()

        sem_handles = {}
        for name in ("dl", "ds", "pe", "ve", "ac"):
            sem_handles[name] = E_(nc.semaphore(name))

        block = E_(nc.Block())
        p = Prog(nc)
        for name, h in sem_handles.items():
            p.sem(name, h)

        build_program(p, locals())

        @block.sync
        def _(sync):
            for f in p.ops["sync"]:
                f(sync)

        @block.tensor
        def _(tensor):
            for f in p.ops["tensor"]:
                f(tensor)

        @block.vector
        def _(vector):
            for f in p.ops["vector"]:
                f(vector)

        @block.scalar
        def _(scalar):
            for f in p.ops["scalar"]:
                f(scalar)

        @block.gpsimd
        def _(gpsimd):
            for f in p.ops["gpsimd"]:
                f(gpsimd)

    return nc


def build_program(p, t):
    xh, xl, r2 = t["xh"], t["xl"], t["r2"]
    kqK, kqQ, vsl = t["kqK"], t["kqQ"], t["vsl"]
    srow1, crow1, rows = t["srow1"], t["crow1"], t["rows"]
    wkb, wqb, wvb = t["wkb"], t["wqb"], t["wvb"]
    wq1b, wq2b, wgtb = t["wq1b"], t["wq2b"], t["wgtb"]
    gsb = t["kqK"][:, 0:1024]
    hfp = t["kqK"][:, 1024:1280]
    stage = wq1b[:, 3072:3328]
    stg2 = wgtb
    uh = wq1b[0:64, 0:1024]
    ident, mski, onest = t["ident"], t["mski"], t["onest"]
    onescol, onesrow = t["onescol"], t["onesrow"]
    b1s, bks, bqs, colt = t["b1s"], t["bks"], t["bqs"], t["colt"]
    bvs = t["rows"][0:1, 1024:2048]
    bos = t["rows"][0:1, 1024:2048]
    b2s = t["rows"][0:1, 1024:2048]
    pA, pB, pC, pD, pE, pF = t["pA"], t["pB"], t["pC"], t["pD"], t["pE"], t["pF"]
    x_tm, x_fm = t["x_tm"], t["x_fm"]
    wk_t, wq_t, wv_t, wout = t["wk_t"], t["wq_t"], t["wv_t"], t["wout"]
    w1, b1_m, w2 = t["w1"], t["b1_m"], t["w2"]
    identc, maskinv, out = t["identc"], t["maskinv"], t["out"]
    wq1t, wq2t, g_scr = t["wq1t"], t["wq2t"], t["g_scr"]
    wgh_scr = t["wgh_scr"]
    awh, woh = t["awh"], t["woh"]
    vslv = vsl.rearrange("p (c h e) -> p c h e", c=16, h=4)

    p.load(ident[:, :], identc)
    p.load(mski[:, :], maskinv)
    p.load(b1s[:, :], t["b1_m"])
    p.load(bks[:, :], t["bk_m"])
    tok_consts = p.load(bqs[:, :], t["bq_m"])

    p.run("vector", lambda E: E.memset(onest[:, :], 1.0))
    p.run("vector", lambda E: E.memset(onescol[:, :], 1.0))
    p.run("vector", lambda E: E.memset(onesrow[:, :], 1.0))
    p.run("vector", lambda E: E.memset(vslv[:, :, :, 64:65], 1.0))
    p.fence("vector")

    f32bufs = [r2[:, 0:1024], r2[:, 1024:2048]]
    st_hist_aw = [None, None]
    dl_hist_aw = [None, None]
    st_aw = None
    aw_i = 0

    def split_stage(src_ap, dst_h):
        nonlocal st_aw, aw_i
        par = aw_i % 2
        aw_i += 1
        f32buf = f32bufs[par]
        sgh = stg2[:, par * 1024:(par + 1) * 1024]
        p.wait("sync", "ds", st_hist_aw[par])
        p.wait("sync", "ve", p.semc["ve"])
        dt_ = p.load(f32buf, src_ap)
        dl_hist_aw[par] = dt_
        p.wait("vector", "dl", dt_)
        p.wait("vector", "ds", st_hist_aw[par])
        vv = p.run("vector", lambda E, sgh=sgh, f32buf=f32buf:
                   E.tensor_copy(sgh, f32buf))
        p.wait("gpsimd", "ve", vv)
        st_aw = p.store(dst_h, sgh)
        st_hist_aw[par] = st_aw

    for mi, wsrc in enumerate((wk_t, wq_t, wv_t)):
        for i in range(8):
            split_stage(wsrc[i * 128:(i + 1) * 128, :],
                        awh[mi, i * 128:(i + 1) * 128, :])
    for i in range(8):
        split_stage(wout[i * 128:(i + 1) * 128, :],
                    woh[i * 128:(i + 1) * 128, :])

    def quantize(w_dram, wq_dram, n_rows, acc_i, st_prev):
        n_cols = w_dram.shape[1]
        ntile = n_rows // 128
        ncol_t = n_cols // 128
        wbuf = [r2[:, 0:n_cols], r2[:, n_cols:2 * n_cols]]
        acc = colt[:, acc_i:acc_i + 1]
        sc = colt[:, acc_i + 1:acc_i + 2]
        red = colt[:, 6:7]
        p.run("vector", lambda E, a=acc: E.memset(a, 0.0))
        p.fence("vector")
        ve_toks = [None] * ntile
        for i in range(ntile):
            b = wbuf[i % 2]
            if i >= 2:
                p.wait("sync", "ve", ve_toks[i - 2])
            else:
                p.wait("sync", "ve", p.semc["ve"])
                p.wait("sync", "pe", p.semc["pe"])
                p.wait("sync", "ac", p.semc["ac"])
            dt_ = p.load(b, w_dram[i * 128:(i + 1) * 128, :])
            p.wait("scalar", "dl", dt_)
            aca = p.run("scalar", lambda E, b=b: E.activation(b, b, AF.Abs))
            p.wait("vector", "ac", aca)
            p.run("vector", lambda E, b=b, r=red: E.tensor_reduce(
                r, b, AX, ALU.add))
            p.fence("vector")
            ve_toks[i] = p.run("vector", lambda E, a=acc, r=red: E.tensor_tensor(
                a, a, r, ALU.add))
            p.fence("vector")
        p.wait("tensor", "ve", ve_toks[-1])
        p.wait("tensor", "dl", tok_consts)
        pe1 = p.mm(pF[0:1, 0:1], acc, onescol[:, :], True, True, inc="pe")
        p.wait("vector", "pe", pe1)
        sc1 = rows[0:1, 0:1]
        p.run("vector", lambda E, s=sc1: E.tensor_scalar(
            s, pF[0:1, 0:1], 1.0 / (n_rows * n_cols), 1e-5, ALU.mult, ALU.add))
        p.fence("vector")
        ve1 = p.run("vector", lambda E, s=sc1: E.reciprocal(s, s))
        p.wait("tensor", "ve", ve1)
        pe2 = p.mm(pF[0:128, 0:1], onesrow[0:1, 0:128], sc1, True, True, inc="pe")
        p.wait("vector", "pe", pe2)
        ve2 = p.run("vector", lambda E, s=sc: E.tensor_copy(s, pF[0:128, 0:1]))
        pe_cons = [None] * ntile
        st_toks = [st_prev, st_prev]
        last_store = st_prev
        vet_prev = ve2
        for i in range(ntile):
            b = wbuf[i % 2]
            if i == 0:
                p.wait("sync", "ve", ve2)
            else:
                p.wait("sync", "pe", pe_cons[i - 1])
            dt_ = p.load(b, w_dram[i * 128:(i + 1) * 128, :])
            p.wait("vector", "dl", dt_)
            p.run("vector", lambda E, b=b, s=sc: E.tensor_scalar(
                b, b, s, MAGIC, ALU.mult, ALU.add))
            p.fence("vector")
            p.run("vector", lambda E, b=b: E.tensor_scalar(
                b, b, MAGIC, -1.0, ALU.subtract, ALU.max))
            p.fence("vector")
            veq = p.run("vector", lambda E, b=b: E.tensor_scalar(
                b, b, 1.0, None, ALU.min))
            p.wait("tensor", "ve", veq)
            for j in range(ncol_t):
                sg = stage[:, (j % 2) * 128:(j % 2) * 128 + 128]
                p.wait("tensor", "ve", vet_prev)
                pet = p.run("tensor", lambda E, src=b[:, j * 128:(j + 1) * 128]:
                            E.transpose(pE[0:128, 0:128], src, ident[:, :]), inc="pe")
                p.wait("vector", "ds", st_toks[j % 2])
                p.wait("vector", "pe", pet)
                vet = p.run("vector", lambda E, sg=sg: E.tensor_copy(
                    sg, pE[0:128, 0:128]))
                vet_prev = vet
                p.wait("gpsimd", "ve", vet)
                stt = p.store(wq_dram[j * 128:(j + 1) * 128, i * 128:(i + 1) * 128], sg)
                st_toks[j % 2] = stt
                last_store = stt
            pe_cons[i] = p.semc["pe"]
        return last_store

    st_total = st_aw
    if "Q" in DBG_PHASES:
        st_total = quantize(w1, wq1t, DFF, 0, st_total)
        st_total = quantize(w2, wq2t, D, 2, st_total)
    q_done = st_total
    ve_xn = 0

    for b in range(DBG_BATCHES):
        xf = r2[:, 0:4096]
        sq = [r2[:, 4096:4608], r2[:, 4608:5120]]
        mb = r2[:, 5120:5632]
        rb = r2[:, 5632:6144]
        xfv = xf.rearrange("p (dc t) -> p dc t", dc=8)
        x_fmv = x_fm[b].rearrange("(dc p) t -> p dc t", p=128)
        m_row = rows[0:1, 0:512]
        var_row = rows[0:1, 512:1024]
        rstd_row = rows[0:1, 1024:1536]
        for tt in range(4 if "L" in DBG_PHASES else 0):
            p.wait("sync", "pe", p.semc["pe"])
            p.wait("sync", "ve", p.semc["ve"])
            p.wait("sync", "ac", p.semc["ac"])
            dt_ = p.load(xfv, x_fmv[:, :, tt * 512:(tt + 1) * 512])
            p.barrier_psum("tensor")
            p.wait("tensor", "dl", dt_)
            p.wait("scalar", "dl", dt_)
            for dc in range(8):
                p.mm(pA[0:1, 0:512], onescol[:, :], xfv[:, dc, :], dc == 0, dc == 7,
                     inc="pe" if dc == 7 else None)
            sq_mm = [None, None]
            for dc in range(8):
                p.wait("scalar", "pe", sq_mm[dc % 2])
                act = p.run("scalar", lambda E, dc=dc, s=sq[dc % 2]: E.activation(
                    s, xfv[:, dc, :], AF.Square))
                p.wait("tensor", "ac", act)
                sq_mm[dc % 2] = p.mm(pB[0:1, 0:512], onescol[:, :], sq[dc % 2],
                                     dc == 0, dc == 7, inc="pe")
            pe_stats = p.semc["pe"]
            p.wait("vector", "pe", pe_stats)
            p.run("vector", lambda E: E.tensor_scalar(
                m_row, pA[0:1, 0:512], 1.0 / D, None, ALU.mult))
            p.fence("vector")
            p.run("vector", lambda E: E.tensor_tensor(
                var_row, m_row, m_row, ALU.mult))
            p.fence("vector")
            p.run("vector", lambda E: E.scalar_tensor_tensor(
                var_row, pB[0:1, 0:512], 1.0 / D, var_row, ALU.mult, ALU.subtract))
            p.fence("vector")
            ve_v = p.run("vector", lambda E: E.tensor_scalar(
                var_row, var_row, 1e-5, None, ALU.add))
            p.wait("scalar", "ve", ve_v)
            p.run("scalar", lambda E: E.activation(rstd_row, var_row, AF.Ln))
            p.fence("scalar")
            ac_r = p.run("scalar", lambda E: E.activation(
                rstd_row, rstd_row, AF.Exp, scale=-0.5))
            p.wait("tensor", "ve", ve_v)
            p.wait("tensor", "ac", ac_r)
            p.mm(pC[:, 0:512], onesrow[0:1, 0:128], m_row, True, True)
            pe_bc = p.mm(pC[:, 512:1024], onesrow[0:1, 0:128], rstd_row, True, True,
                         inc="pe")
            p.wait("vector", "pe", pe_bc)
            p.run("vector", lambda E: E.tensor_copy(mb, pC[:, 0:512]))
            p.run("vector", lambda E: E.tensor_copy(rb, pC[:, 512:1024]))
            p.wait("vector", "pe", p.semc["pe"])
            p.wait("vector", "ds", st_total if b > 0 else 0)
            p.fence("vector")
            for dc in range(8):
                p.run("vector", lambda E, i=xfv[:, dc, :]: E.scalar_tensor_tensor(
                    i, i, 1.0, mb, ALU.mult, ALU.subtract))
            p.fence("vector")
            for dc in range(8):
                p.run("vector", lambda E, i=xfv[:, dc, :]: E.tensor_tensor(
                    i, i, rb, ALU.mult))
            p.fence("vector")
            for dc in range(8):
                o = xh[:, dc * 2048 + tt * 512: dc * 2048 + (tt + 1) * 512]
                ve_xn = p.run("vector", lambda E, o=o, i=xfv[:, dc, :]:
                              E.tensor_copy(o, i))

        r2bf = r2.bitcast(BF16)
        hh = r2bf[:, 0:8192]
        wq1v = wq1t.rearrange("(dc p) f -> p dc f", p=128)
        wq1bv = wq1b.rearrange("p (u dc f) -> p u dc f", u=4, dc=8)
        wq2bv = wq2b.rearrange("p (u f) -> p u f", u=3)
        if "F" in DBG_PHASES:
            p.wait("sync", "ds", q_done)
            p.wait("sync", "ac", p.semc["ac"])
            p.wait("sync", "pe", p.semc["pe"])
            d_b2 = p.load(b2s, t["b2_row"])
            p.wait("tensor", "dl", d_b2)
        d_toks = {}
        pe_wq = {}
        hdone = {}
        rl = {}

        def emit_ffn2(fc, u, gi):
            ac_h = hdone[fc]
            p.wait("tensor", "ac", ac_h)
            for sub in range(2):
                gp = pC if sub == 0 else pD
                for half in range(2):
                    p.mm(gp[:, half * 512:(half + 1) * 512],
                         hh[:, fc * 256 + sub * 128:
                            fc * 256 + sub * 128 + 128],
                         wq2bv[:, u, half * 512:(half + 1) * 512],
                         fc == 0, False,
                         inc="pe" if (sub == 1 and half == 1) else None)
            pe_wq[gi] = p.semc["pe"]

        for ts in range(8 if "F" in DBG_PHASES else 0):
            t0 = ts * 256
            for fc in range(32):
                gi = ts * 32 + fc
                u = gi % 3
                if gi == 0:
                    p.wait("sync", "pe", p.semc["pe"])
                    p.wait("sync", "ac", p.semc["ac"])
                    p.wait("sync", "ve", p.semc["ve"])
                    p.wait("sync", "ds", st_total)
                if gi >= 3:
                    p.wait("sync", "pe", pe_wq[gi - 3])
                p.load(wq1bv[:, u, :, :], wq1v[:, :, fc * 128:(fc + 1) * 128])
                d_toks[gi] = p.load(wq2bv[:, u, :], wq2t[fc * 128:(fc + 1) * 128, :])
                if gi == 0:
                    p.wait("tensor", "ve", ve_xn)
                    p.barrier_psum("tensor")
                p.wait("tensor", "dl", d_toks[gi])
                hp = pA if fc % 2 == 0 else pB
                p.wait("tensor", "ac", rl.get(gi - 2, 0))
                for dc in range(8):
                    p.mm(hp[:, 0:256], wq1bv[:, u, dc, :],
                         xh[:, dc * 2048 + t0: dc * 2048 + t0 + 256],
                         dc == 0, dc == 7,
                         inc="pe" if dc == 7 else None)
                pe_h = p.semc["pe"]
                p.wait("scalar", "pe", pe_h)
                rl[gi] = p.run("scalar", lambda E, i=hp[:, 0:256], fc=fc:
                               E.activation(hfp[:, :], i, AF.Relu,
                                            bias=b1s[:, fc:fc + 1]))
                p.fence("scalar")
                ac_h = p.run("scalar", lambda E, o=hh[:, fc * 256:(fc + 1) * 256]:
                             E.activation(o, hfp[:, :], AF.Copy))
                hdone[fc] = ac_h
                if fc > 0:
                    emit_ffn2(fc - 1, (gi - 1) % 3, gi - 1)
            emit_ffn2(31, (ts * 32 + 31) % 3, ts * 32 + 31)
            for sub in range(2):
                gp = pC if sub == 0 else pD
                for half in range(2):
                    p.mm(gp[:, half * 512:(half + 1) * 512], onesrow[0:1, 0:128],
                         b2s[0:1, half * 512:(half + 1) * 512], False, True,
                         inc="pe" if (sub == 1 and half == 1) else None)
            pe_g = p.semc["pe"]
            pe_wq[ts * 32 + 31] = pe_g
            p.wait("scalar", "pe", pe_g)
            for sub in range(2):
                gp = pC if sub == 0 else pD
                p.wait("scalar", "ds", st_total)
                ac_g = p.run("scalar", lambda E, gp=gp: E.activation(
                    gsb[:, :], gp[:, :], AF.Silu))
                p.wait("gpsimd", "ac", ac_g)
                st_total = p.store(g_scr[b, t0 + sub * 128: t0 + sub * 128 + 128, :],
                                   gsb[:, :])

        esl = r2
        bank_ac = {0: p.semc["ac"], 1: p.semc["ac"]}
        ustg_ds = [st_total, st_total]
        nbufs = wq2b.bitcast(F32)
        nbuf = [nbufs[0:65, 0:512], nbufs[0:65, 512:1024]]
        zrow = [rows[0:1, 0:512], rows[0:1, 512:1024]]
        pend = None
        ui = 0
        prev_fix = p.semc["ve"]
        zx_par = {0: 0, 1: 0}
        zb_par = {0: 0, 1: 0}
        veW_prev = 0

        def flush_norm():
            nonlocal pend, st_total, veW_prev
            if pend is None:
                return
            par0, n0, qt0 = pend["par"], pend["n"], pend["qt"]
            p.wait("tensor", "ac", pend["zx"])
            p.wait("tensor", "ve", veW_prev)
            zb = p.mm(pA[0:64, :], onesrow[0:1, 0:64], zrow[par0], True, True,
                      inc="pe")
            zb_par[par0] = zb
            nb = nbuf[par0]
            p.wait("vector", "pe", zb)
            p.wait("vector", "ds", ustg_ds[par0])
            p.fence("vector")
            uhs = uh[:, par0 * 512:(par0 + 1) * 512]
            p.run("vector", lambda E, nb=nb: E.tensor_tensor(
                nb[0:64, :], nb[0:64, :], pA[0:64, :], ALU.mult))
            p.fence("vector")
            veW = p.run("vector", lambda E, o=uhs, nb=nb: E.tensor_copy(
                o, nb[0:64, :]))
            veW_prev = veW
            p.wait("gpsimd", "ve", veW)
            st_total = p.store(
                wgh_scr[64 * n0: 64 * n0 + 64, qt0 * 512:(qt0 + 1) * 512], uhs)
            ustg_ds[par0] = st_total
            pend = None
        awhv = awh.rearrange("m (dc p) n -> m p dc n", p=128)
        wkbv = wkb.rearrange("p (u dc n) -> p u dc n", u=2, dc=8)
        wqbv = wqb.rearrange("p (u dc n) -> p u dc n", u=2, dc=8)
        wvbv = wvb.rearrange("p (u dc n) -> p u dc n", u=2, dc=8)
        for qd in range(4 if "A" in DBG_PHASES else 0):
            if qd == 0:
                p.wait("sync", "ac", p.semc["ac"])
                p.wait("sync", "ve", p.semc["ve"])
                p.wait("sync", "pe", p.semc["pe"])
                d_bv = p.load(bvs, t["bv_row"])
                p.wait("tensor", "dl", d_bv)
            dv = p.load(wvbv[:, 0, :, :], awhv[2, :, :, qd * 256:(qd + 1) * 256])
            p.wait("tensor", "dl", dv)
            p.wait("tensor", "ve", p.semc["ve"])
            p.barrier_psum("tensor")
            for kc in range(16):
                for dc in range(8):
                    p.mm(pF[:, 0:256],
                         xh[:, dc * 2048 + kc * 128: dc * 2048 + (kc + 1) * 128],
                         wvbv[:, 0, dc, :],
                         dc == 0, False)
                peV = p.mm(pF[:, 0:256], onesrow[0:1, 0:128],
                           bvs[0:1, qd * 256:(qd + 1) * 256], False, True, inc="pe")
                p.wait("vector", "pe", peV)
                veV = p.run("vector", lambda E, kc=kc: E.tensor_copy(
                    vslv[:, kc, :, 0:64],
                    pF[:, 0:256].rearrange("p (h e) -> p h e", h=4)))
                p.wait("tensor", "ve", veV)
            kqKb = kqK.bitcast(BF16)
            kqQb = kqQ.bitcast(BF16)
            ktmp = wgtb.bitcast(F32)[:, 0:512]
            qtmp = wgtb.bitcast(F32)[:, 512:1024]
            klt = wq1b[:, 1024:1536]
            for pp in range(2):
                gp_ = 2 * qd + pp
                p.wait("sync", "pe", p.semc["pe"])
                p.load(wkbv[:, 0, :, :], awhv[0, :, :, gp_ * 128:(gp_ + 1) * 128])
                dq = p.load(wqbv[:, 0, :, :], awhv[1, :, :, gp_ * 128:(gp_ + 1) * 128])
                p.wait("tensor", "dl", dq)
                p.wait("tensor", "ve", p.semc["ve"])
                p.wait("tensor", "ds", p.semc["ds"])
                p.barrier_psum("tensor")
                pr_ds = 0
                for tt in range(4):
                    ttsl = slice(tt * 512, (tt + 1) * 512)
                    for ps, wb in ((pA, wkbv), (pB, wqbv)):
                        for dc in range(8):
                            p.mm(ps[:, :], wb[:, 0, dc, :],
                                 xh[:, dc * 2048 + tt * 512:
                                    dc * 2048 + (tt + 1) * 512],
                                 dc == 0, dc == 7,
                                 inc="pe" if dc == 7 else None)
                    peQ = p.semc["pe"]
                    p.wait("vector", "pe", peQ)
                    p.wait("vector", "ds", pr_ds)
                    p.run("vector", lambda E, g=gp_: E.tensor_scalar(
                        ktmp, pA[:, :], bks[:, g:g + 1], None, ALU.add))
                    ve_ts = p.run("vector", lambda E, g=gp_: E.tensor_scalar(
                        qtmp, pB[:, :], bqs[:, g:g + 1], None, ALU.add))
                    p.fence("vector")
                    for hs in range(2):
                        R = slice(64 * hs, 64 * hs + 64)
                        kP_h = kqKb[:, hs * 2048:hs * 2048 + 2048]
                        qhh_h = kqQb[:, hs * 2048:hs * 2048 + 2048]
                        p.run("vector", lambda E, o=kP_h[R, ttsl], i=ktmp[R, :]:
                              E.tensor_copy(o, i))
                        p.run("vector", lambda E, o=qhh_h[R, ttsl], i=qtmp[R, :]:
                              E.tensor_copy(o, i))
                    p.fence("vector")
                    for hs in range(2):
                        R = slice(64 * hs, 64 * hs + 64)
                        kP_h = kqKb[:, hs * 2048:hs * 2048 + 2048]
                        p.run("vector", lambda E, o=klt[R, :], i=ktmp[R, :],
                              h=kP_h[R, ttsl]: E.tensor_tensor(o, i, h, ALU.subtract))
                    ve_ev = p.run("vector", lambda E: E.engine_nop())
                    p.wait("gpsimd", "ve", ve_ev)
                    for hs in range(2):
                        R = slice(64 * hs, 64 * hs + 64)
                        Rp = slice(64 * (1 - hs), 64 * (1 - hs) + 64)
                        kP_h = kqKb[:, hs * 2048:hs * 2048 + 2048]
                        qhh_h = kqQb[:, hs * 2048:hs * 2048 + 2048]
                        p.store(kP_h[Rp, ttsl], klt[R, :])
                        pr_ds = p.store(qhh_h[Rp, ttsl], qhh_h[R, ttsl])
                    p.wait("tensor", "ve", ve_ts)
                pair_ds = pr_ds
                for hs in range(2):
                    j = 2 * pp + hs
                    n = 4 * qd + j
                    kP = kqKb[:, hs * 2048:hs * 2048 + 2048]
                    qhh = kqQb[:, hs * 2048:hs * 2048 + 2048]
                    p.wait("tensor", "ds", pair_ds)
                    p.wait("tensor", "ac", bank_ac[0])
                    p.wait("tensor", "ve", p.semc["ve"])
                    for cb in range(4):
                        for u in range(4):
                            kc = 4 * cb + u
                            p.mm(pC[0:1, u * 65: u * 65 + 65], onescol[:, :],
                                 vslv[:, kc, j, :], True, True,
                                 inc="pe" if u == 3 else None)
                        peC = p.semc["pe"]
                        p.wait("vector", "pe", peC)
                        vcp = p.run("vector", lambda E, cb=cb: E.tensor_copy(
                            crow1[0:1, cb * 260:(cb + 1) * 260], pC[0:1, 0:260]))
                        p.wait("tensor", "ve", vcp)
                    def scp(dst_kc, src_kc):
                        p.fence("vector")
                        p.run("vector", lambda E, d=dst_kc, s=src_kc: E.tensor_copy(
                            srow1[0:1, d * 65:(d + 1) * 65],
                            srow1[0:1, s * 65:(s + 1) * 65]))

                    def sacc(dst_kc, src_kcs, copy_first):
                        first = copy_first
                        for kc in src_kcs:
                            p.fence("vector")
                            src = crow1[0:1, kc * 65:(kc + 1) * 65]
                            dsl = srow1[0:1, dst_kc * 65:(dst_kc + 1) * 65]
                            if first:
                                p.run("vector", lambda E, d=dsl, s=src: E.tensor_copy(d, s))
                                first = False
                            else:
                                p.run("vector", lambda E, d=dsl, s=src: E.tensor_tensor(
                                    d, d, s, ALU.add))
                    sacc(11, [12, 13, 14, 15], copy_first=True)
                    scp(7, 11)
                    sacc(7, [8, 9, 10, 11], copy_first=False)
                    scp(3, 7)
                    sacc(3, [4, 5, 6, 7], copy_first=False)
                    p.fence("vector")
                    ve_sfx = p.semc["ve"]
                    p.wait("tensor", "ve", vcp)
                    for qt in range(4):
                        nmat = 4 * qt + 4
                        has_tail = nmat < 16
                        par = ui % 2
                        fix_tok = {}
                        exp_tok = {}

                        def emit_pv(kc, last):
                            p.wait("tensor", "ac", exp_tok[kc // 2])
                            p.wait("tensor", "ve", fix_tok.get(kc, 0))
                            p.mm(pE[0:65, :], vslv[:, kc, j, :],
                                 esl[:, kc * 512:(kc + 1) * 512], kc == 0, last,
                                 inc="pe" if last else None)

                        for g in range(nmat // 2):
                            sp, bank = (pC, 0) if g % 2 == 0 else (pD, 1)
                            p.wait("tensor", "ac", bank_ac[bank])
                            if g == 0 and pend is not None:
                                p.wait("tensor", "ve", pend["ncp"])
                            for u in range(2):
                                kc = 2 * g + u
                                lhs = kP[:, kc * 128:(kc + 1) * 128]
                                p.mm(sp[:, u * 512:(u + 1) * 512], lhs,
                                     qhh[:, qt * 512:(qt + 1) * 512], True, True,
                                     inc="pe" if u == 1 else None)
                            p.wait("scalar", "pe", p.semc["pe"])
                            p.wait("scalar", "ve", prev_fix)
                            exp_tok[g] = p.run(
                                "scalar", lambda E, g=g, sp=sp: E.activation(
                                    esl[:, 2 * g * 512: 2 * g * 512 + 1024],
                                    sp[:, :], AF.Exp, scale=0.125))
                            bank_ac[bank] = exp_tok[g]
                            p.wait("vector", "ac", exp_tok[g])
                            for u in range(2):
                                kc = 2 * g + u
                                j2 = kc - 4 * qt
                                if 0 <= j2 <= 3:
                                    tk = None
                                    for j3 in range(j2):
                                        tk = p.run("vector", lambda E, kc=kc, j3=j3:
                                                   E.memset(
                                                       esl[:, kc * 512 + j3 * 128:
                                                           kc * 512 + (j3 + 1) * 128],
                                                       1.0))
                                    tk = p.run("vector", lambda E, kc=kc, j2=j2:
                                               E.copy_predicated(
                                                   esl[:, kc * 512 + j2 * 128:
                                                       kc * 512 + (j2 + 1) * 128],
                                                   mski[:, :], onest[:, :]))
                                    fix_tok[kc] = tk
                            if g >= 1:
                                emit_pv(2 * (g - 1), False)
                                emit_pv(2 * (g - 1) + 1, False)
                        emit_pv(nmat - 2, False)
                        emit_pv(nmat - 1, not has_tail)
                        if has_tail:
                            p.wait("tensor", "ve", ve_sfx)
                            sfx = srow1[0:1, (4 * qt + 3) * 65:(4 * qt + 3) * 65 + 65]
                            p.mm(pE[0:65, :], sfx, onesrow[0:1, 0:512],
                                 False, True, inc="pe")
                        peU = p.semc["pe"]
                        flush_norm()
                        p.wait("vector", "pe", peU)
                        p.wait("vector", "ac", zx_par[par])
                        ncp = p.run("vector", lambda E, nb=nbuf[par]: E.tensor_copy(
                            nb, pE[0:65, :]))
                        p.wait("scalar", "ve", ncp)
                        p.wait("scalar", "pe", zb_par[par])
                        p.run("scalar", lambda E, z=zrow[par], nb=nbuf[par]:
                              E.activation(z, nb[64:65, :], AF.Ln))
                        p.fence("scalar")
                        zx = p.run("scalar", lambda E, z=zrow[par]: E.activation(
                            z, z, AF.Exp, scale=-1.0))
                        zx_par[par] = zx
                        pend = {"par": par, "ncp": ncp, "zx": zx, "n": n, "qt": qt}
                        prev_fix = max(fix_tok.values()) if fix_tok else prev_fix
                        ui += 1
        flush_norm()

        xlf = xl.bitcast(F32)
        wob = xh
        xbuf = [xlf[:, 0:1024], xlf[:, 4096:5120]]
        gbuf = [xlf[:, 1024:2048], xlf[:, 5120:6144]]
        obuf = [xlf[:, 3072:4096], xlf[:, 6144:7168]]
        t1 = xlf[:, 2048:3072]
        wobv = wob.rearrange("p (u c d) -> p u c d", u=2, c=8)
        wgtbv = wgtb.rearrange("p (bb u c k) -> p bb u c k", bb=2, u=2, c=8)
        wohv = woh.rearrange("(c p) d -> p c d", p=128)
        wghv = wgh_scr.rearrange("(c p) s -> p c s", p=128)
        if "W" in DBG_PHASES:
            p.wait("sync", "pe", p.semc["pe"])
            p.wait("sync", "ve", p.semc["ve"])
            p.wait("sync", "ds", st_total)
            p.wait("sync", "ac", p.semc["ac"])
            d_bo = p.load(bos, t["bo_row"])
            p.wait("tensor", "dl", d_bo)
            dwo = p.load(wobv[:, 0, :, :], wohv)
            dtok = {}
            pe_hist = {}
            ve_hist = {}
            st_hist = {}

            def w_loads(i):
                bb = i % 2
                p.wait("sync", "pe", pe_hist.get(i - 2, 0))
                p.wait("sync", "ve", ve_hist.get(i - 2, 0))
                p.wait("sync", "ds", st_hist.get(i - 2, 0))
                p.load(wgtbv[:, bb, 0, :, :], wghv[:, :, i * 128:(i + 1) * 128])
                p.load(xbuf[bb], x_tm[b, i * 128:(i + 1) * 128, :])
                dtok[i] = p.load(gbuf[bb], g_scr[b, i * 128:(i + 1) * 128, :])

            w_loads(0)
        for tt2 in range(16 if "W" in DBG_PHASES else 0):
            bb = tt2 % 2
            if tt2 + 1 < 16:
                w_loads(tt2 + 1)
            p.wait("tensor", "dl", dtok[tt2])
            p.wait("tensor", "dl", dwo)
            p.barrier_psum("tensor")
            for c in range(8):
                for half in range(2):
                    p.mm(pD[:, half * 512:(half + 1) * 512],
                         wgtbv[:, bb, 0, c, :],
                         wobv[:, 0, c, half * 512:(half + 1) * 512],
                         c == 0, False)
            for half in range(2):
                p.mm(pD[:, half * 512:(half + 1) * 512], onesrow[0:1, 0:128],
                     bos[0:1, half * 512:(half + 1) * 512], False, True,
                     inc="pe" if half == 1 else None)
            peS2 = p.semc["pe"]
            pe_hist[tt2] = peS2
            p.wait("vector", "pe", peS2)
            p.wait("vector", "dl", dtok[tt2])
            p.wait("vector", "ds", st_hist.get(tt2 - 2, 0))
            p.run("vector", lambda E, bb=bb: E.tensor_tensor(
                t1, pD[:, :], gbuf[bb], ALU.mult))
            p.fence("vector")
            veO = p.run("vector", lambda E, bb=bb: E.tensor_tensor(
                obuf[bb], t1, xbuf[bb], ALU.add))
            ve_hist[tt2] = veO
            p.wait("gpsimd", "ve", veO)
            st_total = p.store(out[b, tt2 * 128:(tt2 + 1) * 128, :], obuf[bb])
            st_hist[tt2] = st_total

    p.wait("gpsimd", "ds", p.semc["ds"])


def kernel(**inputs):
    x = np.asarray(inputs["L0_attn_residual"], dtype=np.float32)
    Wk = np.asarray(inputs["L0_attn_W_key"], dtype=np.float32)
    Wq = np.asarray(inputs["L0_attn_W_query"], dtype=np.float32)
    Wv = np.asarray(inputs["L0_attn_W_values"], dtype=np.float32)
    Wo = np.asarray(inputs["L0_attn_W_output"], dtype=np.float32)
    Bk = np.asarray(inputs["L0_attn_Bias_key"], dtype=np.float32)
    Bq = np.asarray(inputs["L0_attn_Bias_query"], dtype=np.float32)
    Bv = np.asarray(inputs["L0_attn_Bias_values"], dtype=np.float32)
    Bo = np.asarray(inputs["L0_attn_Bias_output"], dtype=np.float32)
    W1 = np.asarray(inputs["L0_ffn_W1"], dtype=np.float32)
    b1 = np.asarray(inputs["L0_ffn_b1"], dtype=np.float32)
    W2 = np.asarray(inputs["L0_ffn_W2"], dtype=np.float32)
    b2 = np.asarray(inputs["L0_ffn_b2"], dtype=np.float32)

    wk_t = np.ascontiguousarray(Wk.transpose(1, 0, 2).reshape(D, NHEAD * H))
    wq_t = np.ascontiguousarray(Wq.transpose(1, 0, 2).reshape(D, NHEAD * H))
    wv_t = np.ascontiguousarray(Wv.transpose(1, 0, 2).reshape(D, NHEAD * H))
    wout = np.ascontiguousarray(Wo.reshape(NHEAD * H, D))
    bk_m = np.ascontiguousarray(Bk.reshape(8, 128).T)
    bq_m = np.ascontiguousarray(Bq.reshape(8, 128).T)
    bv_row = np.ascontiguousarray(Bv.reshape(1, NHEAD * H))
    bo_row = np.ascontiguousarray(Bo.reshape(1, D))
    b1_m = np.ascontiguousarray(b1.reshape(32, 128).T)
    b2_row = np.ascontiguousarray(b2.reshape(1, D))
    identc = np.eye(128, dtype=np.float32)
    maskinv = (np.arange(128)[None, :] < np.arange(128)[:, None]).astype(np.uint8)

    nc = build_nc()
    in_maps = []
    for c in range(N_CORES):
        xs = x[B_CORE * c: B_CORE * (c + 1)]
        in_maps.append({
            "x_tm": np.ascontiguousarray(xs),
            "x_fm": np.ascontiguousarray(xs.transpose(0, 2, 1)),
            "wk_t": wk_t, "wq_t": wq_t, "wv_t": wv_t, "wout": wout,
            "bk_m": bk_m, "bq_m": bq_m, "bv_row": bv_row, "bo_row": bo_row,
            "w1": W1, "b1_m": b1_m, "w2": W2, "b2_row": b2_row,
            "identc": identc, "maskinv": maskinv,
        })
    res = run_bass_kernel_spmd(nc, in_maps, core_ids=list(range(N_CORES)))
    return np.concatenate([r["out"] for r in res.results], axis=0)


# version 17 (final)
# speedup vs baseline: 1.8154x; 1.1706x over previous
import numpy as np
import concourse.bass as bass
import concourse.mybir as mybir
from concourse.bass_utils import run_bass_kernel_spmd
from contextlib import ExitStack

F32 = mybir.dt.float32
BF16 = mybir.dt.bfloat16
U8 = mybir.dt.uint8
AF = mybir.ActivationFunctionType
ALU = mybir.AluOpType
AX = mybir.AxisListType.X

B_CORE = 4
S = 2048
D = 1024
NHEAD = 16
H = 64
DFF = 4096
MAGIC = 12582912.0
N_CORES = 8

ENGS = ("sync", "tensor", "vector", "scalar", "gpsimd")

DBG_PHASES = "QLFAW"
DBG_BATCHES = B_CORE
DBG_TAP = False


class Prog:
    def __init__(self, nc):
        self.nc = nc
        self.ops = {e: [] for e in ENGS}
        self.semh = {}
        self.semc = {}
        self.last_wait = {}

    def sem(self, name, handle):
        self.semh[name] = handle
        self.semc[name] = 0

    def wait(self, eng, sem, val):
        if val is None or val <= 0:
            return
        key = (eng, sem)
        if self.last_wait.get(key, -1) >= val:
            return
        self.last_wait[key] = val
        h = self.semh[sem]
        self.ops[eng].append(lambda E, h=h, v=val: E.wait_ge(h, v))

    OWN = {"tensor": "pe", "vector": "ve", "scalar": "ac"}

    def fence(self, eng):
        own = self.OWN[eng]
        self.wait(eng, own, self.semc[own])

    def run(self, eng, fn, inc=None, n=1):
        if inc is None and eng in ("vector", "scalar"):
            inc = self.OWN[eng]
        tok = None
        if inc is not None:
            self.semc[inc] += n
            tok = self.semc[inc]
            h = self.semh[inc]

            def cl(E, fn=fn, h=h, n=n):
                fn(E).then_inc(h, n)
        else:
            def cl(E, fn=fn):
                fn(E)
        self.ops[eng].append(cl)
        return tok

    def load(self, out_ap, in_ap):
        return self.run("sync", lambda E, o=out_ap, i=in_ap: E.dma_start(out=o, in_=i),
                        inc="dl", n=16)

    def store(self, out_ap, in_ap):
        return self.run("gpsimd", lambda E, o=out_ap, i=in_ap: E.dma_start(out=o, in_=i),
                        inc="ds", n=16)

    def mm(self, out, lhsT, rhs, start, stop, inc=None):
        return self.run("tensor",
                        lambda E, o=out, l=lhsT, r=rhs, s=start, p=stop:
                        E.matmul(o, l, r, start=s, stop=p), inc=inc)

    def barrier_psum(self, *engs):
        for e in engs:
            for sem in ("pe", "ve", "ac"):
                if sem == self.OWN.get(e):
                    continue
                self.wait(e, sem, self.semc[sem])


def build_nc():
    nc = bass.Bass("TRN2", target_bir_lowering=False)

    x_tm = nc.dram_tensor("x_tm", [B_CORE, S, D], F32, kind="ExternalInput").ap()
    x_fm = nc.dram_tensor("x_fm", [B_CORE, D, S], F32, kind="ExternalInput").ap()
    wk_t = nc.dram_tensor("wk_t", [D, NHEAD * H], F32, kind="ExternalInput").ap()
    wq_t = nc.dram_tensor("wq_t", [D, NHEAD * H], F32, kind="ExternalInput").ap()
    wv_t = nc.dram_tensor("wv_t", [D, NHEAD * H], F32, kind="ExternalInput").ap()
    wout = nc.dram_tensor("wout", [NHEAD * H, D], F32, kind="ExternalInput").ap()
    bk_m = nc.dram_tensor("bk_m", [128, 8], F32, kind="ExternalInput").ap()
    bq_m = nc.dram_tensor("bq_m", [128, 8], F32, kind="ExternalInput").ap()
    bv_row = nc.dram_tensor("bv_row", [1, NHEAD * H], F32, kind="ExternalInput").ap()
    bo_row = nc.dram_tensor("bo_row", [1, D], F32, kind="ExternalInput").ap()
    w1 = nc.dram_tensor("w1", [DFF, D], F32, kind="ExternalInput").ap()
    b1_m = nc.dram_tensor("b1_m", [128, 32], F32, kind="ExternalInput").ap()
    w2 = nc.dram_tensor("w2", [D, DFF], F32, kind="ExternalInput").ap()
    b2_row = nc.dram_tensor("b2_row", [1, D], F32, kind="ExternalInput").ap()
    identc = nc.dram_tensor("identc", [128, 128], F32, kind="ExternalInput").ap()
    maskinv = nc.dram_tensor("maskinv", [128, 128], U8, kind="ExternalInput").ap()
    out = nc.dram_tensor("out", [B_CORE, S, D], F32, kind="ExternalOutput").ap()

    kw = {"kind": "ExternalOutput"} if DBG_TAP else {}
    wq1t = nc.dram_tensor("wq1t", [D, DFF], BF16, **kw).ap()
    wq2t = nc.dram_tensor("wq2t", [DFF, D], BF16, **kw).ap()
    g_scr = nc.dram_tensor("g_scr", [B_CORE, S, D], F32, **kw).ap()
    wgh_scr = nc.dram_tensor("wgh_scr", [NHEAD * H, S], BF16, **kw).ap()
    awh = nc.dram_tensor("awh", [3, D, NHEAD * H], BF16).ap()
    woh = nc.dram_tensor("woh", [NHEAD * H, D], BF16).ap()

    with ExitStack() as ctx:
        E_ = ctx.enter_context
        xh = E_(nc.sbuf_tensor("xh", [128, 16384], BF16)).ap()
        xl = E_(nc.sbuf_tensor("xl", [128, 16384], BF16)).ap()
        r2 = E_(nc.sbuf_tensor("r2", [128, 8192], F32)).ap()
        kqK = E_(nc.sbuf_tensor("kqK", [128, 4096], F32)).ap()
        kqQ = E_(nc.sbuf_tensor("kqQ", [128, 4096], F32)).ap()
        vsl = E_(nc.sbuf_tensor("vsl", [128, 16 * 4 * 65], F32)).ap()
        srow1 = E_(nc.sbuf_tensor("srow1", [1, 16 * 65], F32)).ap()
        crow1 = E_(nc.sbuf_tensor("crow1", [1, 16 * 65], F32)).ap()
        rows = E_(nc.sbuf_tensor("rows", [1, 2048], F32)).ap()
        wkb = E_(nc.sbuf_tensor("wkb", [128, 2048], BF16)).ap()
        wqb = E_(nc.sbuf_tensor("wqb", [128, 2048], BF16)).ap()
        wvb = E_(nc.sbuf_tensor("wvb", [128, 4096], BF16)).ap()
        wq1b = E_(nc.sbuf_tensor("wq1b", [128, 4096], BF16)).ap()
        wq2b = E_(nc.sbuf_tensor("wq2b", [128, 3072], BF16)).ap()
        wgtb = E_(nc.sbuf_tensor("wgtb", [128, 4096], BF16)).ap()
        ident = E_(nc.sbuf_tensor("ident", [128, 128], F32)).ap()
        mski = E_(nc.sbuf_tensor("mski", [128, 128], U8)).ap()
        onest = E_(nc.sbuf_tensor("onest", [128, 128], F32)).ap()
        onescol = E_(nc.sbuf_tensor("onescol", [128, 1], F32)).ap()
        onescb = E_(nc.sbuf_tensor("onescb", [128, 1], BF16)).ap()
        onesrow = E_(nc.sbuf_tensor("onesrow", [1, 512], F32)).ap()
        onesrb = E_(nc.sbuf_tensor("onesrb", [1, 512], BF16)).ap()
        srowb = E_(nc.sbuf_tensor("srowb", [1, 16 * 65], BF16)).ap()
        b1s = E_(nc.sbuf_tensor("b1s", [128, 32], F32)).ap()
        bks = E_(nc.sbuf_tensor("bks", [128, 8], F32)).ap()
        bqs = E_(nc.sbuf_tensor("bqs", [128, 8], F32)).ap()
        colt = E_(nc.sbuf_tensor("colt", [128, 8], F32)).ap()

        pA = E_(nc.psum_tensor("pA", [128, 512], F32)).ap()
        pB = E_(nc.psum_tensor("pB", [128, 512], F32)).ap()
        pC = E_(nc.psum_tensor("pC", [128, 1024], F32)).ap()
        pD = E_(nc.psum_tensor("pD", [128, 1024], F32)).ap()
        pE = E_(nc.psum_tensor("pE", [128, 512], F32)).ap()
        pF = E_(nc.psum_tensor("pF", [128, 512], F32)).ap()

        sem_handles = {}
        for name in ("dl", "ds", "pe", "ve", "ac"):
            sem_handles[name] = E_(nc.semaphore(name))

        block = E_(nc.Block())
        p = Prog(nc)
        for name, h in sem_handles.items():
            p.sem(name, h)

        build_program(p, locals())

        @block.sync
        def _(sync):
            for f in p.ops["sync"]:
                f(sync)

        @block.tensor
        def _(tensor):
            for f in p.ops["tensor"]:
                f(tensor)

        @block.vector
        def _(vector):
            for f in p.ops["vector"]:
                f(vector)

        @block.scalar
        def _(scalar):
            for f in p.ops["scalar"]:
                f(scalar)

        @block.gpsimd
        def _(gpsimd):
            for f in p.ops["gpsimd"]:
                f(gpsimd)

    return nc


def build_program(p, t):
    xh, xl, r2 = t["xh"], t["xl"], t["r2"]
    kqK, kqQ, vsl = t["kqK"], t["kqQ"], t["vsl"]
    srow1, crow1, rows = t["srow1"], t["crow1"], t["rows"]
    wkb, wqb, wvb = t["wkb"], t["wqb"], t["wvb"]
    wq1b, wq2b, wgtb = t["wq1b"], t["wq2b"], t["wgtb"]
    gsb = t["kqK"][:, 0:1024]
    hfp = t["kqK"][:, 1024:1280]
    stage = wq1b[:, 3072:3328]
    stg2 = wgtb
    uh = wq1b[0:64, 0:1024]
    ident, mski, onest = t["ident"], t["mski"], t["onest"]
    onescol, onesrow = t["onescol"], t["onesrow"]
    onescb, onesrb, srowb = t["onescb"], t["onesrb"], t["srowb"]
    b1s, bks, bqs, colt = t["b1s"], t["bks"], t["bqs"], t["colt"]
    bvs = t["rows"][0:1, 1024:2048]
    bos = t["rows"][0:1, 1024:2048]
    b2s = t["rows"][0:1, 1024:2048]
    pA, pB, pC, pD, pE, pF = t["pA"], t["pB"], t["pC"], t["pD"], t["pE"], t["pF"]
    x_tm, x_fm = t["x_tm"], t["x_fm"]
    wk_t, wq_t, wv_t, wout = t["wk_t"], t["wq_t"], t["wv_t"], t["wout"]
    w1, b1_m, w2 = t["w1"], t["b1_m"], t["w2"]
    identc, maskinv, out = t["identc"], t["maskinv"], t["out"]
    wq1t, wq2t, g_scr = t["wq1t"], t["wq2t"], t["g_scr"]
    wgh_scr = t["wgh_scr"]
    awh, woh = t["awh"], t["woh"]
    vslv = vsl.rearrange("p (c h e) -> p c h e", c=16, h=4)

    p.load(ident[:, :], identc)
    p.load(mski[:, :], maskinv)
    p.load(b1s[:, :], t["b1_m"])
    p.load(bks[:, :], t["bk_m"])
    tok_consts = p.load(bqs[:, :], t["bq_m"])

    p.run("vector", lambda E: E.memset(onest[:, :], 1.0))
    p.run("vector", lambda E: E.memset(onescol[:, :], 1.0))
    p.run("vector", lambda E: E.memset(onescb[:, :], 1.0))
    p.run("vector", lambda E: E.memset(onesrow[:, :], 1.0))
    p.run("vector", lambda E: E.memset(onesrb[:, :], 1.0))
    p.run("vector", lambda E: E.memset(vslv[:, :, :, 64:65], 1.0))
    p.fence("vector")

    f32bufs = [r2[:, 0:1024], r2[:, 1024:2048]]
    st_hist_aw = [None, None]
    dl_hist_aw = [None, None]
    st_aw = None
    aw_i = 0

    def split_stage(src_ap, dst_h):
        nonlocal st_aw, aw_i
        par = aw_i % 2
        aw_i += 1
        f32buf = f32bufs[par]
        sgh = stg2[:, par * 1024:(par + 1) * 1024]
        p.wait("sync", "ds", st_hist_aw[par])
        p.wait("sync", "ve", p.semc["ve"])
        dt_ = p.load(f32buf, src_ap)
        dl_hist_aw[par] = dt_
        p.wait("vector", "dl", dt_)
        p.wait("vector", "ds", st_hist_aw[par])
        vv = p.run("vector", lambda E, sgh=sgh, f32buf=f32buf:
                   E.tensor_copy(sgh, f32buf))
        p.wait("gpsimd", "ve", vv)
        st_aw = p.store(dst_h, sgh)
        st_hist_aw[par] = st_aw

    for mi, wsrc in enumerate((wk_t, wq_t, wv_t)):
        for i in range(8):
            split_stage(wsrc[i * 128:(i + 1) * 128, :],
                        awh[mi, i * 128:(i + 1) * 128, :])
    for i in range(8):
        split_stage(wout[i * 128:(i + 1) * 128, :],
                    woh[i * 128:(i + 1) * 128, :])

    def quantize(w_dram, wq_dram, n_rows, acc_i, st_prev):
        n_cols = w_dram.shape[1]
        ntile = n_rows // 128
        ncol_t = n_cols // 128
        wbuf = [r2[:, 0:n_cols], r2[:, n_cols:2 * n_cols]]
        acc = colt[:, acc_i:acc_i + 1]
        sc = colt[:, acc_i + 1:acc_i + 2]
        red = colt[:, 6:7]
        p.run("vector", lambda E, a=acc: E.memset(a, 0.0))
        p.fence("vector")
        ve_toks = [None] * ntile
        for i in range(ntile):
            b = wbuf[i % 2]
            if i >= 2:
                p.wait("sync", "ve", ve_toks[i - 2])
            else:
                p.wait("sync", "ve", p.semc["ve"])
                p.wait("sync", "pe", p.semc["pe"])
                p.wait("sync", "ac", p.semc["ac"])
            dt_ = p.load(b, w_dram[i * 128:(i + 1) * 128, :])
            p.wait("scalar", "dl", dt_)
            aca = p.run("scalar", lambda E, b=b: E.activation(b, b, AF.Abs))
            p.wait("vector", "ac", aca)
            p.run("vector", lambda E, b=b, r=red: E.tensor_reduce(
                r, b, AX, ALU.add))
            p.fence("vector")
            ve_toks[i] = p.run("vector", lambda E, a=acc, r=red: E.tensor_tensor(
                a, a, r, ALU.add))
            p.fence("vector")
        p.wait("tensor", "ve", ve_toks[-1])
        p.wait("tensor", "dl", tok_consts)
        pe1 = p.mm(pF[0:1, 0:1], acc, onescol[:, :], True, True, inc="pe")
        p.wait("vector", "pe", pe1)
        sc1 = rows[0:1, 0:1]
        p.run("vector", lambda E, s=sc1: E.tensor_scalar(
            s, pF[0:1, 0:1], 1.0 / (n_rows * n_cols), 1e-5, ALU.mult, ALU.add))
        p.fence("vector")
        ve1 = p.run("vector", lambda E, s=sc1: E.reciprocal(s, s))
        p.wait("tensor", "ve", ve1)
        pe2 = p.mm(pF[0:128, 0:1], onesrow[0:1, 0:128], sc1, True, True, inc="pe")
        p.wait("vector", "pe", pe2)
        ve2 = p.run("vector", lambda E, s=sc: E.tensor_copy(s, pF[0:128, 0:1]))
        pe_cons = [None] * ntile
        st_toks = [st_prev, st_prev]
        last_store = st_prev
        vet_prev = ve2
        for i in range(ntile):
            b = wbuf[i % 2]
            if i == 0:
                p.wait("sync", "ve", ve2)
            else:
                p.wait("sync", "pe", pe_cons[i - 1])
            dt_ = p.load(b, w_dram[i * 128:(i + 1) * 128, :])
            p.wait("vector", "dl", dt_)
            p.run("vector", lambda E, b=b, s=sc: E.tensor_scalar(
                b, b, s, MAGIC, ALU.mult, ALU.add))
            p.fence("vector")
            p.run("vector", lambda E, b=b: E.tensor_scalar(
                b, b, MAGIC, -1.0, ALU.subtract, ALU.max))
            p.fence("vector")
            veq = p.run("vector", lambda E, b=b: E.tensor_scalar(
                b, b, 1.0, None, ALU.min))
            p.wait("tensor", "ve", veq)
            for j in range(ncol_t):
                sg = stage[:, (j % 2) * 128:(j % 2) * 128 + 128]
                p.wait("tensor", "ve", vet_prev)
                pet = p.run("tensor", lambda E, src=b[:, j * 128:(j + 1) * 128]:
                            E.transpose(pE[0:128, 0:128], src, ident[:, :]), inc="pe")
                p.wait("vector", "ds", st_toks[j % 2])
                p.wait("vector", "pe", pet)
                vet = p.run("vector", lambda E, sg=sg: E.tensor_copy(
                    sg, pE[0:128, 0:128]))
                vet_prev = vet
                p.wait("gpsimd", "ve", vet)
                stt = p.store(wq_dram[j * 128:(j + 1) * 128, i * 128:(i + 1) * 128], sg)
                st_toks[j % 2] = stt
                last_store = stt
            pe_cons[i] = p.semc["pe"]
        return last_store

    st_total = st_aw
    if "Q" in DBG_PHASES:
        st_total = quantize(w1, wq1t, DFF, 0, st_total)
        st_total = quantize(w2, wq2t, D, 2, st_total)
    q_done = st_total
    ve_xn = 0

    for b in range(DBG_BATCHES):
        xf = r2[:, 0:4096]
        sq = [r2[:, 4096:4608], r2[:, 4608:5120]]
        mb = r2[:, 5120:5632]
        rb = r2[:, 5632:6144]
        xfv = xf.rearrange("p (dc t) -> p dc t", dc=8)
        x_fmv = x_fm[b].rearrange("(dc p) t -> p dc t", p=128)
        m_row = rows[0:1, 0:512]
        var_row = rows[0:1, 512:1024]
        rstd_row = rows[0:1, 1024:1536]
        for tt in range(4 if "L" in DBG_PHASES else 0):
            p.wait("sync", "pe", p.semc["pe"])
            p.wait("sync", "ve", p.semc["ve"])
            p.wait("sync", "ac", p.semc["ac"])
            dt_ = p.load(xfv, x_fmv[:, :, tt * 512:(tt + 1) * 512])
            p.barrier_psum("tensor")
            p.wait("tensor", "dl", dt_)
            p.wait("scalar", "dl", dt_)
            for dc in range(8):
                p.mm(pA[0:1, 0:512], onescol[:, :], xfv[:, dc, :], dc == 0, dc == 7,
                     inc="pe" if dc == 7 else None)
            sq_mm = [None, None]
            for dc in range(8):
                p.wait("scalar", "pe", sq_mm[dc % 2])
                act = p.run("scalar", lambda E, dc=dc, s=sq[dc % 2]: E.activation(
                    s, xfv[:, dc, :], AF.Square))
                p.wait("tensor", "ac", act)
                sq_mm[dc % 2] = p.mm(pB[0:1, 0:512], onescol[:, :], sq[dc % 2],
                                     dc == 0, dc == 7, inc="pe")
            pe_stats = p.semc["pe"]
            p.wait("vector", "pe", pe_stats)
            p.run("vector", lambda E: E.tensor_scalar(
                m_row, pA[0:1, 0:512], 1.0 / D, None, ALU.mult))
            p.fence("vector")
            p.run("vector", lambda E: E.tensor_tensor(
                var_row, m_row, m_row, ALU.mult))
            p.fence("vector")
            p.run("vector", lambda E: E.scalar_tensor_tensor(
                var_row, pB[0:1, 0:512], 1.0 / D, var_row, ALU.mult, ALU.subtract))
            p.fence("vector")
            ve_v = p.run("vector", lambda E: E.tensor_scalar(
                var_row, var_row, 1e-5, None, ALU.add))
            p.wait("scalar", "ve", ve_v)
            p.run("scalar", lambda E: E.activation(rstd_row, var_row, AF.Ln))
            p.fence("scalar")
            ac_r = p.run("scalar", lambda E: E.activation(
                rstd_row, rstd_row, AF.Exp, scale=-0.5))
            p.wait("tensor", "ve", ve_v)
            p.wait("tensor", "ac", ac_r)
            p.mm(pC[:, 0:512], onesrow[0:1, 0:128], m_row, True, True)
            pe_bc = p.mm(pC[:, 512:1024], onesrow[0:1, 0:128], rstd_row, True, True,
                         inc="pe")
            p.wait("vector", "pe", pe_bc)
            p.run("vector", lambda E: E.tensor_copy(mb, pC[:, 0:512]))
            p.run("vector", lambda E: E.tensor_copy(rb, pC[:, 512:1024]))
            p.wait("vector", "pe", p.semc["pe"])
            p.wait("vector", "ds", st_total if b > 0 else 0)
            p.fence("vector")
            for dc in range(8):
                p.run("vector", lambda E, i=xfv[:, dc, :]: E.scalar_tensor_tensor(
                    i, i, 1.0, mb, ALU.mult, ALU.subtract))
            p.fence("vector")
            for dc in range(8):
                p.run("vector", lambda E, i=xfv[:, dc, :]: E.tensor_tensor(
                    i, i, rb, ALU.mult))
            p.fence("vector")
            for dc in range(8):
                o = xh[:, dc * 2048 + tt * 512: dc * 2048 + (tt + 1) * 512]
                ve_xn = p.run("vector", lambda E, o=o, i=xfv[:, dc, :]:
                              E.tensor_copy(o, i))

        r2bf = r2.bitcast(BF16)
        hh = r2bf[:, 0:8192]
        wq1v = wq1t.rearrange("(dc p) f -> p dc f", p=128)
        wq1bv = wq1b.rearrange("p (u dc f) -> p u dc f", u=4, dc=8)
        wq2bv = wq2b.rearrange("p (u f) -> p u f", u=3)
        if "F" in DBG_PHASES:
            p.wait("sync", "ds", q_done)
            p.wait("sync", "ac", p.semc["ac"])
            p.wait("sync", "pe", p.semc["pe"])
            d_b2 = p.load(b2s, t["b2_row"])
            p.wait("tensor", "dl", d_b2)
        d_toks = {}
        pe_wq = {}
        hdone = {}
        rl = {}

        def emit_ffn2(fc, u, gi):
            ac_h = hdone[fc]
            p.wait("tensor", "ac", ac_h)
            for sub in range(2):
                gp = pC if sub == 0 else pD
                for half in range(2):
                    p.mm(gp[:, half * 512:(half + 1) * 512],
                         hh[:, fc * 256 + sub * 128:
                            fc * 256 + sub * 128 + 128],
                         wq2bv[:, u, half * 512:(half + 1) * 512],
                         fc == 0, False,
                         inc="pe" if (sub == 1 and half == 1) else None)
            pe_wq[gi] = p.semc["pe"]

        for ts in range(8 if "F" in DBG_PHASES else 0):
            t0 = ts * 256
            for fc in range(32):
                gi = ts * 32 + fc
                u = gi % 3
                if gi == 0:
                    p.wait("sync", "pe", p.semc["pe"])
                    p.wait("sync", "ac", p.semc["ac"])
                    p.wait("sync", "ve", p.semc["ve"])
                    p.wait("sync", "ds", st_total)
                if gi >= 3:
                    p.wait("sync", "pe", pe_wq[gi - 3])
                p.load(wq1bv[:, u, :, :], wq1v[:, :, fc * 128:(fc + 1) * 128])
                d_toks[gi] = p.load(wq2bv[:, u, :], wq2t[fc * 128:(fc + 1) * 128, :])
                if gi == 0:
                    p.wait("tensor", "ve", ve_xn)
                    p.barrier_psum("tensor")
                p.wait("tensor", "dl", d_toks[gi])
                hp = pA if fc % 2 == 0 else pB
                p.wait("tensor", "ac", rl.get(gi - 2, 0))
                for dc in range(8):
                    p.mm(hp[:, 0:256], wq1bv[:, u, dc, :],
                         xh[:, dc * 2048 + t0: dc * 2048 + t0 + 256],
                         dc == 0, dc == 7,
                         inc="pe" if dc == 7 else None)
                pe_h = p.semc["pe"]
                p.wait("scalar", "pe", pe_h)
                rl[gi] = p.run("scalar", lambda E, i=hp[:, 0:256], fc=fc:
                               E.activation(hfp[:, :], i, AF.Relu,
                                            bias=b1s[:, fc:fc + 1]))
                p.fence("scalar")
                ac_h = p.run("scalar", lambda E, o=hh[:, fc * 256:(fc + 1) * 256]:
                             E.activation(o, hfp[:, :], AF.Copy))
                hdone[fc] = ac_h
                if fc > 0:
                    emit_ffn2(fc - 1, (gi - 1) % 3, gi - 1)
            emit_ffn2(31, (ts * 32 + 31) % 3, ts * 32 + 31)
            for sub in range(2):
                gp = pC if sub == 0 else pD
                for half in range(2):
                    p.mm(gp[:, half * 512:(half + 1) * 512], onesrow[0:1, 0:128],
                         b2s[0:1, half * 512:(half + 1) * 512], False, True,
                         inc="pe" if (sub == 1 and half == 1) else None)
            pe_g = p.semc["pe"]
            pe_wq[ts * 32 + 31] = pe_g
            p.wait("scalar", "pe", pe_g)
            for sub in range(2):
                gp = pC if sub == 0 else pD
                p.wait("scalar", "ds", st_total)
                ac_g = p.run("scalar", lambda E, gp=gp: E.activation(
                    gsb[:, :], gp[:, :], AF.Silu))
                p.wait("gpsimd", "ac", ac_g)
                st_total = p.store(g_scr[b, t0 + sub * 128: t0 + sub * 128 + 128, :],
                                   gsb[:, :])

        esl = r2
        bank_ac = {0: p.semc["ac"], 1: p.semc["ac"]}
        ustg_ds = [st_total, st_total]
        nbufs = wq2b.bitcast(F32)
        nbuf = [nbufs[0:65, 0:512], nbufs[0:65, 512:1024]]
        zrow = [rows[0:1, 0:512], rows[0:1, 512:1024]]
        pend = None
        ui = 0
        prev_fix = p.semc["ve"]
        zx_par = {0: 0, 1: 0}
        zb_par = {0: 0, 1: 0}
        veW_prev = 0

        def flush_norm():
            nonlocal pend, st_total, veW_prev
            if pend is None:
                return
            par0, n0, qt0 = pend["par"], pend["n"], pend["qt"]
            p.wait("tensor", "ac", pend["zx"])
            p.wait("tensor", "ve", veW_prev)
            zb = p.mm(pA[0:64, :], onesrow[0:1, 0:64], zrow[par0], True, True,
                      inc="pe")
            zb_par[par0] = zb
            nb = nbuf[par0]
            p.wait("vector", "pe", zb)
            p.wait("vector", "ds", ustg_ds[par0])
            p.fence("vector")
            uhs = uh[:, par0 * 512:(par0 + 1) * 512]
            p.run("vector", lambda E, nb=nb: E.tensor_tensor(
                nb[0:64, :], nb[0:64, :], pA[0:64, :], ALU.mult))
            p.fence("vector")
            veW = p.run("vector", lambda E, o=uhs, nb=nb: E.tensor_copy(
                o, nb[0:64, :]))
            veW_prev = veW
            p.wait("gpsimd", "ve", veW)
            st_total = p.store(
                wgh_scr[64 * n0: 64 * n0 + 64, qt0 * 512:(qt0 + 1) * 512], uhs)
            ustg_ds[par0] = st_total
            pend = None
        awhv = awh.rearrange("m (dc p) n -> m p dc n", p=128)
        wkbv = wkb.rearrange("p (u dc n) -> p u dc n", u=2, dc=8)
        wqbv = wqb.rearrange("p (u dc n) -> p u dc n", u=2, dc=8)
        wvbv = wvb.rearrange("p (u dc n) -> p u dc n", u=2, dc=8)
        for qd in range(4 if "A" in DBG_PHASES else 0):
            if qd == 0:
                p.wait("sync", "ac", p.semc["ac"])
                p.wait("sync", "ve", p.semc["ve"])
                p.wait("sync", "pe", p.semc["pe"])
                d_bv = p.load(bvs, t["bv_row"])
                p.wait("tensor", "dl", d_bv)
            dv = p.load(wvbv[:, 0, :, :], awhv[2, :, :, qd * 256:(qd + 1) * 256])
            p.wait("tensor", "dl", dv)
            p.wait("tensor", "ve", p.semc["ve"])
            p.barrier_psum("tensor")
            veV_hist = {}
            for kc in range(16):
                pV = pF[:, 0:256] if kc % 2 == 0 else pE[:, 0:256]
                p.wait("tensor", "ve", veV_hist.get(kc - 2, 0))
                for dc in range(8):
                    p.mm(pV,
                         xh[:, dc * 2048 + kc * 128: dc * 2048 + (kc + 1) * 128],
                         wvbv[:, 0, dc, :],
                         dc == 0, False)
                peV = p.mm(pV, onesrow[0:1, 0:128],
                           bvs[0:1, qd * 256:(qd + 1) * 256], False, True, inc="pe")
                p.wait("vector", "pe", peV)
                veV_hist[kc] = p.run("vector", lambda E, kc=kc, pV=pV: E.tensor_copy(
                    vslv[:, kc, :, 0:64],
                    pV.rearrange("p (h e) -> p h e", h=4)))
            kqKb = kqK.bitcast(BF16)
            kqQb = kqQ.bitcast(BF16)
            ktmp = wgtb.bitcast(F32)[:, 0:512]
            qtmp = wgtb.bitcast(F32)[:, 512:1024]
            klt = wq1b[:, 1024:1536]
            for pp in range(2):
                gp_ = 2 * qd + pp
                p.wait("sync", "pe", p.semc["pe"])
                p.load(wkbv[:, 0, :, :], awhv[0, :, :, gp_ * 128:(gp_ + 1) * 128])
                dq = p.load(wqbv[:, 0, :, :], awhv[1, :, :, gp_ * 128:(gp_ + 1) * 128])
                p.wait("tensor", "dl", dq)
                p.wait("tensor", "ve", p.semc["ve"])
                p.wait("tensor", "ds", p.semc["ds"])
                p.barrier_psum("tensor")
                pr_ds = 0
                for tt in range(4):
                    ttsl = slice(tt * 512, (tt + 1) * 512)
                    for ps, wb in ((pA, wkbv), (pB, wqbv)):
                        for dc in range(8):
                            p.mm(ps[:, :], wb[:, 0, dc, :],
                                 xh[:, dc * 2048 + tt * 512:
                                    dc * 2048 + (tt + 1) * 512],
                                 dc == 0, dc == 7,
                                 inc="pe" if dc == 7 else None)
                    peQ = p.semc["pe"]
                    p.wait("vector", "pe", peQ)
                    p.wait("vector", "ds", pr_ds)
                    p.run("vector", lambda E, g=gp_: E.tensor_scalar(
                        ktmp, pA[:, :], bks[:, g:g + 1], None, ALU.add))
                    ve_ts = p.run("vector", lambda E, g=gp_: E.tensor_scalar(
                        qtmp, pB[:, :], bqs[:, g:g + 1], None, ALU.add))
                    p.fence("vector")
                    for hs in range(2):
                        R = slice(64 * hs, 64 * hs + 64)
                        kP_h = kqKb[:, hs * 2048:hs * 2048 + 2048]
                        qhh_h = kqQb[:, hs * 2048:hs * 2048 + 2048]
                        p.run("vector", lambda E, o=kP_h[R, ttsl], i=ktmp[R, :]:
                              E.tensor_copy(o, i))
                        p.run("vector", lambda E, o=qhh_h[R, ttsl], i=qtmp[R, :]:
                              E.tensor_copy(o, i))
                    p.fence("vector")
                    for hs in range(2):
                        R = slice(64 * hs, 64 * hs + 64)
                        kP_h = kqKb[:, hs * 2048:hs * 2048 + 2048]
                        p.run("vector", lambda E, o=klt[R, :], i=ktmp[R, :],
                              h=kP_h[R, ttsl]: E.tensor_tensor(o, i, h, ALU.subtract))
                    ve_ev = p.run("vector", lambda E: E.engine_nop())
                    p.wait("gpsimd", "ve", ve_ev)
                    for hs in range(2):
                        R = slice(64 * hs, 64 * hs + 64)
                        Rp = slice(64 * (1 - hs), 64 * (1 - hs) + 64)
                        kP_h = kqKb[:, hs * 2048:hs * 2048 + 2048]
                        qhh_h = kqQb[:, hs * 2048:hs * 2048 + 2048]
                        p.store(kP_h[Rp, ttsl], klt[R, :])
                        pr_ds = p.store(qhh_h[Rp, ttsl], qhh_h[R, ttsl])
                    p.wait("tensor", "ve", ve_ts)
                pair_ds = pr_ds
                for hs in range(2):
                    j = 2 * pp + hs
                    n = 4 * qd + j
                    kP = kqKb[:, hs * 2048:hs * 2048 + 2048]
                    qhh = kqQb[:, hs * 2048:hs * 2048 + 2048]
                    p.wait("tensor", "ds", pair_ds)
                    p.wait("tensor", "ac", bank_ac[0])
                    p.wait("tensor", "ve", p.semc["ve"])
                    for cb in range(4):
                        for u in range(4):
                            kc = 4 * cb + u
                            p.mm(pC[0:1, u * 65: u * 65 + 65], onescol[:, :],
                                 vslv[:, kc, j, :], True, True,
                                 inc="pe" if u == 3 else None)
                        peC = p.semc["pe"]
                        p.wait("vector", "pe", peC)
                        vcp = p.run("vector", lambda E, cb=cb: E.tensor_copy(
                            crow1[0:1, cb * 260:(cb + 1) * 260], pC[0:1, 0:260]))
                        p.wait("tensor", "ve", vcp)
                    def scp(dst_kc, src_kc):
                        p.fence("vector")
                        p.run("vector", lambda E, d=dst_kc, s=src_kc: E.tensor_copy(
                            srow1[0:1, d * 65:(d + 1) * 65],
                            srow1[0:1, s * 65:(s + 1) * 65]))

                    def sacc(dst_kc, src_kcs, copy_first):
                        first = copy_first
                        for kc in src_kcs:
                            p.fence("vector")
                            src = crow1[0:1, kc * 65:(kc + 1) * 65]
                            dsl = srow1[0:1, dst_kc * 65:(dst_kc + 1) * 65]
                            if first:
                                p.run("vector", lambda E, d=dsl, s=src: E.tensor_copy(d, s))
                                first = False
                            else:
                                p.run("vector", lambda E, d=dsl, s=src: E.tensor_tensor(
                                    d, d, s, ALU.add))
                    sacc(11, [12, 13, 14, 15], copy_first=True)
                    scp(7, 11)
                    sacc(7, [8, 9, 10, 11], copy_first=False)
                    scp(3, 7)
                    sacc(3, [4, 5, 6, 7], copy_first=False)
                    p.fence("vector")
                    ve_sfx = p.semc["ve"]
                    p.wait("tensor", "ve", vcp)
                    for qt in range(4):
                        nmat = 4 * qt + 4
                        has_tail = nmat < 16
                        par = ui % 2
                        fix_tok = {}
                        exp_tok = {}

                        def emit_pv(kc, last):
                            p.wait("tensor", "ac", exp_tok[kc // 2])
                            p.wait("tensor", "ve", fix_tok.get(kc, 0))
                            p.mm(pE[0:65, :], vslv[:, kc, j, :],
                                 esl[:, kc * 512:(kc + 1) * 512], kc == 0, last,
                                 inc="pe" if last else None)

                        for g in range(nmat // 2):
                            sp, bank = (pC, 0) if g % 2 == 0 else (pD, 1)
                            p.wait("tensor", "ac", bank_ac[bank])
                            if g == 0 and pend is not None:
                                p.wait("tensor", "ve", pend["ncp"])
                            for u in range(2):
                                kc = 2 * g + u
                                lhs = kP[:, kc * 128:(kc + 1) * 128]
                                p.mm(sp[:, u * 512:(u + 1) * 512], lhs,
                                     qhh[:, qt * 512:(qt + 1) * 512], True, True,
                                     inc="pe" if u == 1 else None)
                            p.wait("scalar", "pe", p.semc["pe"])
                            p.wait("scalar", "ve", prev_fix)
                            exp_tok[g] = p.run(
                                "scalar", lambda E, g=g, sp=sp: E.activation(
                                    esl[:, 2 * g * 512: 2 * g * 512 + 1024],
                                    sp[:, :], AF.Exp, scale=0.125))
                            bank_ac[bank] = exp_tok[g]
                            p.wait("vector", "ac", exp_tok[g])
                            for u in range(2):
                                kc = 2 * g + u
                                j2 = kc - 4 * qt
                                if 0 <= j2 <= 3:
                                    tk = None
                                    for j3 in range(j2):
                                        tk = p.run("vector", lambda E, kc=kc, j3=j3:
                                                   E.memset(
                                                       esl[:, kc * 512 + j3 * 128:
                                                           kc * 512 + (j3 + 1) * 128],
                                                       1.0))
                                    tk = p.run("vector", lambda E, kc=kc, j2=j2:
                                               E.copy_predicated(
                                                   esl[:, kc * 512 + j2 * 128:
                                                       kc * 512 + (j2 + 1) * 128],
                                                   mski[:, :], onest[:, :]))
                                    fix_tok[kc] = tk
                            if g >= 1:
                                emit_pv(2 * (g - 1), False)
                                emit_pv(2 * (g - 1) + 1, False)
                        emit_pv(nmat - 2, False)
                        emit_pv(nmat - 1, not has_tail)
                        if has_tail:
                            p.wait("tensor", "ve", ve_sfx)
                            sfx = srow1[0:1, (4 * qt + 3) * 65:(4 * qt + 3) * 65 + 65]
                            p.mm(pE[0:65, :], sfx, onesrow[0:1, 0:512],
                                 False, True, inc="pe")
                        peU = p.semc["pe"]
                        flush_norm()
                        p.wait("vector", "pe", peU)
                        p.wait("vector", "ac", zx_par[par])
                        ncp = p.run("vector", lambda E, nb=nbuf[par]: E.tensor_copy(
                            nb, pE[0:65, :]))
                        p.wait("scalar", "ve", ncp)
                        p.wait("scalar", "pe", zb_par[par])
                        p.run("scalar", lambda E, z=zrow[par], nb=nbuf[par]:
                              E.activation(z, nb[64:65, :], AF.Ln))
                        p.fence("scalar")
                        zx = p.run("scalar", lambda E, z=zrow[par]: E.activation(
                            z, z, AF.Exp, scale=-1.0))
                        zx_par[par] = zx
                        pend = {"par": par, "ncp": ncp, "zx": zx, "n": n, "qt": qt}
                        prev_fix = max(fix_tok.values()) if fix_tok else prev_fix
                        ui += 1
        flush_norm()

        xlf = xl.bitcast(F32)
        wob = xh
        xbuf = [xlf[:, 0:1024], xlf[:, 4096:5120]]
        gbuf = [xlf[:, 1024:2048], xlf[:, 5120:6144]]
        obuf = [xlf[:, 3072:4096], xlf[:, 6144:7168]]
        t1 = xlf[:, 2048:3072]
        wobv = wob.rearrange("p (u c d) -> p u c d", u=2, c=8)
        wgtbv = wgtb.rearrange("p (bb u c k) -> p bb u c k", bb=2, u=2, c=8)
        wohv = woh.rearrange("(c p) d -> p c d", p=128)
        wghv = wgh_scr.rearrange("(c p) s -> p c s", p=128)
        if "W" in DBG_PHASES:
            p.wait("sync", "pe", p.semc["pe"])
            p.wait("sync", "ve", p.semc["ve"])
            p.wait("sync", "ds", st_total)
            p.wait("sync", "ac", p.semc["ac"])
            d_bo = p.load(bos, t["bo_row"])
            p.wait("tensor", "dl", d_bo)
            dwo = p.load(wobv[:, 0, :, :], wohv)
            dtok = {}
            pe_hist = {}
            ve_hist = {}
            st_hist = {}

            def w_loads(i):
                bb = i % 2
                p.wait("sync", "pe", pe_hist.get(i - 2, 0))
                p.wait("sync", "ve", ve_hist.get(i - 2, 0))
                p.wait("sync", "ds", st_hist.get(i - 2, 0))
                p.load(wgtbv[:, bb, 0, :, :], wghv[:, :, i * 128:(i + 1) * 128])
                p.load(xbuf[bb], x_tm[b, i * 128:(i + 1) * 128, :])
                dtok[i] = p.load(gbuf[bb], g_scr[b, i * 128:(i + 1) * 128, :])

            w_loads(0)
        for tt2 in range(16 if "W" in DBG_PHASES else 0):
            bb = tt2 % 2
            if tt2 + 1 < 16:
                w_loads(tt2 + 1)
            p.wait("tensor", "dl", dtok[tt2])
            p.wait("tensor", "dl", dwo)
            p.barrier_psum("tensor")
            for c in range(8):
                for half in range(2):
                    p.mm(pD[:, half * 512:(half + 1) * 512],
                         wgtbv[:, bb, 0, c, :],
                         wobv[:, 0, c, half * 512:(half + 1) * 512],
                         c == 0, False)
            for half in range(2):
                p.mm(pD[:, half * 512:(half + 1) * 512], onesrow[0:1, 0:128],
                     bos[0:1, half * 512:(half + 1) * 512], False, True,
                     inc="pe" if half == 1 else None)
            peS2 = p.semc["pe"]
            pe_hist[tt2] = peS2
            p.wait("vector", "pe", peS2)
            p.wait("vector", "dl", dtok[tt2])
            p.wait("vector", "ds", st_hist.get(tt2 - 2, 0))
            p.run("vector", lambda E, bb=bb: E.tensor_tensor(
                t1, pD[:, :], gbuf[bb], ALU.mult))
            p.fence("vector")
            veO = p.run("vector", lambda E, bb=bb: E.tensor_tensor(
                obuf[bb], t1, xbuf[bb], ALU.add))
            ve_hist[tt2] = veO
            p.wait("gpsimd", "ve", veO)
            st_total = p.store(out[b, tt2 * 128:(tt2 + 1) * 128, :], obuf[bb])
            st_hist[tt2] = st_total

    p.wait("gpsimd", "ds", p.semc["ds"])


def kernel(**inputs):
    x = np.asarray(inputs["L0_attn_residual"], dtype=np.float32)
    Wk = np.asarray(inputs["L0_attn_W_key"], dtype=np.float32)
    Wq = np.asarray(inputs["L0_attn_W_query"], dtype=np.float32)
    Wv = np.asarray(inputs["L0_attn_W_values"], dtype=np.float32)
    Wo = np.asarray(inputs["L0_attn_W_output"], dtype=np.float32)
    Bk = np.asarray(inputs["L0_attn_Bias_key"], dtype=np.float32)
    Bq = np.asarray(inputs["L0_attn_Bias_query"], dtype=np.float32)
    Bv = np.asarray(inputs["L0_attn_Bias_values"], dtype=np.float32)
    Bo = np.asarray(inputs["L0_attn_Bias_output"], dtype=np.float32)
    W1 = np.asarray(inputs["L0_ffn_W1"], dtype=np.float32)
    b1 = np.asarray(inputs["L0_ffn_b1"], dtype=np.float32)
    W2 = np.asarray(inputs["L0_ffn_W2"], dtype=np.float32)
    b2 = np.asarray(inputs["L0_ffn_b2"], dtype=np.float32)

    wk_t = np.ascontiguousarray(Wk.transpose(1, 0, 2).reshape(D, NHEAD * H))
    wq_t = np.ascontiguousarray(Wq.transpose(1, 0, 2).reshape(D, NHEAD * H))
    wv_t = np.ascontiguousarray(Wv.transpose(1, 0, 2).reshape(D, NHEAD * H))
    wout = np.ascontiguousarray(Wo.reshape(NHEAD * H, D))
    bk_m = np.ascontiguousarray(Bk.reshape(8, 128).T)
    bq_m = np.ascontiguousarray(Bq.reshape(8, 128).T)
    bv_row = np.ascontiguousarray(Bv.reshape(1, NHEAD * H))
    bo_row = np.ascontiguousarray(Bo.reshape(1, D))
    b1_m = np.ascontiguousarray(b1.reshape(32, 128).T)
    b2_row = np.ascontiguousarray(b2.reshape(1, D))
    identc = np.eye(128, dtype=np.float32)
    maskinv = (np.arange(128)[None, :] < np.arange(128)[:, None]).astype(np.uint8)

    nc = build_nc()
    in_maps = []
    for c in range(N_CORES):
        xs = x[B_CORE * c: B_CORE * (c + 1)]
        in_maps.append({
            "x_tm": np.ascontiguousarray(xs),
            "x_fm": np.ascontiguousarray(xs.transpose(0, 2, 1)),
            "wk_t": wk_t, "wq_t": wq_t, "wv_t": wv_t, "wout": wout,
            "bk_m": bk_m, "bq_m": bq_m, "bv_row": bv_row, "bo_row": bo_row,
            "w1": W1, "b1_m": b1_m, "w2": W2, "b2_row": b2_row,
            "identc": identc, "maskinv": maskinv,
        })
    res = run_bass_kernel_spmd(nc, in_maps, core_ids=list(range(N_CORES)))
    return np.concatenate([r["out"] for r in res.results], axis=0)
